# version 11
# baseline (speedup 1.0000x reference)
import numpy as np
import ml_dtypes
import concourse.bass as bass
import concourse.mybir as mybir
from concourse.bass_utils import run_bass_kernel_spmd

F32 = mybir.dt.float32
F32R = mybir.dt.float32r
BF16 = mybir.dt.bfloat16
ALU = mybir.AluOpType
ACTF = mybir.ActivationFunctionType
AX = mybir.AxisListType

D = 1024
NCH = 8
TLOC = 2048
HALO = 256
TTOT = TLOC + HALO
NG = 4
TG = 512
NG3 = 8
TG3 = 256
H, DH, WIN = 16, 64, 256
INNER = 4096
NEX = 20
MAGIC = 12582912.0
CORES = 8
ENGS = ("sync", "vector", "scalar", "tensor", "gpsimd")

PARTS = ("l0a", "l0f", "l1a", "l1m")
DEBUG = False


class K:
    def __init__(self, nc):
        self.nc = nc
        self.q = {e: [] for e in ENGS}
        self.cnt = {}
        self.sems = {}
        self._exits = []
        self.blocks = 0
        self.nbar = 0
        self._pace = {}
        self._bound = {}
        self._lastw = {}
        self._hist = {}

    def sem(self, name):
        cm = self.nc.semaphore(name)
        self.sems[name] = cm.__enter__()
        self._exits.append(cm)
        self.cnt[name] = 0

    def sbuf(self, name, shape, dtype=F32):
        cm = self.nc.sbuf_tensor(name, shape, dtype)
        t = cm.__enter__()
        self._exits.append(cm)
        return t

    def psum(self, name, shape, dtype=F32):
        cm = self.nc.psum_tensor(name, shape, dtype)
        t = cm.__enter__()
        self._exits.append(cm)
        return t

    def op(self, eng, fn, inc=None, k=1):
        tok = None
        if inc is not None:
            self.cnt[inc] += k
            tok = (inc, self.cnt[inc])
            self.q[eng].append(("op", fn, inc, k))
        else:
            self.q[eng].append(("op", fn, None, 0))
        return tok

    def stream(self, name, n=2):
        sems = []
        for i in range(n):
            nm = f"{name}{i}"
            self.sem(nm)
            sems.append(nm)
        st = {"i": 0, "sems": sems}

        def issue(eng, fn):
            sm = st["sems"][st["i"] % n]
            st["i"] += 1
            return self.op(eng, fn, inc=sm, k=16)
        return issue

    def mark(self, eng, inc, k=1):
        it = self.q[eng][-1]
        assert it[0] == "op" and it[2] is None, it
        self.cnt[inc] += k
        self.q[eng][-1] = ("op", it[1], inc, k)
        return (inc, self.cnt[inc])

    def wait(self, eng, tok):
        if tok is not None:
            self.q[eng].append(("wait", tok))

    def waits(self, eng, toks):
        for t in toks:
            self.wait(eng, t)

    def tok(self, name):
        return (name, self.cnt[name])

    def barrier(self):
        self.nbar += 1
        target = 5 * self.nbar
        self.q["sync"].append(("wait", self.tok("sd")))
        self.q["sync"].append(("wait", self.tok("so")))
        self.q["gpsimd"].append(("wait", self.tok("sg")))
        for e in ENGS:
            self.cnt["bar"] += 1
            self.q[e].append(("op", lambda eng: eng.nop(), "bar", 1))
        for e in ENGS:
            self.q[e].append(("wait", ("bar", target)))

    def flush(self, name=None):
        self.blocks += 1
        name = name or f"b{self.blocks}"
        qs = {e: self.q[e] for e in ENGS if self.q[e]}
        self.q = {e: [] for e in ENGS}
        with self.nc.named_scope(name), self.nc.Block(name) as b:
            for e, items in qs.items():
                def body(eng, items=items):
                    for it in items:
                        if it[0] == "wait":
                            _, (sn, sv) = it
                            eng.wait_ge(self.sems[sn], sv)
                        else:
                            _, fn, inc, kk = it
                            ins = fn(eng)
                            if inc is not None:
                                ins.then_inc(self.sems[inc], kk)
                getattr(b, e)(body)

    def close(self):
        for cm in reversed(self._exits):
            cm.__exit__(None, None, None)


def build_program():
    nc = bass.Bass("TRN2", target_bir_lowering=False)
    k = K(nc)
    for s in ("sd", "sv", "ss", "st", "sg", "sgc", "so", "cc", "bar"):
        k.sem(s)

    def din(name, shape, dtype=F32):
        return nc.dram_tensor(name, shape, dtype, kind="ExternalInput").ap()

    xT = din("xT", [D, TTOT])
    qkvT = din("qkvT", [D, 3 * D])
    outwT = din("outwT", [D, D])
    ln2w = din("ln2w", [128, NCH]); ln2b = din("ln2b", [128, NCH])
    wgT = din("wgT", [D, INNER]); wuT = din("wuT", [D, INNER])
    wdT = din("wdT", [128, 32 * D])
    Wt = din("Wt", [D, D]); Wo = din("Wo", [D, D]); Wg = din("Wg", [2 * D, D])
    bgv = din("bgv", [128, NCH])
    ln1w = din("ln1w", [128, NCH]); ln1b = din("ln1b", [128, NCH])
    lncw = din("lncw", [128, NCH]); lncb = din("lncb", [128, NCH])
    lngw = din("lngw", [128, NCH]); lngb = din("lngb", [128, NCH])
    wg1 = din("wg1", [D, 256]); bg1 = din("bg1", [128, 2])
    wg2 = din("wg2", [256, 40]); bg2 = din("bg2", [40, 1])
    We = din("We", [NCH, NEX, D, 128])
    bev = din("bev", [NEX, D])
    maskA = din("maskA", [128, 4 * 256], BF16)
    maskB = din("maskB", [128, 4 * 256], BF16)
    selgd = din("selg", [128, NEX * 128], BF16)
    ident = din("ident", [128, 128])
    decayc = din("decayc", [128, NCH])
    oh2 = din("oh2", [1, CORES])
    y = nc.dram_tensor("y", [D, TLOC], F32, kind="ExternalOutput").ap()
    ternG = nc.dram_tensor("ternG", [128, NCH * INNER], BF16).ap()
    ternU = nc.dram_tensor("ternU", [128, NCH * INNER], BF16).ap()
    ternD = nc.dram_tensor("ternD", [128, 32 * D], BF16).ap()
    WeBf = nc.dram_tensor("WeBf", [NCH, NEX, D, 128], BF16).ap()
    cc_in = nc.dram_tensor("cc_in", [CORES * 128, NCH], F32).ap()
    cc_out = nc.dram_tensor("cc_out", [128, NCH], F32).ap()
    rows_d = nc.dram_tensor("rows_d", [5, TLOC], F32).ap()
    dbg = nc.dram_tensor("dbg", [128, 16384], F32).ap()

    x1 = k.sbuf("x1", [128, NCH * TLOC])
    POOLN = 12352
    pool = k.sbuf("pool", [128, POOLN])
    regA = k.sbuf("regA", [128, 16384], BF16)
    regB = k.sbuf("regB", [128, 8192], BF16)
    regC = k.sbuf("regC", [128, 8192], BF16)
    rowsb = k.sbuf("rowsb", [128, TLOC])
    rowstage = k.sbuf("rowstage", [1, TG])
    qscale = k.sbuf("qscale", [128, 24])
    oscale = k.sbuf("oscale", [128, 8])
    pvec = k.sbuf("pvec", [128, 26])
    dvec = k.sbuf("dvec", [128, NCH])
    betav = k.sbuf("betav", [128, NCH])
    mu_b = k.sbuf("mu_b", [128, TG])
    r_b = k.sbuf("r_b", [128, TG])
    mu2_b = k.sbuf("mu2_b", [128, TG])
    r2_b = k.sbuf("r2_b", [128, TG])
    sbf2 = k.sbuf("sbf2", [128, 2 * TG], BF16)
    probs_sb = k.sbuf("probs_sb", [128, 4 * 256], BF16)
    ident_sb = k.sbuf("ident_sb", [128, 128])
    maskA_sb = k.sbuf("maskA_sb", [128, 4 * 256], BF16)
    maskB_sb = k.sbuf("maskB_sb", [128, 4 * 256], BF16)
    ones_bf = k.sbuf("ones_bf", [128, 128], BF16)
    ones1f = k.sbuf("ones1f", [1, 128])
    onesf = k.sbuf("onesf", [128, 128])
    carry_sb = k.sbuf("carry_sb", [128, NCH])
    ccst = k.sbuf("ccst", [128, CORES * NCH])
    ccres = k.sbuf("ccres", [128, NCH])
    oh2_sb = k.sbuf("oh2_sb", [128, CORES])
    bg1_sb = k.sbuf("bg1_sb", [128, 2])
    bg2_sb = k.sbuf("bg2_sb", [40, 1])
    gtm = k.sbuf("gtm", [128, 128])
    gsc = k.sbuf("gsc", [128, 8])
    epsv = k.sbuf("epsv", [128, 2])
    lpar = k.sbuf("lpar", [128, 56])

    R_MU0, R_R0, R_AQM = 0, 32, 64
    D_MUX, D_RX, D_MU1, D_R1, D_AQM = range(5)

    def rowg(i, g, t=TG, w=None):
        return rowsb[i:i + 1, g * t: g * t + (w or t)]

    def x1c(ch, g=None, t=TG, off=0, w=None):
        if g is None:
            return x1[:, ch * TLOC:(ch + 1) * TLOC]
        a = ch * TLOC + g * t + off
        return x1[:, a: a + (w or t)]

    PF = pool
    PB = pool.bitcast(BF16)

    def pf(off, cols):
        return PF[:, off:off + cols]

    def pbf(off, cols):
        return PB[:, 2 * off:2 * off + cols]

    pb = [k.psum(f"pb{i}", [128, 512]) for i in range(8)]
    pb_tok = [None] * 8

    def mm_bank(i, toks=()):
        k.wait("tensor", pb_tok[i])
        for t in toks:
            k.wait("tensor", t)

    def ln_stats(src_fn, T, eps, mu, r, rows_to=None):
        epsap = epsv[:, 0:1] if eps == 1e-5 else epsv[:, 1:2]
        """Feature-major LN stats; src_fn(ch)->AP [128,T] (fp32 or bf16)."""
        for ch in range(NCH):
            k.op("vector", lambda e, ch=ch: e.tensor_copy(sbf2[:, 0:T], src_fn(ch)))
            k.op("vector", lambda e: e.drain())
            k.op("vector", lambda e: e.tensor_tensor(
                sbf2[:, T:2 * T], sbf2[:, 0:T], sbf2[:, 0:T], ALU.mult))
            tv = k.op("vector", lambda e: e.drain(), inc="sv")
            if ch == 0:
                k.wait("tensor", pb_tok[0])
                k.wait("tensor", pb_tok[1])
            k.wait("tensor", tv)
            k.op("tensor", lambda e, ch=ch: e.matmul(
                pb[0][:, 0:T], ones_bf[:], sbf2[:, 0:T],
                start=(ch == 0), stop=(ch == NCH - 1)))
            k.op("tensor", lambda e, ch=ch: e.matmul(
                pb[1][:, 0:T], ones_bf[:], sbf2[:, T:2 * T],
                start=(ch == 0), stop=(ch == NCH - 1)))
            tt = k.mark("tensor", "st")
            k.wait("vector", tt)
        k.op("vector", lambda e: e.tensor_scalar(
            mu[:, 0:T], pb[0][:, 0:T], 1.0 / D, None, ALU.mult, ALU.bypass))
        k.op("vector", lambda e: e.drain())
        k.op("vector", lambda e: e.tensor_scalar(
            r[:, 0:T], pb[1][:, 0:T], 1.0 / D, None, ALU.mult, ALU.bypass))
        k.op("vector", lambda e: e.drain())
        sf = sbf2.bitcast(F32)
        k.op("vector", lambda e, sf=sf: e.tensor_tensor(
            sf[:, 0:T], mu[:, 0:T], mu[:, 0:T], ALU.mult))
        k.op("vector", lambda e: e.drain())
        k.op("vector", lambda e, sf=sf: e.tensor_tensor(
            r[:, 0:T], r[:, 0:T], sf[:, 0:T], ALU.subtract))
        vv = k.op("vector", lambda e: e.drain(), inc="sv")
        pb_tok[0] = vv
        pb_tok[1] = vv
        k.wait("scalar", vv)
        sv = k.op("scalar", lambda e, epsap=epsap: e.activation(
            r[:, 0:T], r[:, 0:T], ACTF.Sqrt, bias=epsap, scale=1.0), inc="ss")
        k.wait("vector", sv)
        k.op("vector", lambda e: e.reciprocal(r[:, 0:T], r[:, 0:T]))
        vtok = k.op("vector", lambda e: e.drain(), inc="sv")
        if rows_to is not None:
            mu_row, r_row = rows_to
            ip_mu = mu_row.base_partition() if callable(mu_row.base_partition) else mu_row.base_partition
            ip_r = r_row.base_partition() if callable(r_row.base_partition) else r_row.base_partition
            k.op("vector", lambda e, mu_row=mu_row, ip=ip_mu: e.tensor_copy(
                mu_row, mu[ip:ip + 1, 0:T]))
            k.op("vector", lambda e, r_row=r_row, ip=ip_r: e.tensor_copy(
                r_row, r[ip:ip + 1, 0:T]))
            vtok = k.op("vector", lambda e: e.drain(), inc="sv")
        return vtok

    stRS = k.stream("stRS", 2)

    def bcast_drows(specs, T, toks=()):
        first = True
        for rid, c0, dst in specs:
            k.wait("sync", k.tok("st"))
            td = stRS("sync", lambda e, rid=rid, c0=c0, T=T: e.dma_start(
                out=rowstage[0:1, 0:T], in_=rows_d[rid, c0:c0 + T]))
            mm_bank(0, (toks if first else ()) + (td,) if False else (td,))
            if first:
                for t_ in toks:
                    k.wait("tensor", t_)
            first = False
            tm = k.op("tensor", lambda e, T=T: e.matmul(
                pb[0][:, 0:T], ones1f[0:1, :], rowstage[0:1, 0:T],
                start=True, stop=True), inc="st")
            k.wait("vector", tm)
            pb_tok[0] = k.op("vector", lambda e, dst=dst, T=T: e.tensor_copy(
                dst[:, 0:T], pb[0][:, 0:T]), inc="sv")
            k.op("vector", lambda e: e.drain())

    def bcast_rows(specs, T, toks=()):
        first = True
        for rowap, dst in specs:
            mm_bank(0, toks if first else ())
            first = False
            bp = rowap.base_partition
            bp = bp() if callable(bp) else bp
            tm = k.op("tensor", lambda e, rowap=rowap, T=T, bp=bp: e.matmul(
                pb[0][:, 0:T], onesf[bp:bp + 1, :], rowap, start=True, stop=True),
                inc="st")
            k.wait("vector", tm)
            pb_tok[0] = k.op("vector", lambda e, dst=dst, T=T: e.tensor_copy(
                dst[:, 0:T], pb[0][:, 0:T]), inc="sv")
            k.op("vector", lambda e: e.drain())

    for ch in range(NCH):
        k.op("sync", lambda e, ch=ch: e.dma_start(
            out=x1c(ch), in_=xT[ch * 128:(ch + 1) * 128, HALO:TTOT]), inc="sd", k=16)
    k.op("sync", lambda e: e.dma_start(out=ident_sb[:], in_=ident), inc="sd", k=16)
    k.op("sync", lambda e: e.dma_start(out=maskA_sb[:], in_=maskA), inc="sd", k=16)
    k.op("sync", lambda e: e.dma_start(out=maskB_sb[:], in_=maskB), inc="sd", k=16)
    k.op("sync", lambda e: e.dma_start(out=pvec[:, 0:NCH], in_=ln2w), inc="sd", k=16)
    k.op("sync", lambda e: e.dma_start(out=pvec[:, 8:16], in_=ln2b), inc="sd", k=16)
    k.op("sync", lambda e: e.dma_start(out=dvec[:], in_=decayc), inc="sd", k=16)
    k.op("sync", lambda e: e.dma_start(out=bg1_sb[:], in_=bg1), inc="sd", k=16)
    k.op("sync", lambda e: e.dma_start(out=bg2_sb[:], in_=bg2), inc="sd", k=16)
    k.op("sync", lambda e: e.dma_start(out=oh2_sb[0:1, :], in_=oh2), inc="sd", k=16)
    for i_, ap_ in enumerate([ln1w, ln1b, lncw, lncb, lngw, lngb, bgv]):
        k.op("sync", lambda e, i_=i_, ap_=ap_: e.dma_start(
            out=lpar[:, i_ * 8:(i_ + 1) * 8], in_=ap_), inc="sd", k=16)
    k.op("vector", lambda e: e.memset(ones_bf[:], 1.0))
    k.op("vector", lambda e: e.memset(ones1f[:], 1.0))
    k.op("vector", lambda e: e.memset(onesf[:], 1.0))
    k.op("vector", lambda e: e.memset(epsv[:, 0:1], 1e-5))
    k.op("vector", lambda e: e.memset(epsv[:, 1:2], 1e-6))
    k.op("vector", lambda e: e.drain(), inc="sv")
    k.barrier()
    k.op("vector", lambda e: e.tensor_scalar(betav[:], dvec[:], -1.0, 1.0,
                                             ALU.mult, ALU.add))
    k.op("vector", lambda e: e.drain(), inc="sv")
    k.flush("p0_loads")

    qw = pf(0, 4096)
    qaf = pf(4096, 4096)
    qa = qaf
    qsb = mu_b
    qrb = r_b

    def quant_rowscaled(wT_ap, dout, dst_fn, sc_out):
        for jg in range(dout // TG):
            k.wait("sync", k.tok("sv"))
            toks = [k.op("sync", lambda e, ch=ch, jg=jg: e.dma_start(
                out=qw[:, ch * TG:(ch + 1) * TG],
                in_=wT_ap[ch * 128:(ch + 1) * 128, jg * TG:(jg + 1) * TG]),
                inc="sd", k=16) for ch in range(NCH)]
            k.wait("vector", toks[-1])
            for ch in range(NCH):
                k.op("vector", lambda e, ch=ch: e.scalar_tensor_tensor(
                    qa[:, ch * TG:(ch + 1) * TG], qw[:, ch * TG:(ch + 1) * TG],
                    -1.0, qw[:, ch * TG:(ch + 1) * TG], ALU.mult, ALU.max))
            va = k.op("vector", lambda e: e.drain(), inc="sv")
            mm_bank(2, (va,))
            for ch in range(NCH):
                k.op("tensor", lambda e, ch=ch: e.matmul(
                    pb[2][:, 0:TG], onesf[:], qa[:, ch * TG:(ch + 1) * TG],
                    start=(ch == 0), stop=(ch == NCH - 1)))
            tmm = k.mark("tensor", "st")
            k.wait("vector", tmm)
            k.op("vector", lambda e: e.tensor_scalar(
                qsb[:], pb[2][:, 0:TG], 1.0 / D, 1e-5, ALU.mult, ALU.max))
            vq = k.op("vector", lambda e: e.drain(), inc="sv")
            pb_tok[2] = vq
            k.op("vector", lambda e: e.reciprocal(qrb[:], qsb[:]))
            k.op("vector", lambda e: e.drain())
            for q in range(TG // 128):
                j = jg * (TG // 128) + q
                vq2 = k.op("vector", lambda e: e.drain(), inc="sv")
                mm_bank(3, (vq2,))
                tq = k.op("tensor", lambda e, q=q: e.transpose(
                    pb[3][:, 0:128], qsb[:, q * 128:(q + 1) * 128], ident_sb[:]),
                    inc="st")
                k.wait("vector", tq)
                pb_tok[3] = k.op("vector", lambda e, j=j: e.tensor_copy(
                    sc_out[:, j:j + 1], pb[3][:, 0:1]), inc="sv")
                k.op("vector", lambda e: e.drain())
            for ch in range(NCH):
                tmp = qaf[:, ch * TG:(ch + 1) * TG]
                neg = qw[:, ch * TG:(ch + 1) * TG]
                k.op("vector", lambda e, ch=ch, tmp=tmp, neg=neg: e.tensor_tensor(
                    tmp, neg, qrb[:], ALU.mult))
                k.op("vector", lambda e: e.drain())
                k.op("vector", lambda e, tmp=tmp, neg=neg: e.tensor_scalar(
                    neg, tmp, -0.5, None, ALU.is_lt, ALU.bypass))
                k.op("vector", lambda e: e.drain())
                for q in range(TG // 128):
                    j = jg * (TG // 128) + q
                    k.op("vector", lambda e, ch=ch, tmp=tmp, neg=neg, q=q, j=j:
                         e.scalar_tensor_tensor(
                        dst_fn(ch, j), tmp[:, q * 128:(q + 1) * 128], 0.5,
                        neg[:, q * 128:(q + 1) * 128], ALU.is_gt, ALU.subtract))
                k.op("vector", lambda e: e.drain())
            k.op("vector", lambda e: e.drain(), inc="sv")
            k.flush(f"q_{wT_ap.tensor.name}_{jg}")

    qk_t = regA
    v_t = regB
    ow_t = regC

    def qkv_dst(ch, j):
        if j >= 16:
            return v_t[:, ch * 1024 + (j - 16) * 128: ch * 1024 + (j - 15) * 128]
        return qk_t[:, ch * 2048 + j * 128: ch * 2048 + (j + 1) * 128]

    def ow_dst(ch, j):
        return ow_t[:, ch * 1024 + j * 128: ch * 1024 + (j + 1) * 128]

    quant_rowscaled(qkvT, 3 * D, qkv_dst, qscale)
    quant_rowscaled(outwT, D, ow_dst, oscale)
    k.op("vector", lambda e: e.tensor_scalar(
        qscale[:, 0:8], qscale[:, 0:8], DH ** -0.5, None, ALU.mult, ALU.bypass))
    k.op("vector", lambda e: e.drain(), inc="sv")
    k.barrier()
    k.flush("p0_done")

    halo_x = pf(0, 2048)
    q_bf = pbf(0, 4096)
    xn_g = pf(2048, 4096)
    h_bf = pbf(2048, 4096)
    araw = pf(4096, 2048)
    kroll = pbf(6144, 6144)
    vroll = pbf(9216, 6240)
    VW = H * 65
    aqm_b = r_b
    probs_b = None
    prev_tb = {}
    prev_tpv = {}

    for ch in range(NCH):
        k.op("sync", lambda e, ch=ch: e.dma_start(
            out=halo_x[:, ch * HALO:(ch + 1) * HALO],
            in_=xT[ch * 128:(ch + 1) * 128, 0:HALO]), inc="sd", k=16)
    t_halo = k.tok("sd")
    for u in range(6):
        for hh in range(H):
            k.op("vector", lambda e, u=u, hh=hh: e.memset(
                vroll[:, u * VW + hh * 65 + 64: u * VW + hh * 65 + 65], 1.0))
    k.op("vector", lambda e: e.drain(), inc="sv")
    k.flush("p1_pre")

    def emit_h(xsrc, T, g, store_rows):
        ln_stats(xsrc, T, 1e-5, mu_b, r_b,
                 rows_to=(rowg(R_MU0, g, w=T), rowg(R_R0, g, w=T)) if store_rows else None)
        vt = k.tok("sv")
        k.wait("vector", vt)
        for ch in range(NCH):
            k.op("vector", lambda e, ch=ch, T=T: e.tensor_tensor(
                xn_g[:, ch * TG: ch * TG + T], xsrc(ch), mu_b[:, 0:T], ALU.subtract))
            k.op("vector", lambda e: e.drain())
            k.op("vector", lambda e, ch=ch, T=T: e.tensor_tensor(
                xn_g[:, ch * TG: ch * TG + T], xn_g[:, ch * TG: ch * TG + T],
                r_b[:, 0:T], ALU.mult))
            k.op("vector", lambda e: e.drain())
        k.op("vector", lambda e: e.drain(), inc="sv")

        def xnsrc(ch, T=T):
            return xn_g[:, ch * TG: ch * TG + T]

        ln_stats(xnsrc, T, 1e-5, mu2_b, r2_b)
        sf = sbf2.bitcast(F32)
        if store_rows:
            vt2 = k.tok("sv")
            k.wait("vector", vt2)
            k.op("vector", lambda e, T=T: e.scalar_tensor_tensor(
                aqm_b[:, 0:T], xnsrc(0), -1.0, xnsrc(0), ALU.mult, ALU.max))
            k.op("vector", lambda e: e.drain())
            for ch in range(1, NCH):
                k.op("vector", lambda e, ch=ch, T=T, sf=sf: e.scalar_tensor_tensor(
                    sf[:, 0:T], xnsrc(ch), -1.0, xnsrc(ch), ALU.mult, ALU.max))
                k.op("vector", lambda e: e.drain())
                k.op("vector", lambda e, T=T, sf=sf: e.tensor_tensor(
                    aqm_b[:, 0:T], aqm_b[:, 0:T], sf[:, 0:T], ALU.max))
                k.op("vector", lambda e: e.drain())
            va = k.op("vector", lambda e: e.drain(), inc="sv")
            for c in range(T // 128):
                mm_bank(2, (va,) if c == 0 else ())
                t1 = k.op("tensor", lambda e, c=c: e.transpose(
                    pb[2][:, 0:128], aqm_b[:, c * 128:(c + 1) * 128],
                    ident_sb[:]), inc="st")
                k.wait("vector", t1)
                k.op("vector", lambda e: e.reduce_max(
                    gsc[:, 0:1], pb[2][:, 0:128], AX.X))
                k.op("vector", lambda e: e.drain())
                k.op("vector", lambda e: e.tensor_scalar(
                    gsc[:, 1:2], gsc[:, 0:1], 1e-5, None, ALU.max, ALU.bypass))
                vcl = k.op("vector", lambda e: e.drain(), inc="sv")
                pb_tok[2] = vcl
                mm_bank(2)
                t2 = k.op("tensor", lambda e: e.transpose(
                    pb[2][0:1, 0:128], gsc[:, 1:2], ident_sb[:]), inc="st")
                k.wait("vector", t2)
                if c == 0:
                    k.wait("vector", k.tok("so"))
                pb_tok[2] = k.op("vector", lambda e, c=c: e.tensor_copy(
                    rowstage[0:1, c * 128:(c + 1) * 128],
                    pb[2][0:1, 0:128]), inc="sv")
                k.op("vector", lambda e: e.drain())
            vaqr = k.op("vector", lambda e: e.drain(), inc="sv")
            k.wait("sync", vaqr)
            k.op("sync", lambda e, g=g, T=T: e.dma_start(
                out=rows_d[D_AQM, g * TG: g * TG + T],
                in_=rowstage[0:1, 0:T]), inc="so", k=16)
        vt3 = k.tok("sv")
        for ch in range(NCH):
            if ch == 0:
                k.wait("vector", vt3)
            k.op("vector", lambda e, ch=ch, T=T, sf=sf: e.tensor_tensor(
                sf[:, 0:T], xnsrc(ch), mu2_b[:, 0:T], ALU.subtract))
            k.op("vector", lambda e: e.drain())
            k.op("vector", lambda e, T=T, sf=sf: e.tensor_tensor(
                sf[:, 0:T], sf[:, 0:T], r2_b[:, 0:T], ALU.mult))
            vch = k.op("vector", lambda e: e.drain(), inc="sv")
            k.wait("scalar", vch)
            sch = k.op("scalar", lambda e, ch=ch, T=T, sf=sf: e.activation(
                h_bf[:, ch * TG: ch * TG + T], sf[:, 0:T],
                ACTF.Identity, bias=pvec[:, 8 + ch:9 + ch], scale=pvec[:, ch:ch + 1]),
                inc="ss")
            k.wait("vector", sch)
        return k.tok("ss")

    def emit_kv(T, slot0):
        for m in range(8, 16):
            bank = 4 + (m % 2)
            mm_bank(bank)
            for kc in range(NCH):
                k.op("tensor", lambda e, m=m, kc=kc, bank=bank, T=T: e.matmul(
                    pb[bank][:, 0:T],
                    qk_t[:, kc * 2048 + m * 128: kc * 2048 + (m + 1) * 128],
                    h_bf[:, kc * TG: kc * TG + T],
                    start=(kc == 0), stop=(kc == NCH - 1)))
            tmm = k.mark("tensor", "st")
            k.wait("scalar", tmm)
            for sb in range(T // 256):
                slot = (slot0 + sb) % 3
                pb_tok[bank] = k.op(
                    "scalar", lambda e, m=m, sb=sb, slot=slot, bank=bank: e.activation(
                        kroll[:, (m - 8) * 768 + slot * 256:
                              (m - 8) * 768 + slot * 256 + 256],
                        pb[bank][:, sb * 256:(sb + 1) * 256],
                        ACTF.Identity, bias=0.0, scale=qscale[:, m:m + 1]), inc="ss")
        cnt = 0
        for mt in range(T // 128):
            for nn in range(2):
                bank = 4 + (cnt % 2)
                cnt += 1
                mm_bank(bank)
                for kc in range(NCH):
                    k.op("tensor", lambda e, mt=mt, nn=nn, kc=kc, bank=bank, T=T: e.matmul(
                        pb[bank][:, 0:512],
                        h_bf[:, kc * TG + mt * 128: kc * TG + (mt + 1) * 128],
                        v_t[:, kc * 1024 + nn * 512: kc * 1024 + (nn + 1) * 512],
                        start=(kc == 0), stop=(kc == NCH - 1)))
                tmm = k.mark("tensor", "st")
                k.wait("vector", tmm)
                u = ((slot0 + mt // 2) % 3) * 2 + (mt % 2)
                for hh in range(nn * 8, nn * 8 + 8):
                    k.op("vector", lambda e, u=u, hh=hh, nn=nn, bank=bank: e.tensor_copy(
                        vroll[:, u * VW + hh * 65: u * VW + hh * 65 + 64],
                        pb[bank][:, (hh - nn * 8) * 64:(hh - nn * 8) * 64 + 64]))
                pb_tok[bank] = k.op("vector", lambda e: e.drain(), inc="sv")

    k.waits("vector", [t_halo])

    def halosrc(ch):
        return halo_x[:, ch * HALO:(ch + 1) * HALO]

    hs = emit_h(halosrc, HALO, 0, False)
    k.wait("tensor", hs)
    emit_kv(HALO, 2)
    k.flush("p1_halo")

    for g in range(NG):
        def xsrc(ch, g=g):
            return x1c(ch, g)

        k.wait("scalar", k.tok("sv"))
        hs = emit_h(xsrc, TG, g, True)
        k.wait("tensor", hs)
        emit_kv(TG, (2 * g) % 3)
        for m in range(8):
            bank = 4 + (m % 2)
            mm_bank(bank)
            for kc in range(NCH):
                k.op("tensor", lambda e, m=m, kc=kc, bank=bank: e.matmul(
                    pb[bank][:, 0:TG],
                    qk_t[:, kc * 2048 + m * 128: kc * 2048 + (m + 1) * 128],
                    h_bf[:, kc * TG:(kc + 1) * TG],
                    start=(kc == 0), stop=(kc == NCH - 1)))
            tmm = k.mark("tensor", "st")
            k.wait("scalar", tmm)
            pb_tok[bank] = k.op("scalar", lambda e, m=m, bank=bank: e.activation(
                q_bf[:, m * TG:(m + 1) * TG], pb[bank][:, 0:TG],
                ACTF.Identity, bias=0.0, scale=qscale[:, m:m + 1]), inc="ss")
        k.flush(f"p1_qkv_g{g}")

        for blk in range(2):
            B = 2 * g + blk
            cslot = B % 3
            pslot = (B - 1) % 3
            mask_sb = maskA_sb if B == 0 else maskB_sb
            tqkv = k.tok("ss")
            probs_b = [probs_sb, sbf2]
            den_b = [mu_b[64:65, 0:256], mu_b[64:65, 256:512]]
            tsc = [None] * H; se = [None] * H; vp = [None] * H
            tpv = [None] * H; vden = [None] * H; tb = [None] * H
            vstt = [None] * H
            for i in range(H + 2):
                if i < H:
                    h = i
                    kch, poff = h // 2, (h % 2) * 64
                    b4 = 4 if h % 2 == 0 else 2
                    if h < 2:
                        k.wait("tensor", tqkv)
                        k.wait("tensor", pb_tok[b4])
                        k.wait("tensor", pb_tok[b4 + 1])
                    else:
                        k.wait("tensor", se[h - 2])
                    for kc in range(4):
                        slot = pslot if kc < 2 else cslot
                        sub = kc % 2
                        bank = b4 + kc // 2
                        k.op("tensor", lambda e, kch=kch, poff=poff, slot=slot,
                             sub=sub, bank=bank, kc=kc, blk=blk: e.matmul(
                            pb[bank][:, (kc % 2) * 256:(kc % 2) * 256 + 256],
                            kroll[poff:poff + 64,
                                  kch * 768 + slot * 256 + sub * 128:
                                  kch * 768 + slot * 256 + sub * 128 + 128],
                            q_bf[poff:poff + 64,
                                 kch * TG + blk * 256: kch * TG + blk * 256 + 256],
                            start=True, stop=True))
                    tsc[h] = k.mark("tensor", "st")
                if 0 <= i - 1 < H:
                    h = i - 1
                    pvb = 6 if h % 2 == 0 else 0
                    k.wait("tensor", vp[h])
                    if h < 2:
                        k.wait("tensor", pb_tok[pvb])
                    else:
                        k.wait("tensor", vden[h - 2])
                    for kc in range(4):
                        slot = pslot if kc < 2 else cslot
                        u = slot * 2 + (kc % 2)
                        k.op("tensor", lambda e, u=u, h=h, kc=kc, pvb=pvb: e.matmul(
                            pb[pvb][0:65, 0:256],
                            vroll[:, u * VW + h * 65: u * VW + (h + 1) * 65],
                            probs_b[h % 2][:, kc * 256:(kc + 1) * 256],
                            start=(kc == 0), stop=(kc == 3)))
                    tpv[h] = k.mark("tensor", "st")
                if 0 <= i - 2 < H:
                    h = i - 2
                    bcb = 7 if h % 2 == 0 else 1
                    k.wait("tensor", vden[h])
                    if h < 2:
                        k.wait("tensor", pb_tok[bcb])
                    else:
                        k.wait("tensor", vstt[h - 2])
                    tb[h] = k.op("tensor", lambda e, h=h, bcb=bcb: e.matmul(
                        pb[bcb][:, 0:256], onesf[64:65, :], den_b[h % 2],
                        start=True, stop=True), inc="st")
                if i < H:
                    h = i
                    b4 = 4 if h % 2 == 0 else 2
                    k.wait("scalar", tsc[h])
                    gd = tpv[h - 2] if h >= 2 else prev_tpv.get(h % 2)
                    k.wait("scalar", gd)
                    for kc in range(4):
                        bank = b4 + kc // 2
                        k.op("scalar", lambda e, kc=kc, bank=bank, h=h: e.activation(
                            probs_b[h % 2][:, kc * 256:(kc + 1) * 256],
                            pb[bank][:, (kc % 2) * 256:(kc % 2) * 256 + 256],
                            ACTF.Exp, bias=0.0, scale=1.0))
                    se[h] = k.op("scalar", lambda e: e.drain(), inc="ss")
                    pb_tok[b4] = se[h]
                    pb_tok[b4 + 1] = se[h]
                if i < H:
                    h = i
                    k.wait("vector", se[h])
                    k.op("vector", lambda e, mask_sb=mask_sb, h=h: e.tensor_tensor(
                        probs_b[h % 2][:], probs_b[h % 2][:], mask_sb[:], ALU.mult))
                    vp[h] = k.op("vector", lambda e: e.drain(), inc="sv")
                if 0 <= i - 1 < H:
                    h = i - 1
                    pvb = 6 if h % 2 == 0 else 0
                    kch, poff = h // 2, (h % 2) * 64
                    k.wait("vector", tpv[h])
                    gd = tb[h - 2] if h >= 2 else prev_tb.get(h % 2)
                    k.wait("vector", gd)
                    k.op("vector", lambda e, kch=kch, poff=poff, pvb=pvb:
                         e.tensor_copy(
                        araw[poff:poff + 64, kch * 256:(kch + 1) * 256],
                        pb[pvb][0:64, 0:256]))
                    k.op("vector", lambda e, h=h, pvb=pvb: e.tensor_copy(
                        den_b[h % 2], pb[pvb][64:65, 0:256]))
                    k.op("vector", lambda e: e.drain())
                    k.op("vector", lambda e, h=h: e.reciprocal(
                        den_b[h % 2], den_b[h % 2]))
                    vden[h] = k.op("vector", lambda e: e.drain(), inc="sv")
                    pb_tok[pvb] = vden[h]
                if 0 <= i - 2 < H:
                    h = i - 2
                    bcb = 7 if h % 2 == 0 else 1
                    kch, poff = h // 2, (h % 2) * 64
                    k.wait("vector", tb[h])
                    k.op("vector", lambda e, kch=kch, poff=poff, h=h, bcb=bcb:
                         e.scalar_tensor_tensor(
                        h_bf[poff:poff + 64, kch * TG: kch * TG + 256],
                        araw[poff:poff + 64, kch * 256:(kch + 1) * 256],
                        qscale[poff:poff + 64, 16 + kch:17 + kch],
                        pb[bcb][poff:poff + 64, 0:256],
                        ALU.mult, ALU.mult))
                    vstt[h] = k.op("vector", lambda e: e.drain(), inc="sv")
                    pb_tok[bcb] = vstt[h]
            prev_tb = {0: tb[H - 2], 1: tb[H - 1]}
            prev_tpv = {0: tpv[H - 2], 1: tpv[H - 1]}
            va = k.tok("sv")
            for m in range(NCH):
                bank = 4 + (m % 2)
                mm_bank(bank, (va,))
                for kc in range(NCH):
                    k.op("tensor", lambda e, m=m, kc=kc, bank=bank: e.matmul(
                        pb[bank][:, 0:256],
                        ow_t[:, kc * 1024 + m * 128: kc * 1024 + (m + 1) * 128],
                        h_bf[:, kc * TG: kc * TG + 256],
                        start=(kc == 0), stop=(kc == NCH - 1)))
                tmm = k.mark("tensor", "st")
                k.wait("vector", tmm)
                pb_tok[bank] = k.op(
                    "vector", lambda e, m=m, g=g, blk=blk, bank=bank:
                    e.scalar_tensor_tensor(
                        x1c(m, g, off=blk * 256, w=256),
                        pb[bank][:, 0:256], oscale[:, m:m + 1],
                        x1c(m, g, off=blk * 256, w=256),
                        ALU.mult, ALU.add), inc="sv")
                k.op("vector", lambda e: e.drain())
            k.flush(f"p1_att_g{g}b{blk}")

    k.barrier()
    k.flush("p1_done")

    if "l0f" in PARTS:
        wb = [pf(0, 2048), pf(2048, 2048)]
        sb2 = [pf(4096, 2048), pf(6144, 2048)]
        ob = [pbf(8192, 2048), pbf(9216, 2048)]
        stP = [k.stream("stPa", 2), k.stream("stPb", 2)]
        P2Q = ("sync", "gpsimd")
        stQ = [k.stream("stQa", 1), k.stream("stQb", 1)]
        wb_free = [None, None]
        sb2_free = [None, None]
        ob_free = [None, None]

        def tile_src(wap, flat, t):
            if flat:
                return wap[:, t * 2048:(t + 1) * 2048]
            kc, h = t // 2, t % 2
            return wap[kc * 128:(kc + 1) * 128, h * 2048:(h + 1) * 2048]

        def tile_dst(dram, flat, t):
            if flat:
                return dram[:, t * 2048:(t + 1) * 2048]
            kc, h = t // 2, t % 2
            return dram[:, kc * INNER + h * 2048: kc * INNER + (h + 1) * 2048]

        NT = 16
        for wi, (wap, dram, flat) in enumerate([
                (wgT, ternG, False), (wuT, ternU, False), (wdT, ternD, True)]):
            r2w = r2_b[:, wi:wi + 1]
            muw = mu2_b[:, wi:wi + 1]
            mmtok = [None] * NT
            for t in range(NT):
                b = t % 2
                k.wait(P2Q[b], wb_free[b])
                tl = stP[b](P2Q[b], lambda e, wap=wap, flat=flat, t=t, b=b:
                            e.dma_start(out=wb[b][:], in_=tile_src(wap, flat, t)))
                eng = "scalar" if t % 2 == 0 else "vector"
                k.wait(eng, tl)
                k.wait(eng, sb2_free[b])
                if eng == "scalar":
                    k.op("scalar", lambda e, b=b: e.activation(
                        sb2[b][:], wb[b][:], ACTF.Abs, bias=0.0, scale=1.0))
                    atok = k.op("scalar", lambda e: e.drain(), inc="ss")
                else:
                    k.op("vector", lambda e, b=b: e.scalar_tensor_tensor(
                        sb2[b][:], wb[b][:], -1.0, wb[b][:], ALU.mult, ALU.max))
                    atok = k.op("vector", lambda e: e.drain(), inc="sv")
                wb_free[b] = atok
                k.wait("tensor", atok)
                if t == 0:
                    k.wait("tensor", pb_tok[2])
                for sx in range(4):
                    k.op("tensor", lambda e, b=b, sx=sx, t=t: e.matmul(
                        pb[2][:, 0:512], onesf[:],
                        sb2[b][:, sx * 512:(sx + 1) * 512],
                        start=(t == 0 and sx == 0), stop=(t == NT - 1 and sx == 3)))
                mmtok[t] = k.mark("tensor", "st")
                sb2_free[b] = mmtok[t]
            k.wait("vector", mmtok[NT - 1])
            k.op("vector", lambda e, muw=muw: e.reduce_sum(
                muw, pb[2][:, 0:512], AX.X))
            k.op("vector", lambda e: e.drain())
            k.op("vector", lambda e, muw=muw: e.tensor_scalar(
                muw, muw, 1.0 / (D * INNER), 1e-5, ALU.mult, ALU.max))
            k.op("vector", lambda e: e.drain())
            k.op("vector", lambda e, wi=wi, muw=muw: e.tensor_copy(
                pvec[:, 16 + wi:17 + wi], muw))
            k.op("vector", lambda e: e.drain())
            k.op("vector", lambda e, r2w=r2w, muw=muw: e.reciprocal(r2w, muw))
            vr2 = k.op("vector", lambda e: e.drain(), inc="sv")
            pb_tok[2] = vr2
            k.flush(f"p2s_{wi}")
            for t in range(NT):
                b = t % 2
                k.wait(P2Q[b], wb_free[b])
                tl = stP[b](P2Q[b], lambda e, wap=wap, flat=flat, t=t, b=b:
                            e.dma_start(out=wb[b][:], in_=tile_src(wap, flat, t)))
                k.wait("scalar", tl)
                if t == 0:
                    k.wait("scalar", vr2)
                k.wait("scalar", sb2_free[b])
                k.op("scalar", lambda e, b=b, r2w=r2w: e.activation(
                    sb2[b][:], wb[b][:], ACTF.Identity, bias=0.0, scale=r2w))
                tq = k.op("scalar", lambda e: e.drain(), inc="ss")
                k.wait("vector", tq)
                k.wait("vector", ob_free[b])
                k.op("vector", lambda e, b=b: e.tensor_scalar(
                    wb[b][:], sb2[b][:], -0.5, None, ALU.is_lt, ALU.bypass))
                k.op("vector", lambda e: e.drain())
                k.op("vector", lambda e, b=b: e.scalar_tensor_tensor(
                    ob[b][:], sb2[b][:], 0.5, wb[b][:], ALU.is_gt, ALU.subtract))
                vtt = k.op("vector", lambda e: e.drain(), inc="sv")
                wb_free[b] = vtt
                sb2_free[b] = vtt
                k.wait(P2Q[1 - b], vtt)
                ob_free[b] = stQ[b](P2Q[1 - b], lambda e, dram=dram, flat=flat,
                                    t=t, b=b: e.dma_start(
                    out=tile_dst(dram, flat, t), in_=ob[b][:]))
            k.flush(f"p2t_{wi}")
        k.wait("sync", ob_free[0])
        k.wait("sync", ob_free[1])
        k.barrier()
        k.flush("p2_done")

    if "l1m" in PARTS:
        web = pbf(11264, 2048)
        stWL = k.stream("stWL", 2)
        stWS = k.stream("stWS", 2)
        wetoks = []
        i_w = 0
        for dout in range(NCH):
            for e_ in range(NEX):
                buf = i_w % 2
                if i_w >= 2:
                    k.wait("gpsimd", wetoks[i_w - 2])
                tg1 = stWL("gpsimd", lambda e, dout=dout, e_=e_, buf=buf:
                           e.dma_start(
                    out=web[:, buf * 1024:(buf + 1) * 1024],
                    in_=We[dout, e_].rearrange("(a p) m -> p a m", p=128)))
                k.wait("gpsimd", tg1)
                wetoks.append(stWS("gpsimd", lambda e, dout=dout, e_=e_, buf=buf:
                                   e.dma_start(
                    out=WeBf[dout, e_].rearrange("(a p) m -> p a m", p=128),
                    in_=web[:, buf * 1024:(buf + 1) * 1024])))
                i_w += 1
        if wetoks:
            k.wait("gpsimd", wetoks[-1])
            k.wait("gpsimd", wetoks[-2])

    if "l0f" in PARTS:
        gu = pf(0, 8192)
        xg3 = pf(8192, 2048)
        fct_b = pf(10240, 512)
        fct2_b = pf(10752, 512)
        stG = k.stream("stG", 4)
        stU = k.stream("stU", 4)
        stD = k.stream("stD", 4)
        aqx3 = regC[:, 0:2048]
        fsg = regA[:, 0:4096]
        fsu = regA[:, 4096:8192]
        fsd = regA[:, 8192:12288]
        T3 = TG3

        for g3 in range(NG3):
            k.wait("sync", k.tok("sv"))
            toks = [k.op("sync", lambda e, ch=ch, g3=g3: e.dma_start(
                out=xg3[:, ch * T3:(ch + 1) * T3],
                in_=xT[ch * 128:(ch + 1) * 128, HALO + g3 * T3: HALO + (g3 + 1) * T3]),
                inc="sd", k=16) for ch in range(NCH)]
            bcast_rows([(rowg(R_MU0, g3, t=T3), mu_b),
                        (rowg(R_R0, g3, t=T3), r_b)], T3)
            bcast_drows([(D_AQM, g3 * T3, mu2_b)], T3)
            k.wait("vector", toks[-1])
            k.op("vector", lambda e: e.reciprocal(r2_b[:, 0:T3], mu2_b[:, 0:T3]))
            k.op("vector", lambda e: e.drain())
            k.op("vector", lambda e: e.tensor_scalar(
                r2_b[:, 0:T3], r2_b[:, 0:T3], 127.0, None, ALU.mult, ALU.bypass))
            k.op("vector", lambda e: e.drain())
            k.op("vector", lambda e: e.tensor_scalar(
                fct_b[:, 0:T3], mu2_b[:, 0:T3], pvec[:, 16:17], 1.0 / 127.0,
                ALU.mult, ALU.mult))
            k.op("vector", lambda e: e.drain())
            k.op("vector", lambda e: e.tensor_scalar(
                fct2_b[:, 0:T3], mu2_b[:, 0:T3], pvec[:, 17:18], 1.0 / 127.0,
                ALU.mult, ALU.mult))
            k.op("vector", lambda e: e.drain())
            for ch in range(NCH):
                sl = xg3[:, ch * T3:(ch + 1) * T3]
                k.op("vector", lambda e, sl=sl: e.tensor_tensor(
                    sl, sl, mu_b[:, 0:T3], ALU.subtract))
                k.op("vector", lambda e: e.drain())
                k.op("vector", lambda e, sl=sl: e.tensor_tensor(
                    sl, sl, r_b[:, 0:T3], ALU.mult))
                k.op("vector", lambda e: e.drain())
                k.op("vector", lambda e, sl=sl: e.tensor_tensor(
                    sl, sl, r2_b[:, 0:T3], ALU.mult))
                k.op("vector", lambda e: e.drain())
                k.op("vector", lambda e, sl=sl: e.tensor_scalar(
                    sl, sl, MAGIC, -MAGIC, ALU.add, ALU.add))
                k.op("vector", lambda e: e.drain())
                k.op("vector", lambda e, ch=ch, sl=sl: e.tensor_scalar(
                    aqx3[:, ch * T3:(ch + 1) * T3], sl, -128.0, 127.0,
                    ALU.max, ALU.min))
                k.op("vector", lambda e: e.drain())
            vaq = k.op("vector", lambda e: e.drain(), inc="sv")
            if DEBUG and g3 == 0:
                k.wait("sync", vaq)
                k.op("sync", lambda e: e.dma_start(
                    out=dbg.bitcast(BF16)[0:128, 0:2048], in_=aqx3[:, 0:2048]),
                    inc="so", k=16)
                k.op("sync", lambda e: e.dma_start(
                    out=dbg[0:128, 4096:4096 + 256], in_=fct_b[:, 0:256]),
                    inc="so", k=16)
                k.op("sync", lambda e: e.dma_start(
                    out=dbg[0:128, 4352:4352 + 256], in_=fct2_b[:, 0:256]),
                    inc="so", k=16)
                k.op("sync", lambda e: e.dma_start(
                    out=dbg[0:128, 4608:4608 + 256], in_=r2_b[:, 0:256]),
                    inc="so", k=16)
                k.wait("vector", k.tok("so"))

            gutoks = []
            for m in range(32):
                buf = m % 4
                if m >= 4:
                    k.wait("sync", gutoks[m - 4])
                tg_ = stG("sync", lambda e, m=m, buf=buf: e.dma_start(
                    out=fsg[:, buf * 1024:(buf + 1) * 1024],
                    in_=ternG.rearrange("p (c x) -> p c x", x=INNER)[
                        :, :, m * 128:(m + 1) * 128]))
                tu_ = stU("sync", lambda e, m=m, buf=buf: e.dma_start(
                    out=fsu[:, buf * 1024:(buf + 1) * 1024],
                    in_=ternU.rearrange("p (c x) -> p c x", x=INNER)[
                        :, :, m * 128:(m + 1) * 128]))
                bg_ = 0 if (m % 2) == 0 else 2
                bu_ = 1 if (m % 2) == 0 else 3
                mm_bank(bg_, (tg_, tu_, vaq) if m < 2 else (tg_, tu_))
                for kc in range(NCH):
                    k.op("tensor", lambda e, m=m, kc=kc, buf=buf, bg_=bg_: e.matmul(
                        pb[bg_][:, 0:T3],
                        fsg[:, buf * 1024 + kc * 128: buf * 1024 + (kc + 1) * 128],
                        aqx3[:, kc * T3:(kc + 1) * T3],
                        start=(kc == 0), stop=(kc == NCH - 1)))
                mm_bank(bu_, (tu_,))
                for kc in range(NCH):
                    k.op("tensor", lambda e, m=m, kc=kc, buf=buf, bu_=bu_: e.matmul(
                        pb[bu_][:, 0:T3],
                        fsu[:, buf * 1024 + kc * 128: buf * 1024 + (kc + 1) * 128],
                        aqx3[:, kc * T3:(kc + 1) * T3],
                        start=(kc == 0), stop=(kc == NCH - 1)))
                tmm = k.mark("tensor", "st")
                sf = sbf2.bitcast(F32)
                k.wait("vector", tmm)
                k.op("vector", lambda e, bg_=bg_: e.tensor_tensor(
                    sf[:, 0:T3], pb[bg_][:, 0:T3], fct_b[:, 0:T3], ALU.mult))
                vg = k.op("vector", lambda e: e.drain(), inc="sv")
                k.wait("scalar", vg)
                ssil = k.op("scalar", lambda e, sf=sf: e.activation(
                    sf[:, T3:2 * T3], sf[:, 0:T3], ACTF.Silu, bias=0.0, scale=1.0),
                    inc="ss")
                k.wait("vector", ssil)
                gu_m = gu[:, m * T3:(m + 1) * T3]
                k.op("vector", lambda e, gu_m=gu_m, sf=sf, bu_=bu_: e.tensor_tensor(
                    gu_m, sf[:, T3:2 * T3], pb[bu_][:, 0:T3], ALU.mult))
                vgu = k.op("vector", lambda e: e.drain(), inc="sv")
                pb_tok[bg_] = vgu
                pb_tok[bu_] = vgu
                gutoks.append(tmm)
                if m == 0:
                    k.op("vector", lambda e, gu_m=gu_m: e.scalar_tensor_tensor(
                        r_b[:, 0:T3], gu_m, -1.0, gu_m, ALU.mult, ALU.max))
                else:
                    k.op("vector", lambda e, gu_m=gu_m, sf=sf:
                         e.scalar_tensor_tensor(
                        sf[:, 0:T3], gu_m, -1.0, gu_m, ALU.mult, ALU.max))
                    k.op("vector", lambda e: e.drain())
                    k.op("vector", lambda e, sf=sf: e.tensor_tensor(
                        r_b[:, 0:T3], r_b[:, 0:T3], sf[:, 0:T3], ALU.max))
                k.op("vector", lambda e: e.drain())
            vmx = k.op("vector", lambda e: e.drain(), inc="sv")
            t3 = None
            for c in range(T3 // 128):
                mm_bank(4, (vmx,) if c == 0 else ())
                t1 = k.op("tensor", lambda e, c=c: e.transpose(
                    pb[4][:, 0:128], r_b[:, c * 128:(c + 1) * 128],
                    ident_sb[:]), inc="st")
                k.wait("vector", t1)
                k.op("vector", lambda e: e.reduce_max(
                    gsc[:, 0:1], pb[4][:, 0:128], AX.X))
                vcl = k.op("vector", lambda e: e.drain(), inc="sv")
                pb_tok[4] = vcl
                mm_bank(5, (vcl,))
                t2 = k.op("tensor", lambda e: e.transpose(
                    pb[5][0:1, 0:128], gsc[:, 0:1], ident_sb[:]), inc="st")
                k.wait("vector", t2)
                k.wait("vector", t3)
                vrow = k.op("vector", lambda e: e.tensor_copy(
                    rowstage[0:1, 0:128], pb[5][0:1, 0:128]), inc="sv")
                pb_tok[5] = vrow
                k.op("vector", lambda e: e.drain())
                mm_bank(4, (vrow,))
                t3 = k.op("tensor", lambda e: e.matmul(
                    pb[4][:, 0:128], onesf[0:1, :], rowstage[0:1, 0:128],
                    start=True, stop=True), inc="st")
                k.wait("vector", t3)
                pb_tok[4] = k.op("vector", lambda e, c=c: e.tensor_copy(
                    mu_b[:, c * 128:(c + 1) * 128], pb[4][:, 0:128]), inc="sv")
                k.op("vector", lambda e: e.drain())
            k.op("vector", lambda e: e.tensor_tensor(
                mu_b[:, 0:T3], mu_b[:, 0:T3], fct2_b[:, 0:T3], ALU.mult))
            k.op("vector", lambda e: e.drain())
            k.op("vector", lambda e: e.tensor_scalar(
                mu_b[:, 0:T3], mu_b[:, 0:T3], 1e-5, None, ALU.max, ALU.bypass))
            k.op("vector", lambda e: e.drain())
            k.op("vector", lambda e: e.tensor_scalar(
                r2_b[:, 0:T3], mu_b[:, 0:T3], pvec[:, 18:19], 1.0 / 127.0,
                ALU.mult, ALU.mult))
            k.op("vector", lambda e: e.drain())
            k.op("vector", lambda e: e.reciprocal(mu_b[:, 0:T3], mu_b[:, 0:T3]))
            k.op("vector", lambda e: e.drain())
            k.op("vector", lambda e: e.tensor_tensor(
                mu_b[:, 0:T3], mu_b[:, 0:T3], fct2_b[:, 0:T3], ALU.mult))
            k.op("vector", lambda e: e.drain())
            k.op("vector", lambda e: e.tensor_scalar(
                mu_b[:, 0:T3], mu_b[:, 0:T3], 127.0, None, ALU.mult, ALU.bypass))
            k.op("vector", lambda e: e.drain())
            for m in range(32):
                gu_m = gu[:, m * T3:(m + 1) * T3]
                aq2_m = gu.bitcast(BF16)[:, 2 * m * T3: 2 * m * T3 + T3]
                k.op("vector", lambda e, gu_m=gu_m: e.tensor_tensor(
                    gu_m, gu_m, mu_b[:, 0:T3], ALU.mult))
                k.op("vector", lambda e: e.drain())
                k.op("vector", lambda e, gu_m=gu_m: e.tensor_scalar(
                    gu_m, gu_m, MAGIC, -MAGIC, ALU.add, ALU.add))
                k.op("vector", lambda e: e.drain())
                k.op("vector", lambda e, gu_m=gu_m, aq2_m=aq2_m: e.tensor_scalar(
                    aq2_m, gu_m, -128.0, 127.0, ALU.max, ALU.min))
                k.op("vector", lambda e: e.drain())
            vaq2 = k.op("vector", lambda e: e.drain(), inc="sv")
            if DEBUG and g3 == 0:
                k.wait("sync", vaq2)
                for mm_ in range(4):
                    k.op("sync", lambda e, mm_=mm_: e.dma_start(
                        out=dbg.bitcast(BF16)[0:128, 16384 + mm_ * 256:
                                              16384 + (mm_ + 1) * 256],
                        in_=gu.bitcast(BF16)[:, 2 * mm_ * T3: 2 * mm_ * T3 + T3]),
                        inc="so", k=16)
                k.op("sync", lambda e: e.dma_start(
                    out=dbg[0:128, 9216:9216 + 256], in_=mu_b[:, 0:256]),
                    inc="so", k=16)
                k.wait("vector", k.tok("so"))
            for i in range(8):
                k.wait("tensor", pb_tok[i])
            dttoks = []
            for m in range(32):
                buf = m % 4
                if m >= 4:
                    k.wait("sync", dttoks[m - 4])
                tw = stD("sync", lambda e, m=m, buf=buf: e.dma_start(
                    out=fsd[:, buf * 1024:(buf + 1) * 1024],
                    in_=ternD[:, m * D:(m + 1) * D]))
                k.wait("tensor", tw)
                if m == 0:
                    k.wait("tensor", vaq2)
                for dout in range(NCH):
                    k.op("tensor", lambda e, m=m, buf=buf, dout=dout: e.matmul(
                        pb[dout][:, 0:T3],
                        fsd[:, buf * 1024 + dout * 128: buf * 1024 + (dout + 1) * 128],
                        gu.bitcast(BF16)[:, 2 * m * T3: 2 * m * T3 + T3],
                        start=(m == 0), stop=(m == 31)))
                dttoks.append(k.mark("tensor", "st"))
            tdn = dttoks[-1]
            k.wait("vector", tdn)
            sf = sbf2.bitcast(F32)
            for dout in range(NCH):
                k.op("vector", lambda e, dout=dout, sf=sf: e.tensor_tensor(
                    sf[:, 0:T3], pb[dout][:, 0:T3], r2_b[:, 0:T3], ALU.mult))
                k.op("vector", lambda e: e.drain())
                pb_tok[dout] = k.op(
                    "vector", lambda e, dout=dout, g3=g3, sf=sf: e.tensor_tensor(
                        x1c(dout, g3, t=T3), x1c(dout, g3, t=T3), sf[:, 0:T3],
                        ALU.add), inc="sv")
                k.op("vector", lambda e: e.drain())
            k.flush(f"p3_g{g3}")
        k.barrier()
        k.flush("p3_done")


    if "l1a" in PARTS:
        xn1_bf = regA
        ucx = pbf(0, 16384)
        wslot = pbf(8192, 8192)
        wsf = pf(8192, 4096)
        decb = wsf[:, 0:2048]
        xb = regB[:, 0:4096]
        sc32 = regB.bitcast(F32)

        twt = []
        for kc in range(NCH):
            twt.append(k.op("gpsimd", lambda e, kc=kc: e.dma_start(
                out=wslot[:, kc * 1024:(kc + 1) * 1024],
                in_=Wt[kc * 128:(kc + 1) * 128, :]), inc="sg", k=16))

        for g in range(NG):
            k.wait("vector", k.tok("so"))
            vst0 = ln_stats(lambda ch, g=g: x1c(ch, g), TG, 1e-5, mu_b, r_b)
            k.wait("sync", vst0)
            k.op("sync", lambda e, g=g: e.dma_start(
                out=rows_d[D_MUX, g * TG:(g + 1) * TG], in_=mu_b[0:1, 0:TG]),
                inc="so", k=16)
            k.op("sync", lambda e, g=g: e.dma_start(
                out=rows_d[D_RX, g * TG:(g + 1) * TG], in_=r_b[0:1, 0:TG]),
                inc="so", k=16)
            vt = k.tok("sv")
            k.wait("vector", vt)
            sf = sbf2.bitcast(F32)
            for ch in range(NCH):
                k.op("vector", lambda e, ch=ch, g=g, sf=sf: e.tensor_tensor(
                    sf[:, 0:TG], x1c(ch, g), mu_b[:, 0:TG], ALU.subtract))
                k.op("vector", lambda e: e.drain())
                k.op("vector", lambda e, ch=ch, g=g, sf=sf: e.tensor_tensor(
                    xn1_bf[:, ch * 2048 + g * TG: ch * 2048 + (g + 1) * TG],
                    sf[:, 0:TG], r_b[:, 0:TG], ALU.mult))
                k.op("vector", lambda e: e.drain())
            k.op("vector", lambda e: e.drain(), inc="sv")
            vst1 = ln_stats(lambda ch, g=g: xn1_bf[:, ch * 2048 + g * TG:
                                            ch * 2048 + (g + 1) * TG],
                            TG, 1e-6, mu2_b, r2_b)
            k.wait("sync", vst1)
            k.op("sync", lambda e, g=g: e.dma_start(
                out=rows_d[D_MU1, g * TG:(g + 1) * TG], in_=mu2_b[0:1, 0:TG]),
                inc="so", k=16)
            k.op("sync", lambda e, g=g: e.dma_start(
                out=rows_d[D_R1, g * TG:(g + 1) * TG], in_=r2_b[0:1, 0:TG]),
                inc="so", k=16)
            vt2 = k.tok("sv")
            k.wait("vector", vt2)
            for ch in range(NCH):
                k.op("vector", lambda e, ch=ch, g=g, sf=sf: e.tensor_tensor(
                    sf[:, 0:TG], x1c(ch, g), mu_b[:, 0:TG], ALU.subtract))
                k.op("vector", lambda e: e.drain())
                k.op("vector", lambda e, sf=sf: e.tensor_tensor(
                    sf[:, 0:TG], sf[:, 0:TG], r_b[:, 0:TG], ALU.mult))
                k.op("vector", lambda e: e.drain())
                k.op("vector", lambda e, sf=sf: e.tensor_tensor(
                    sf[:, 0:TG], sf[:, 0:TG], mu2_b[:, 0:TG], ALU.subtract))
                k.op("vector", lambda e: e.drain())
                k.op("vector", lambda e, sf=sf: e.tensor_tensor(
                    sf[:, 0:TG], sf[:, 0:TG], r2_b[:, 0:TG], ALU.mult))
                vc = k.op("vector", lambda e: e.drain(), inc="sv")
                k.wait("scalar", vc)
                sc = k.op("scalar", lambda e, ch=ch, sf=sf: e.activation(
                    xb[:, ch * TG:(ch + 1) * TG], sf[:, 0:TG], ACTF.Identity,
                    bias=lpar[:, 8 + ch:9 + ch], scale=lpar[:, ch:ch + 1]), inc="ss")
                k.wait("vector", sc)
            hx = k.tok("ss")
            k.wait("tensor", twt[-1])
            k.wait("tensor", hx)
            for m in range(NCH):
                bank = 4 + (m % 2)
                mm_bank(bank)
                for kc in range(NCH):
                    k.op("tensor", lambda e, m=m, kc=kc, bank=bank: e.matmul(
                        pb[bank][:, 0:TG],
                        wslot[:, kc * 1024 + m * 128: kc * 1024 + (m + 1) * 128],
                        xb[:, kc * TG:(kc + 1) * TG],
                        start=(kc == 0), stop=(kc == NCH - 1)))
                tmm = k.mark("tensor", "st")
                k.wait("scalar", tmm)
                pb_tok[bank] = k.op("scalar", lambda e, m=m, g=g, bank=bank:
                                    e.activation(
                    ucx[:, m * 2048 + g * TG: m * 2048 + (g + 1) * TG],
                    pb[bank][:, 0:TG], ACTF.Identity, bias=0.0,
                    scale=betav[:, m:m + 1]), inc="ss")
            k.flush(f"p4a_g{g}")

        uw = k.tok("ss")
        u32v = wsf[:, 2048:4096]
        k.waits("vector", [uw])
        for ch in range(NCH):
            k.op("vector", lambda e: e.memset(decb[:, 0:2048], 1.0))
            k.op("vector", lambda e: e.drain())
            k.op("vector", lambda e, ch=ch: e.tensor_scalar(
                decb[:, 0:2048], decb[:, 0:2048], dvec[:, ch:ch + 1], None,
                ALU.mult, ALU.bypass))
            k.op("vector", lambda e: e.drain())
            k.op("vector", lambda e, ch=ch: e.tensor_copy(
                u32v[:, 0:2048], ucx[:, ch * 2048:(ch + 1) * 2048]))
            k.op("vector", lambda e: e.drain())
            k.op("vector", lambda e: e.tensor_tensor_scan(
                sc32[:, 0:2048], decb[:, 0:2048],
                u32v[:, 0:2048], 0.0, ALU.mult, ALU.add))
            k.op("vector", lambda e: e.drain())
            k.op("vector", lambda e, ch=ch: e.tensor_copy(
                carry_sb[:, ch:ch + 1], sc32[:, 2047:2048]))
            k.op("vector", lambda e: e.drain())
        vst = k.tok("sv")
        bcast_oh_done = None
        mm_bank(0, (vst,))
        tb = k.op("tensor", lambda e: e.matmul(
            pb[0][:, 0:CORES], ones1f[0:1, :], oh2_sb[0:1, :],
            start=True, stop=True), inc="st")
        k.wait("vector", tb)
        pb_tok[0] = k.op("vector", lambda e: e.tensor_copy(
            oh2_sb[:, 0:CORES], pb[0][:, 0:CORES]), inc="sv")
        k.op("vector", lambda e: e.drain())
        for r_ in range(CORES):
            k.op("vector", lambda e, r_=r_: e.tensor_scalar(
                ccst[:, r_ * NCH:(r_ + 1) * NCH], carry_sb[:],
                oh2_sb[:, r_:r_ + 1], None, ALU.mult, ALU.bypass))
        vcc = k.op("vector", lambda e: e.drain(), inc="sv")
        k.wait("sync", vcc)
        dts = [k.op("sync", lambda e, r_=r_: e.dma_start(
            out=cc_in[r_ * 128:(r_ + 1) * 128, :],
            in_=ccst[:, r_ * NCH:(r_ + 1) * NCH]), inc="so", k=16)
            for r_ in range(CORES)]
        k.wait("gpsimd", dts[-1])
        tcc = k.op("gpsimd", lambda e: e.collective_compute(
            "ReduceScatter", ALU.add, replica_groups=[list(range(CORES))],
            ins=[cc_in[:, :].opt()], outs=[cc_out[:, :].opt()]), inc="cc")
        k.wait("gpsimd", tcc)
        tccd = k.op("gpsimd", lambda e: e.dma_start(out=ccres[:], in_=cc_out[:, :]),
                    inc="sg", k=16)
        k.flush("p4b_scan1")

        gbf = rowsb.bitcast(BF16)[:, 0:2048]
        wsfr = wsf
        xfr = wsfr[:, 0:2048]
        wg1v = wsfr[:, 2048:4096]
        wg2v = r2_b[:, 0:80]
        stWG = k.stream("stWG", 2)
        k.wait("sync", k.tok("sv"))
        twg1 = stWG("sync", lambda e: e.dma_start(
            out=wg1v, in_=wg1.rearrange("(a p) m -> p a m", p=128)))
        twg2l = stWG("sync", lambda e: e.dma_start(
            out=wg2v, in_=wg2.rearrange("(a p) m -> p a m", p=128)))
        k.wait("tensor", twg1)
        k.wait("tensor", twg2l)
        T4 = 256
        k.wait("sync", k.tok("so"))
        for gh in range(8):
            bcast_drows([(D_MUX, gh * T4, mu_b), (D_RX, gh * T4, r_b)], T4)
            vt = k.tok("sv")
            k.wait("vector", vt)
            sf = sbf2.bitcast(F32)
            for ch in range(NCH):
                k.op("vector", lambda e, ch=ch, gh=gh, sf=sf: e.tensor_tensor(
                    sf[:, 0:T4], x1[:, ch * TLOC + gh * T4: ch * TLOC + (gh + 1) * T4],
                    mu_b[:, 0:T4], ALU.subtract))
                k.op("vector", lambda e: e.drain())
                k.op("vector", lambda e, ch=ch, sf=sf: e.tensor_tensor(
                    xfr[:, ch * T4:(ch + 1) * T4], sf[:, 0:T4],
                    r_b[:, 0:T4], ALU.mult))
                k.op("vector", lambda e: e.drain())
            vx = k.op("vector", lambda e: e.drain(), inc="sv")
            for m2 in range(2):
                mm_bank(2 + m2, (vx,) if m2 == 0 else ())
                for kc in range(NCH):
                    k.op("tensor", lambda e, m2=m2, kc=kc: e.matmul(
                        pb[2 + m2][:, 0:T4],
                        wg1v[:, kc * 256 + m2 * 128: kc * 256 + (m2 + 1) * 128],
                        xfr[:, kc * T4:(kc + 1) * T4],
                        start=(kc == 0), stop=(kc == NCH - 1)))
            tg1 = k.mark("tensor", "st")
            k.wait("scalar", tg1)
            for m2 in range(2):
                pb_tok[2 + m2] = k.op("scalar", lambda e, m2=m2: e.activation(
                    probs_sb.bitcast(F32)[:, m2 * T4:(m2 + 1) * T4],
                    pb[2 + m2][:, 0:T4], ACTF.Gelu, bias=bg1_sb[:, m2:m2 + 1],
                    scale=1.0), inc="ss")
            sg1 = k.tok("ss")
            mm_bank(4, (sg1,))
            for m2 in range(2):
                k.op("tensor", lambda e, m2=m2: e.matmul(
                    pb[4][0:40, 0:T4], wg2v[:, m2 * 40:(m2 + 1) * 40],
                    probs_sb.bitcast(F32)[:, m2 * T4:(m2 + 1) * T4],
                    start=(m2 == 0), stop=(m2 == 1)))
            tg2 = k.mark("tensor", "st")
            k.wait("scalar", tg2)
            slog = k.op("scalar", lambda e: e.activation(
                mu2_b[0:20, 0:T4], pb[4][0:20, 0:T4], ACTF.Identity,
                bias=bg2_sb[0:20, 0:1], scale=1.0), inc="ss")
            pb_tok[4] = slog
            for q in range(2):
                mm_bank(5, (slog,) if q == 0 else ())
                tq = k.op("tensor", lambda e, q=q: e.transpose(
                    pb[5][0:128, 0:20], mu2_b[0:20, q * 128:(q + 1) * 128],
                    ident_sb[0:20, 0:20]), inc="st")
                k.wait("vector", tq)
                k.op("vector", lambda e: e.tensor_copy(gtm[:, 0:20], pb[5][:, 0:20]))
                k.op("vector", lambda e: e.drain())
                k.op("vector", lambda e: e.reduce_max(gsc[:, 0:1], gtm[:, 0:20], AX.X))
                vmx = k.op("vector", lambda e: e.drain(), inc="sv")
                k.wait("scalar", vmx)
                k.op("vector", lambda e: e.tensor_scalar(
                    gsc[:, 1:2], gsc[:, 0:1], -1.0, None, ALU.mult, ALU.bypass))
                vneg = k.op("vector", lambda e: e.drain(), inc="sv")
                k.wait("scalar", vneg)
                sexp = k.op("scalar", lambda e: e.activation(
                    gtm[:, 32:52], gtm[:, 0:20], ACTF.Exp, bias=gsc[:, 1:2],
                    scale=1.0, accum_out=gsc[:, 2:3]), inc="ss")
                k.wait("vector", sexp)
                k.op("vector", lambda e: e.reciprocal(gsc[:, 3:4], gsc[:, 2:3]))
                k.op("vector", lambda e: e.drain())
                k.op("vector", lambda e: e.tensor_scalar(
                    gtm[:, 32:52], gtm[:, 32:52], gsc[:, 3:4], None,
                    ALU.mult, ALU.bypass))
                k.op("vector", lambda e: e.drain())
                k.op("vector", lambda e: e.reduce_max(gsc[:, 0:1], gtm[:, 32:52], AX.X))
                k.op("vector", lambda e: e.drain())
                k.op("vector", lambda e: e.tensor_scalar(
                    gtm[:, 64:84], gtm[:, 32:52], gsc[:, 0:1], None,
                    ALU.is_equal, ALU.bypass))
                k.op("vector", lambda e: e.drain())
                k.op("vector", lambda e: e.reduce_sum(gsc[:, 1:2], gtm[:, 64:84], AX.X))
                k.op("vector", lambda e: e.drain())
                k.op("vector", lambda e: e.scalar_tensor_tensor(
                    gtm[:, 64:84], gtm[:, 64:84], -2.0, gtm[:, 32:52],
                    ALU.mult, ALU.add))
                k.op("vector", lambda e: e.drain())
                k.op("vector", lambda e: e.reduce_max(gsc[:, 2:3], gtm[:, 64:84], AX.X))
                k.op("vector", lambda e: e.drain())
                k.op("vector", lambda e: e.tensor_scalar(
                    gsc[:, 4:5], gsc[:, 1:2], 2.0, None, ALU.is_ge, ALU.bypass))
                k.op("vector", lambda e: e.drain())
                k.op("vector", lambda e: e.tensor_tensor(
                    gsc[:, 5:6], gsc[:, 0:1], gsc[:, 2:3], ALU.subtract))
                k.op("vector", lambda e: e.drain())
                k.op("vector", lambda e: e.tensor_tensor(
                    gsc[:, 5:6], gsc[:, 5:6], gsc[:, 4:5], ALU.mult))
                k.op("vector", lambda e: e.drain())
                k.op("vector", lambda e: e.tensor_tensor(
                    gsc[:, 5:6], gsc[:, 5:6], gsc[:, 2:3], ALU.add))
                k.op("vector", lambda e: e.drain())
                k.op("vector", lambda e: e.tensor_scalar(
                    gtm[:, 96:116], gtm[:, 32:52], gsc[:, 5:6], None,
                    ALU.is_ge, ALU.bypass))
                k.op("vector", lambda e: e.drain())
                k.op("vector", lambda e: e.tensor_tensor(
                    gtm[:, 96:116], gtm[:, 96:116], gtm[:, 32:52], ALU.mult))
                vg2 = k.op("vector", lambda e: e.drain(), inc="sv")
                mm_bank(6, (vg2,))
                tq2 = k.op("tensor", lambda e: e.transpose(
                    pb[6][0:20, 0:128], gtm[:, 96:116], ident_sb[:]), inc="st")
                k.wait("vector", tq2)
                pb_tok[6] = k.op("vector", lambda e, gh=gh, q=q: e.tensor_copy(
                    gbf[0:20, gh * T4 + q * 128: gh * T4 + (q + 1) * 128],
                    pb[6][0:20, 0:128]), inc="sv")
                k.op("vector", lambda e: e.drain())
                pb_tok[5] = pb_tok[6]
        k.flush("p4c_gating")

        k.wait("vector", k.tok("cc"))
        k.waits("vector", [tccd])
        for ch in range(NCH):
            k.op("vector", lambda e: e.memset(decb[:, 0:2048], 1.0))
            k.op("vector", lambda e: e.drain())
            k.op("vector", lambda e, ch=ch: e.tensor_scalar(
                decb[:, 0:2048], decb[:, 0:2048], dvec[:, ch:ch + 1], None,
                ALU.mult, ALU.bypass))
            k.op("vector", lambda e: e.drain())
            k.op("vector", lambda e, ch=ch: e.tensor_copy(
                u32v[:, 0:2048], ucx[:, ch * 2048:(ch + 1) * 2048]))
            k.op("vector", lambda e: e.drain())
            k.op("vector", lambda e, ch=ch: e.tensor_tensor_scan(
                ucx[:, ch * 2048:(ch + 1) * 2048], decb[:, 0:2048],
                u32v[:, 0:2048], ccres[:, ch:ch + 1], ALU.mult, ALU.add))
            k.op("vector", lambda e: e.drain())
        k.op("vector", lambda e: e.drain(), inc="sv")
        k.flush("p4d_scan2")

        scan2done = k.tok("sv")
        lncx = regB[:, 0:4096]
        out_bf = regB[:, 4096:8192]
        res_bf = regC[:, 0:4096]
        z_bf = regC[:, 4096:8192]
        s32a = wsf[:, 0:512]
        s32b = wsf[:, 512:1024]

        for g in range(NG):
            ln_stats(lambda ch, g=g: ucx[:, ch * 2048 + g * TG:
                                         ch * 2048 + (g + 1) * TG],
                     TG, 1e-6, mu_b, r_b)
            vt = k.tok("sv")
            k.wait("vector", vt)
            sf = sbf2.bitcast(F32)
            for ch in range(NCH):
                k.op("vector", lambda e, ch=ch, g=g, sf=sf: e.tensor_copy(
                    sf[:, 0:TG], ucx[:, ch * 2048 + g * TG: ch * 2048 + (g + 1) * TG]))
                k.op("vector", lambda e: e.drain())
                k.op("vector", lambda e, sf=sf: e.tensor_tensor(
                    sf[:, 0:TG], sf[:, 0:TG], mu_b[:, 0:TG], ALU.subtract))
                k.op("vector", lambda e: e.drain())
                k.op("vector", lambda e, sf=sf: e.tensor_tensor(
                    sf[:, 0:TG], sf[:, 0:TG], r_b[:, 0:TG], ALU.mult))
                vc = k.op("vector", lambda e: e.drain(), inc="sv")
                k.wait("scalar", vc)
                scx = k.op("scalar", lambda e, ch=ch, sf=sf: e.activation(
                    lncx[:, ch * TG:(ch + 1) * TG], sf[:, 0:TG], ACTF.Identity,
                    bias=lpar[:, 24 + ch:25 + ch], scale=lpar[:, 16 + ch:17 + ch]),
                    inc="ss")
                k.wait("vector", scx)
            hlncx = k.tok("ss")
            k.wait("gpsimd", scan2done)
            k.wait("gpsimd", k.tok("st"))
            k.wait("gpsimd", k.tok("sv"))
            two = [k.op("gpsimd", lambda e, kc=kc: e.dma_start(
                out=wslot[:, kc * 1024:(kc + 1) * 1024],
                in_=Wo[kc * 128:(kc + 1) * 128, :]), inc="sg", k=16)
                for kc in range(NCH)]
            k.waits("tensor", [two[-1], hlncx])
            for m in range(NCH):
                bank = 2 + (m % 2)
                mm_bank(bank)
                for kc in range(NCH):
                    k.op("tensor", lambda e, m=m, kc=kc, bank=bank: e.matmul(
                        pb[bank][:, 0:TG],
                        wslot[:, kc * 1024 + m * 128: kc * 1024 + (m + 1) * 128],
                        lncx[:, kc * TG:(kc + 1) * TG],
                        start=(kc == 0), stop=(kc == NCH - 1)))
                tmm = k.mark("tensor", "st")
                k.wait("scalar", tmm)
                pb_tok[bank] = k.op("scalar", lambda e, m=m, bank=bank: e.activation(
                    out_bf[:, m * TG:(m + 1) * TG], pb[bank][:, 0:TG],
                    ACTF.Identity, bias=0.0, scale=1.0), inc="ss")
            bcast_drows([(D_MUX, g * TG, mu_b), (D_RX, g * TG, r_b),
                         (D_MU1, g * TG, mu2_b), (D_R1, g * TG, r2_b)], TG)
            vt2 = k.tok("sv")
            k.wait("vector", vt2)
            for ch in range(NCH):
                k.op("vector", lambda e, ch=ch, g=g, sf=sf: e.tensor_tensor(
                    sf[:, 0:TG], x1c(ch, g), mu_b[:, 0:TG], ALU.subtract))
                k.op("vector", lambda e: e.drain())
                k.op("vector", lambda e, sf=sf: e.tensor_tensor(
                    sf[:, 0:TG], sf[:, 0:TG], r_b[:, 0:TG], ALU.mult))
                k.op("vector", lambda e: e.drain())
                k.op("vector", lambda e, sf=sf: e.tensor_tensor(
                    sf[:, 0:TG], sf[:, 0:TG], mu2_b[:, 0:TG], ALU.subtract))
                k.op("vector", lambda e: e.drain())
                k.op("vector", lambda e, sf=sf: e.tensor_tensor(
                    sf[:, 0:TG], sf[:, 0:TG], r2_b[:, 0:TG], ALU.mult))
                vc = k.op("vector", lambda e: e.drain(), inc="sv")
                k.wait("scalar", vc)
                sres = k.op("scalar", lambda e, ch=ch, sf=sf: e.activation(
                    res_bf[:, ch * TG:(ch + 1) * TG], sf[:, 0:TG], ACTF.Identity,
                    bias=lpar[:, 8 + ch:9 + ch], scale=lpar[:, ch:ch + 1]), inc="ss")
                k.wait("vector", sres)
            hres = k.tok("ss")
            k.wait("gpsimd", k.tok("st"))
            twg = [k.op("gpsimd", lambda e, kc=kc: e.dma_start(
                out=wslot[:, kc * 1024:(kc + 1) * 1024],
                in_=Wg[kc * 128:(kc + 1) * 128, :]), inc="sg", k=16)
                for kc in range(NCH)]
            k.wait("tensor", twg[-1])
            k.wait("tensor", hres)
            for i in range(8):
                k.wait("tensor", pb_tok[i])
            for m in range(NCH):
                for kc in range(NCH):
                    k.op("tensor", lambda e, m=m, kc=kc: e.matmul(
                        pb[m][:, 0:TG],
                        wslot[:, kc * 1024 + m * 128: kc * 1024 + (m + 1) * 128],
                        res_bf[:, kc * TG:(kc + 1) * TG],
                        start=(kc == 0), stop=False))
            th1 = k.mark("tensor", "st")
            k.wait("gpsimd", th1)
            twg2 = [k.op("gpsimd", lambda e, kc=kc: e.dma_start(
                out=wslot[:, kc * 1024:(kc + 1) * 1024],
                in_=Wg[D + kc * 128: D + (kc + 1) * 128, :]), inc="sg", k=16)
                for kc in range(NCH)]
            k.wait("tensor", twg2[-1])
            for m in range(NCH):
                for kc in range(NCH):
                    k.op("tensor", lambda e, m=m, kc=kc: e.matmul(
                        pb[m][:, 0:TG],
                        wslot[:, kc * 1024 + m * 128: kc * 1024 + (m + 1) * 128],
                        out_bf[:, kc * TG:(kc + 1) * TG],
                        start=False, stop=(kc == NCH - 1)))
            th2 = k.mark("tensor", "st")
            k.wait("scalar", th2)
            for m in range(NCH):
                pb_tok[m] = k.op("scalar", lambda e, m=m: e.activation(
                    z_bf[:, m * TG:(m + 1) * TG], pb[m][:, 0:TG], ACTF.Identity,
                    bias=lpar[:, 48 + m:49 + m], scale=1.0), inc="ss")
            zs = k.tok("ss")
            k.wait("vector", zs)
            ln_stats(lambda ch: z_bf[:, ch * TG:(ch + 1) * TG], TG, 1e-6, mu_b, r_b)
            vt3 = k.tok("sv")
            k.wait("vector", vt3)
            for ch in range(NCH):
                k.op("vector", lambda e, ch=ch, sf=sf: e.tensor_copy(
                    sf[:, 0:TG], z_bf[:, ch * TG:(ch + 1) * TG]))
                k.op("vector", lambda e: e.drain())
                k.op("vector", lambda e, sf=sf: e.tensor_tensor(
                    sf[:, 0:TG], sf[:, 0:TG], mu_b[:, 0:TG], ALU.subtract))
                k.op("vector", lambda e: e.drain())
                k.op("vector", lambda e, sf=sf: e.tensor_tensor(
                    sf[:, 0:TG], sf[:, 0:TG], r_b[:, 0:TG], ALU.mult))
                vc = k.op("vector", lambda e: e.drain(), inc="sv")
                k.wait("scalar", vc)
                sgate = k.op("scalar", lambda e, ch=ch, sf=sf: e.activation(
                    s32a[:, 0:TG], sf[:, 0:TG], ACTF.Sigmoid,
                    bias=lpar[:, 40 + ch:41 + ch], scale=lpar[:, 32 + ch:33 + ch]),
                    inc="ss")
                k.wait("vector", sgate)
                k.op("vector", lambda e, ch=ch: e.tensor_tensor(
                    s32b[:, 0:TG], out_bf[:, ch * TG:(ch + 1) * TG],
                    res_bf[:, ch * TG:(ch + 1) * TG], ALU.subtract))
                k.op("vector", lambda e: e.drain())
                k.op("vector", lambda e: e.tensor_tensor(
                    s32b[:, 0:TG], s32b[:, 0:TG], s32a[:, 0:TG], ALU.mult))
                k.op("vector", lambda e: e.drain())
                k.op("vector", lambda e, ch=ch, sf=sf: e.tensor_copy(
                    sf[:, 0:TG], res_bf[:, ch * TG:(ch + 1) * TG]))
                k.op("vector", lambda e: e.drain())
                k.op("vector", lambda e, sf=sf: e.tensor_tensor(
                    s32b[:, 0:TG], s32b[:, 0:TG], sf[:, 0:TG], ALU.add))
                k.op("vector", lambda e: e.drain())
                k.op("vector", lambda e, ch=ch, g=g: e.tensor_tensor(
                    x1c(ch, g), x1c(ch, g), s32b[:, 0:TG], ALU.add))
                k.op("vector", lambda e: e.drain(), inc="sv")
            k.flush(f"p4d_g{g}")
        k.barrier()
        k.flush("p4_done")

    if "l1m" in PARTS:
        xn1_bf = regA
        gbf = rowsb.bitcast(BF16)[:, 0:2048]
        sc_e = [regB, regC]
        wsl = [pbf(512 + i * 512, 1024) for i in range(3)] + \
              [pbf(8960 + i * 512, 1024) for i in range(3)]
        WEQ = ("sync", "gpsimd", "scalar")
        stWE = [k.stream(f"stWE{i}", 2) for i in range(3)]
        be_sb = pbf(2048, 1024)
        gall = pbf(2560, 10240)
        stBE = k.stream("stBE", 1)
        tbe = stBE("gpsimd", lambda e: e.dma_start(out=be_sb[0:NEX, :], in_=bev))
        selg = pbf(7680, 2560)
        vsel = k.op("sync", lambda e: e.dma_start(out=selg[:], in_=selgd),
                    inc="sd", k=16)

        for cg in range(4):
            if cg == 0:
                k.wait("tensor", vsel)
            for i in range(8):
                k.wait("tensor", pb_tok[i])
            for e_ in range(NEX):
                bank = e_ % 2
                if e_ >= 2:
                    k.wait("tensor", pb_tok[bank])
                tb_ = k.op("tensor", lambda e, e_=e_, cg=cg, bank=bank: e.matmul(
                    pb[bank][:, 0:TG], selg[0:NEX, e_ * 128:(e_ + 1) * 128],
                    gbf[0:NEX, cg * TG:(cg + 1) * TG],
                    start=True, stop=True), inc="st")
                k.wait("vector", tb_)
                pb_tok[bank] = k.op("vector", lambda e, e_=e_, bank=bank:
                                    e.tensor_copy(
                    gall[:, e_ * TG:(e_ + 1) * TG], pb[bank][:, 0:TG]), inc="sv")
                k.op("vector", lambda e: e.drain())
            k.wait("tensor", pb_tok[0])
            k.wait("tensor", pb_tok[1])
            sltoks = []
            sli = 0
            exp_last = []
            for e_ in range(NEX):
                scb = sc_e[e_ % 2]
                gbx = gall[:, e_ * TG:(e_ + 1) * TG]
                if e_ >= 2:
                    k.wait("vector", exp_last[e_ - 2])
                for ch in range(NCH):
                    k.op("vector", lambda e, ch=ch, cg=cg, scb=scb, gbx=gbx:
                         e.tensor_tensor(
                        scb[:, ch * TG:(ch + 1) * TG],
                        xn1_bf[:, ch * 2048 + cg * TG: ch * 2048 + (cg + 1) * TG],
                        gbx[:, 0:TG], ALU.mult))
                vsc = k.op("vector", lambda e: e.drain(), inc="sv")
                if e_ == 0:
                    k.wait("tensor", tbe)
                k.wait("tensor", vsc)
                for dout in range(NCH):
                    wbuf = wsl[sli % 6]
                    q_ = WEQ[(sli % 6) % 3]
                    if sli >= 6:
                        k.wait(q_, sltoks[sli - 6])
                    tw = stWE[(sli % 6) % 3](q_, lambda e, dout=dout, e_=e_,
                                             wbuf=wbuf: e.dma_start(
                        out=wbuf[:, 0:1024],
                        in_=WeBf[dout, e_].rearrange("(a p) m -> p a m", p=128)))
                    k.wait("tensor", tw)
                    if e_ == 0:
                        k.op("tensor", lambda e, dout=dout, cg=cg: e.matmul(
                            pb[dout][:, 0:TG],
                            be_sb[0:NEX, dout * 128:(dout + 1) * 128],
                            gbf[0:NEX, cg * TG:(cg + 1) * TG],
                            start=True, stop=False))
                    for din in range(NCH):
                        k.op("tensor", lambda e, dout=dout, din=din, scb=scb,
                             wbuf=wbuf, last=(e_ == NEX - 1 and din == NCH - 1):
                             e.matmul(
                            pb[dout][:, 0:TG],
                            wbuf[:, din * 128:(din + 1) * 128],
                            scb[:, din * TG:(din + 1) * TG],
                            start=False, stop=last))
                    sltoks.append(k.mark("tensor", "st"))
                    sli += 1
                exp_last.append(sltoks[-1])
            tcl = sltoks[-1]
            k.wait("vector", tcl)
            for dout in range(NCH):
                pb_tok[dout] = k.op("vector", lambda e, dout=dout, cg=cg:
                                    e.tensor_tensor(
                    x1c(dout, cg), x1c(dout, cg), pb[dout][:, 0:TG], ALU.add),
                    inc="sv")
                k.op("vector", lambda e: e.drain())
            k.flush(f"p5_cg{cg}")
        k.barrier()
        k.flush("p5_done")

    k.barrier()
    for ch in range(NCH):
        k.op("sync", lambda e, ch=ch: e.dma_start(
            out=y[ch * 128:(ch + 1) * 128, :], in_=x1c(ch)), inc="so", k=16)
    k.wait("sync", k.tok("so"))
    k.flush("out")
    k.close()
    return nc


def _build_mask(first_block_causal):
    q = np.arange(256)[None, :]
    j = np.arange(512)[:, None]
    rel = j - 256 - q
    valid = (rel <= 0) & (rel > -WIN)
    if first_block_causal:
        valid = valid & (j >= 256)
    m = valid.astype(ml_dtypes.bfloat16)
    out = np.zeros((128, 4 * 256), dtype=ml_dtypes.bfloat16)
    for kc in range(4):
        out[:, kc * 256:(kc + 1) * 256] = m[kc * 128:(kc + 1) * 128, :]
    return out


def _build_selg():
    s = np.zeros((128, NEX * 128), dtype=ml_dtypes.bfloat16)
    for e in range(NEX):
        s[e, e * 128:(e + 1) * 128] = 1.0
    return s


def make_inmaps(inputs):
    x = np.asarray(inputs["L0_attn_x"])
    T = lambda a: np.ascontiguousarray(np.asarray(a).T)
    C = np.ascontiguousarray
    PC = lambda a: C(np.asarray(a).reshape(NCH, 128).T)
    decay = (0.5 ** (1.0 / np.linspace(2.0, 8.0, D, dtype=np.float32))).astype(np.float32)
    We = np.asarray(inputs["L1_ffn_We"])
    wdT0 = np.asarray(inputs["L0_ffn_w_down"]).T
    base = {
        "qkvT": T(inputs["L0_attn_qkv_w"]),
        "outwT": T(inputs["L0_attn_out_w"]),
        "ln2w": PC(inputs["L0_attn_norm_w"]),
        "ln2b": PC(inputs["L0_attn_norm_b"]),
        "wgT": T(inputs["L0_ffn_w_gate"]),
        "wuT": T(inputs["L0_ffn_w_up"]),
        "wdT": C(wdT0.reshape(32, 128, D).transpose(1, 0, 2).reshape(128, 32 * D)),
        "Wt": C(np.asarray(inputs["L1_attn_Wt"])),
        "Wo": C(np.asarray(inputs["L1_attn_Wo"])),
        "Wg": C(np.asarray(inputs["L1_attn_Wg"])),
        "bgv": PC(inputs["L1_attn_bg"]),
        "ln1w": PC(inputs["L1_attn_ln1_w"]),
        "ln1b": PC(inputs["L1_attn_ln1_b"]),
        "lncw": PC(inputs["L1_attn_lnc_w"]),
        "lncb": PC(inputs["L1_attn_lnc_b"]),
        "lngw": PC(inputs["L1_attn_lng_w"]),
        "lngb": PC(inputs["L1_attn_lng_b"]),
        "wg1": C(np.asarray(inputs["L1_ffn_wg1"])),
        "bg1": C(np.asarray(inputs["L1_ffn_bg1"]).reshape(2, 128).T),
        "wg2": C(np.asarray(inputs["L1_ffn_wg2"])),
        "bg2": C(np.asarray(inputs["L1_ffn_bg2"]).reshape(40, 1)),
        "We": C(We.reshape(NEX, D, NCH, 128).transpose(2, 0, 1, 3)),
        "bev": C(np.asarray(inputs["L1_ffn_be"])),
        "ident": np.eye(128, dtype=np.float32),
        "decayc": C(decay.reshape(NCH, 128).T),
        "maskB": _build_mask(False),
        "selg": _build_selg(),
    }
    mA_first = _build_mask(True)
    in_maps = []
    for c in range(CORES):
        p, hh = c // 2, c % 2
        xp = np.asarray(x[p])
        if hh == 0:
            halo = np.zeros((HALO, D), np.float32)
            own = xp[0:TLOC]
        else:
            halo = xp[TLOC - HALO:TLOC]
            own = xp[TLOC:2 * TLOC]
        oh = np.zeros((1, CORES), np.float32)
        if hh == 0:
            oh[0, c + 1] = 1.0
        m = dict(base)
        m["xT"] = C(np.concatenate([halo, own], axis=0).T)
        m["oh2"] = oh
        m["maskA"] = mA_first if hh == 0 else base["maskB"]
        in_maps.append(m)
    return in_maps


_NC_CACHE = {}


def kernel(**inputs):
    if PARTS not in _NC_CACHE:
        _NC_CACHE[PARTS] = build_program()
    nc = _NC_CACHE[PARTS]
    in_maps = make_inmaps(inputs)
    res = run_bass_kernel_spmd(nc, in_maps, core_ids=list(range(CORES)))
    out = np.empty((4, 4096, 1024), np.float32)
    for c in range(CORES):
        p, hh = c // 2, c % 2
        out[p, hh * TLOC:(hh + 1) * TLOC, :] = np.asarray(res.results[c]["y"]).T
    return out


# version 12
# speedup vs baseline: 1.0043x; 1.0043x over previous
import numpy as np
import ml_dtypes
import concourse.bass as bass
import concourse.mybir as mybir
from concourse.bass_utils import run_bass_kernel_spmd

F32 = mybir.dt.float32
F32R = mybir.dt.float32r
BF16 = mybir.dt.bfloat16
ALU = mybir.AluOpType
ACTF = mybir.ActivationFunctionType
AX = mybir.AxisListType

D = 1024
NCH = 8
TLOC = 2048
HALO = 256
TTOT = TLOC + HALO
NG = 4
TG = 512
NG3 = 8
TG3 = 256
H, DH, WIN = 16, 64, 256
INNER = 4096
NEX = 20
MAGIC = 12582912.0
CORES = 8
ENGS = ("sync", "vector", "scalar", "tensor", "gpsimd")

PARTS = ("l0a", "l0f", "l1a", "l1m")
DEBUG = False


class K:
    def __init__(self, nc):
        self.nc = nc
        self.q = {e: [] for e in ENGS}
        self.cnt = {}
        self.sems = {}
        self._exits = []
        self.blocks = 0
        self.nbar = 0
        self._pace = {}
        self._bound = {}
        self._lastw = {}
        self._hist = {}

    def sem(self, name):
        cm = self.nc.semaphore(name)
        self.sems[name] = cm.__enter__()
        self._exits.append(cm)
        self.cnt[name] = 0

    def sbuf(self, name, shape, dtype=F32):
        cm = self.nc.sbuf_tensor(name, shape, dtype)
        t = cm.__enter__()
        self._exits.append(cm)
        return t

    def psum(self, name, shape, dtype=F32):
        cm = self.nc.psum_tensor(name, shape, dtype)
        t = cm.__enter__()
        self._exits.append(cm)
        return t

    def op(self, eng, fn, inc=None, k=1):
        tok = None
        if inc is not None:
            self.cnt[inc] += k
            tok = (inc, self.cnt[inc])
            self.q[eng].append(("op", fn, inc, k))
        else:
            self.q[eng].append(("op", fn, None, 0))
        return tok

    def stream(self, name, n=2):
        sems = []
        for i in range(n):
            nm = f"{name}{i}"
            self.sem(nm)
            sems.append(nm)
        st = {"i": 0, "sems": sems}

        def issue(eng, fn):
            sm = st["sems"][st["i"] % n]
            st["i"] += 1
            return self.op(eng, fn, inc=sm, k=16)
        return issue

    def mark(self, eng, inc, k=1):
        it = self.q[eng][-1]
        assert it[0] == "op" and it[2] is None, it
        self.cnt[inc] += k
        self.q[eng][-1] = ("op", it[1], inc, k)
        return (inc, self.cnt[inc])

    def wait(self, eng, tok):
        if tok is not None:
            self.q[eng].append(("wait", tok))

    def waits(self, eng, toks):
        for t in toks:
            self.wait(eng, t)

    def tok(self, name):
        return (name, self.cnt[name])

    def barrier(self):
        self.nbar += 1
        target = 5 * self.nbar
        self.q["sync"].append(("wait", self.tok("sd")))
        self.q["sync"].append(("wait", self.tok("so")))
        self.q["gpsimd"].append(("wait", self.tok("sg")))
        for e in ENGS:
            self.cnt["bar"] += 1
            self.q[e].append(("op", lambda eng: eng.nop(), "bar", 1))
        for e in ENGS:
            self.q[e].append(("wait", ("bar", target)))

    def flush(self, name=None):
        self.blocks += 1
        name = name or f"b{self.blocks}"
        qs = {e: self.q[e] for e in ENGS if self.q[e]}
        self.q = {e: [] for e in ENGS}
        with self.nc.named_scope(name), self.nc.Block(name) as b:
            for e, items in qs.items():
                def body(eng, items=items):
                    for it in items:
                        if it[0] == "wait":
                            _, (sn, sv) = it
                            eng.wait_ge(self.sems[sn], sv)
                        else:
                            _, fn, inc, kk = it
                            ins = fn(eng)
                            if inc is not None:
                                ins.then_inc(self.sems[inc], kk)
                getattr(b, e)(body)

    def close(self):
        for cm in reversed(self._exits):
            cm.__exit__(None, None, None)


def build_program():
    nc = bass.Bass("TRN2", target_bir_lowering=False)
    k = K(nc)
    for s in ("sd", "sv", "ss", "st", "sg", "sgc", "so", "cc", "bar"):
        k.sem(s)

    def din(name, shape, dtype=F32):
        return nc.dram_tensor(name, shape, dtype, kind="ExternalInput").ap()

    xT = din("xT", [D, TTOT])
    qkvT = din("qkvT", [D, 3 * D])
    outwT = din("outwT", [D, D])
    ln2w = din("ln2w", [128, NCH]); ln2b = din("ln2b", [128, NCH])
    wgT = din("wgT", [D, INNER]); wuT = din("wuT", [D, INNER])
    wdT = din("wdT", [128, 32 * D])
    Wt = din("Wt", [D, D]); Wo = din("Wo", [D, D]); Wg = din("Wg", [2 * D, D])
    bgv = din("bgv", [128, NCH])
    ln1w = din("ln1w", [128, NCH]); ln1b = din("ln1b", [128, NCH])
    lncw = din("lncw", [128, NCH]); lncb = din("lncb", [128, NCH])
    lngw = din("lngw", [128, NCH]); lngb = din("lngb", [128, NCH])
    wg1 = din("wg1", [D, 256]); bg1 = din("bg1", [128, 2])
    wg2 = din("wg2", [256, 40]); bg2 = din("bg2", [40, 1])
    We = din("We", [NCH, NEX, D, 128])
    bev = din("bev", [NEX, D])
    maskA = din("maskA", [128, 4 * 256], BF16)
    maskB = din("maskB", [128, 4 * 256], BF16)
    selgd = din("selg", [128, NEX * 128], BF16)
    ident = din("ident", [128, 128])
    decayc = din("decayc", [128, NCH])
    oh2 = din("oh2", [1, CORES])
    y = nc.dram_tensor("y", [D, TLOC], F32, kind="ExternalOutput").ap()
    ternG = nc.dram_tensor("ternG", [128, NCH * INNER], BF16).ap()
    ternU = nc.dram_tensor("ternU", [128, NCH * INNER], BF16).ap()
    ternD = nc.dram_tensor("ternD", [128, 32 * D], BF16).ap()
    WeBf = nc.dram_tensor("WeBf", [NCH, NEX, D, 128], BF16).ap()
    cc_in = nc.dram_tensor("cc_in", [CORES * 128, NCH], F32).ap()
    cc_out = nc.dram_tensor("cc_out", [128, NCH], F32).ap()
    rows_d = nc.dram_tensor("rows_d", [5, TLOC], F32).ap()
    dbg = nc.dram_tensor("dbg", [128, 16384], F32).ap()

    x1 = k.sbuf("x1", [128, NCH * TLOC])
    POOLN = 12352
    pool = k.sbuf("pool", [128, POOLN])
    regA = k.sbuf("regA", [128, 16384], BF16)
    regB = k.sbuf("regB", [128, 8192], BF16)
    regC = k.sbuf("regC", [128, 8192], BF16)
    rowsb = k.sbuf("rowsb", [128, TLOC])
    rowstage = k.sbuf("rowstage", [1, TG])
    qscale = k.sbuf("qscale", [128, 24])
    oscale = k.sbuf("oscale", [128, 8])
    pvec = k.sbuf("pvec", [128, 26])
    dvec = k.sbuf("dvec", [128, NCH])
    betav = k.sbuf("betav", [128, NCH])
    mu_b = k.sbuf("mu_b", [128, TG])
    r_b = k.sbuf("r_b", [128, TG])
    mu2_b = k.sbuf("mu2_b", [128, TG])
    r2_b = k.sbuf("r2_b", [128, TG])
    sbf2 = k.sbuf("sbf2", [128, 2 * TG], BF16)
    probs_sb = k.sbuf("probs_sb", [128, 4 * 256], BF16)
    ident_sb = k.sbuf("ident_sb", [128, 128])
    maskA_sb = k.sbuf("maskA_sb", [128, 4 * 256], BF16)
    maskB_sb = k.sbuf("maskB_sb", [128, 4 * 256], BF16)
    ones_bf = k.sbuf("ones_bf", [128, 128], BF16)
    ones1f = k.sbuf("ones1f", [1, 128])
    onesf = k.sbuf("onesf", [128, 128])
    carry_sb = k.sbuf("carry_sb", [128, NCH])
    ccst = k.sbuf("ccst", [128, CORES * NCH])
    ccres = k.sbuf("ccres", [128, NCH])
    oh2_sb = k.sbuf("oh2_sb", [128, CORES])
    bg1_sb = k.sbuf("bg1_sb", [128, 2])
    bg2_sb = k.sbuf("bg2_sb", [40, 1])
    gtm = k.sbuf("gtm", [128, 128])
    gsc = k.sbuf("gsc", [128, 8])
    epsv = k.sbuf("epsv", [128, 2])
    lpar = k.sbuf("lpar", [128, 56])

    R_MU0, R_R0, R_AQM = 0, 32, 64
    D_MUX, D_RX, D_MU1, D_R1, D_AQM = range(5)

    def rowg(i, g, t=TG, w=None):
        return rowsb[i:i + 1, g * t: g * t + (w or t)]

    def x1c(ch, g=None, t=TG, off=0, w=None):
        if g is None:
            return x1[:, ch * TLOC:(ch + 1) * TLOC]
        a = ch * TLOC + g * t + off
        return x1[:, a: a + (w or t)]

    PF = pool
    PB = pool.bitcast(BF16)

    def pf(off, cols):
        return PF[:, off:off + cols]

    def pbf(off, cols):
        return PB[:, 2 * off:2 * off + cols]

    pb = [k.psum(f"pb{i}", [128, 512]) for i in range(8)]
    pb_tok = [None] * 8

    def mm_bank(i, toks=()):
        k.wait("tensor", pb_tok[i])
        for t in toks:
            k.wait("tensor", t)

    def ln_stats(src_fn, T, eps, mu, r, rows_to=None):
        epsap = epsv[:, 0:1] if eps == 1e-5 else epsv[:, 1:2]
        """Feature-major LN stats; src_fn(ch)->AP [128,T] (fp32 or bf16)."""
        for ch in range(NCH):
            k.op("vector", lambda e, ch=ch: e.tensor_copy(sbf2[:, 0:T], src_fn(ch)))
            k.op("vector", lambda e: e.drain())
            k.op("vector", lambda e: e.tensor_tensor(
                sbf2[:, T:2 * T], sbf2[:, 0:T], sbf2[:, 0:T], ALU.mult))
            tv = k.op("vector", lambda e: e.drain(), inc="sv")
            if ch == 0:
                k.wait("tensor", pb_tok[0])
                k.wait("tensor", pb_tok[1])
            k.wait("tensor", tv)
            k.op("tensor", lambda e, ch=ch: e.matmul(
                pb[0][:, 0:T], ones_bf[:], sbf2[:, 0:T],
                start=(ch == 0), stop=(ch == NCH - 1)))
            k.op("tensor", lambda e, ch=ch: e.matmul(
                pb[1][:, 0:T], ones_bf[:], sbf2[:, T:2 * T],
                start=(ch == 0), stop=(ch == NCH - 1)))
            tt = k.mark("tensor", "st")
            k.wait("vector", tt)
        k.op("vector", lambda e: e.tensor_scalar(
            mu[:, 0:T], pb[0][:, 0:T], 1.0 / D, None, ALU.mult, ALU.bypass))
        k.op("vector", lambda e: e.drain())
        k.op("vector", lambda e: e.tensor_scalar(
            r[:, 0:T], pb[1][:, 0:T], 1.0 / D, None, ALU.mult, ALU.bypass))
        k.op("vector", lambda e: e.drain())
        sf = sbf2.bitcast(F32)
        k.op("vector", lambda e, sf=sf: e.tensor_tensor(
            sf[:, 0:T], mu[:, 0:T], mu[:, 0:T], ALU.mult))
        k.op("vector", lambda e: e.drain())
        k.op("vector", lambda e, sf=sf: e.tensor_tensor(
            r[:, 0:T], r[:, 0:T], sf[:, 0:T], ALU.subtract))
        vv = k.op("vector", lambda e: e.drain(), inc="sv")
        pb_tok[0] = vv
        pb_tok[1] = vv
        k.wait("scalar", vv)
        sv = k.op("scalar", lambda e, epsap=epsap: e.activation(
            r[:, 0:T], r[:, 0:T], ACTF.Sqrt, bias=epsap, scale=1.0), inc="ss")
        k.wait("vector", sv)
        k.op("vector", lambda e: e.reciprocal(r[:, 0:T], r[:, 0:T]))
        vtok = k.op("vector", lambda e: e.drain(), inc="sv")
        if rows_to is not None:
            mu_row, r_row = rows_to
            ip_mu = mu_row.base_partition() if callable(mu_row.base_partition) else mu_row.base_partition
            ip_r = r_row.base_partition() if callable(r_row.base_partition) else r_row.base_partition
            k.op("vector", lambda e, mu_row=mu_row, ip=ip_mu: e.tensor_copy(
                mu_row, mu[ip:ip + 1, 0:T]))
            k.op("vector", lambda e, r_row=r_row, ip=ip_r: e.tensor_copy(
                r_row, r[ip:ip + 1, 0:T]))
            vtok = k.op("vector", lambda e: e.drain(), inc="sv")
        return vtok

    stRS = k.stream("stRS", 2)
    stDN = k.stream("stDN", 1)

    def bcast_drows(specs, T, toks=()):
        first = True
        for rid, c0, dst in specs:
            k.wait("sync", k.tok("st"))
            td = stRS("sync", lambda e, rid=rid, c0=c0, T=T: e.dma_start(
                out=rowstage[0:1, 0:T], in_=rows_d[rid, c0:c0 + T]))
            mm_bank(0, (toks if first else ()) + (td,) if False else (td,))
            if first:
                for t_ in toks:
                    k.wait("tensor", t_)
            first = False
            tm = k.op("tensor", lambda e, T=T: e.matmul(
                pb[0][:, 0:T], ones1f[0:1, :], rowstage[0:1, 0:T],
                start=True, stop=True), inc="st")
            k.wait("vector", tm)
            pb_tok[0] = k.op("vector", lambda e, dst=dst, T=T: e.tensor_copy(
                dst[:, 0:T], pb[0][:, 0:T]), inc="sv")
            k.op("vector", lambda e: e.drain())

    def bcast_rows(specs, T, toks=()):
        first = True
        for rowap, dst in specs:
            mm_bank(0, toks if first else ())
            first = False
            bp = rowap.base_partition
            bp = bp() if callable(bp) else bp
            tm = k.op("tensor", lambda e, rowap=rowap, T=T, bp=bp: e.matmul(
                pb[0][:, 0:T], onesf[bp:bp + 1, :], rowap, start=True, stop=True),
                inc="st")
            k.wait("vector", tm)
            pb_tok[0] = k.op("vector", lambda e, dst=dst, T=T: e.tensor_copy(
                dst[:, 0:T], pb[0][:, 0:T]), inc="sv")
            k.op("vector", lambda e: e.drain())

    for ch in range(NCH):
        k.op("sync", lambda e, ch=ch: e.dma_start(
            out=x1c(ch), in_=xT[ch * 128:(ch + 1) * 128, HALO:TTOT]), inc="sd", k=16)
    k.op("sync", lambda e: e.dma_start(out=ident_sb[:], in_=ident), inc="sd", k=16)
    k.op("sync", lambda e: e.dma_start(out=maskA_sb[:], in_=maskA), inc="sd", k=16)
    k.op("sync", lambda e: e.dma_start(out=maskB_sb[:], in_=maskB), inc="sd", k=16)
    k.op("sync", lambda e: e.dma_start(out=pvec[:, 0:NCH], in_=ln2w), inc="sd", k=16)
    k.op("sync", lambda e: e.dma_start(out=pvec[:, 8:16], in_=ln2b), inc="sd", k=16)
    k.op("sync", lambda e: e.dma_start(out=dvec[:], in_=decayc), inc="sd", k=16)
    k.op("sync", lambda e: e.dma_start(out=bg1_sb[:], in_=bg1), inc="sd", k=16)
    k.op("sync", lambda e: e.dma_start(out=bg2_sb[:], in_=bg2), inc="sd", k=16)
    k.op("sync", lambda e: e.dma_start(out=oh2_sb[0:1, :], in_=oh2), inc="sd", k=16)
    for i_, ap_ in enumerate([ln1w, ln1b, lncw, lncb, lngw, lngb, bgv]):
        k.op("sync", lambda e, i_=i_, ap_=ap_: e.dma_start(
            out=lpar[:, i_ * 8:(i_ + 1) * 8], in_=ap_), inc="sd", k=16)
    k.op("vector", lambda e: e.memset(ones_bf[:], 1.0))
    k.op("vector", lambda e: e.memset(ones1f[:], 1.0))
    k.op("vector", lambda e: e.memset(onesf[:], 1.0))
    k.op("vector", lambda e: e.memset(epsv[:, 0:1], 1e-5))
    k.op("vector", lambda e: e.memset(epsv[:, 1:2], 1e-6))
    k.op("vector", lambda e: e.drain(), inc="sv")
    k.barrier()
    k.op("vector", lambda e: e.tensor_scalar(betav[:], dvec[:], -1.0, 1.0,
                                             ALU.mult, ALU.add))
    k.op("vector", lambda e: e.drain(), inc="sv")
    k.flush("p0_loads")

    qw = pf(0, 4096)
    qaf = pf(4096, 4096)
    qa = qaf
    qsb = mu_b
    qrb = r_b

    def quant_rowscaled(wT_ap, dout, dst_fn, sc_out):
        for jg in range(dout // TG):
            k.wait("sync", k.tok("sv"))
            toks = [k.op("sync", lambda e, ch=ch, jg=jg: e.dma_start(
                out=qw[:, ch * TG:(ch + 1) * TG],
                in_=wT_ap[ch * 128:(ch + 1) * 128, jg * TG:(jg + 1) * TG]),
                inc="sd", k=16) for ch in range(NCH)]
            k.wait("vector", toks[-1])
            for ch in range(NCH):
                k.op("vector", lambda e, ch=ch: e.scalar_tensor_tensor(
                    qa[:, ch * TG:(ch + 1) * TG], qw[:, ch * TG:(ch + 1) * TG],
                    -1.0, qw[:, ch * TG:(ch + 1) * TG], ALU.mult, ALU.max))
            va = k.op("vector", lambda e: e.drain(), inc="sv")
            mm_bank(2, (va,))
            for ch in range(NCH):
                k.op("tensor", lambda e, ch=ch: e.matmul(
                    pb[2][:, 0:TG], onesf[:], qa[:, ch * TG:(ch + 1) * TG],
                    start=(ch == 0), stop=(ch == NCH - 1)))
            tmm = k.mark("tensor", "st")
            k.wait("vector", tmm)
            k.op("vector", lambda e: e.tensor_scalar(
                qsb[:], pb[2][:, 0:TG], 1.0 / D, 1e-5, ALU.mult, ALU.max))
            vq = k.op("vector", lambda e: e.drain(), inc="sv")
            pb_tok[2] = vq
            k.op("vector", lambda e: e.reciprocal(qrb[:], qsb[:]))
            k.op("vector", lambda e: e.drain())
            for q in range(TG // 128):
                j = jg * (TG // 128) + q
                vq2 = k.op("vector", lambda e: e.drain(), inc="sv")
                mm_bank(3, (vq2,))
                tq = k.op("tensor", lambda e, q=q: e.transpose(
                    pb[3][:, 0:128], qsb[:, q * 128:(q + 1) * 128], ident_sb[:]),
                    inc="st")
                k.wait("vector", tq)
                pb_tok[3] = k.op("vector", lambda e, j=j: e.tensor_copy(
                    sc_out[:, j:j + 1], pb[3][:, 0:1]), inc="sv")
                k.op("vector", lambda e: e.drain())
            for ch in range(NCH):
                tmp = qaf[:, ch * TG:(ch + 1) * TG]
                neg = qw[:, ch * TG:(ch + 1) * TG]
                k.op("vector", lambda e, ch=ch, tmp=tmp, neg=neg: e.tensor_tensor(
                    tmp, neg, qrb[:], ALU.mult))
                k.op("vector", lambda e: e.drain())
                k.op("vector", lambda e, tmp=tmp, neg=neg: e.tensor_scalar(
                    neg, tmp, -0.5, None, ALU.is_lt, ALU.bypass))
                k.op("vector", lambda e: e.drain())
                for q in range(TG // 128):
                    j = jg * (TG // 128) + q
                    k.op("vector", lambda e, ch=ch, tmp=tmp, neg=neg, q=q, j=j:
                         e.scalar_tensor_tensor(
                        dst_fn(ch, j), tmp[:, q * 128:(q + 1) * 128], 0.5,
                        neg[:, q * 128:(q + 1) * 128], ALU.is_gt, ALU.subtract))
                k.op("vector", lambda e: e.drain())
            k.op("vector", lambda e: e.drain(), inc="sv")
            k.flush(f"q_{wT_ap.tensor.name}_{jg}")

    qk_t = regA
    v_t = regB
    ow_t = regC

    def qkv_dst(ch, j):
        if j >= 16:
            return v_t[:, ch * 1024 + (j - 16) * 128: ch * 1024 + (j - 15) * 128]
        return qk_t[:, ch * 2048 + j * 128: ch * 2048 + (j + 1) * 128]

    def ow_dst(ch, j):
        return ow_t[:, ch * 1024 + j * 128: ch * 1024 + (j + 1) * 128]

    quant_rowscaled(qkvT, 3 * D, qkv_dst, qscale)
    quant_rowscaled(outwT, D, ow_dst, oscale)
    k.op("vector", lambda e: e.tensor_scalar(
        qscale[:, 0:8], qscale[:, 0:8], DH ** -0.5, None, ALU.mult, ALU.bypass))
    k.op("vector", lambda e: e.drain(), inc="sv")
    k.barrier()
    k.flush("p0_done")

    halo_x = pf(0, 2048)
    q_bf = pbf(0, 4096)
    xn_g = pf(2048, 4096)
    h_bf = pbf(2048, 4096)
    araw = pf(4096, 2048)
    kroll = pbf(6144, 6144)
    vroll = pbf(9216, 6240)
    VW = H * 65
    aqm_b = r_b
    probs_b = None
    prev_tb = {}
    prev_tpv = {}

    for ch in range(NCH):
        k.op("sync", lambda e, ch=ch: e.dma_start(
            out=halo_x[:, ch * HALO:(ch + 1) * HALO],
            in_=xT[ch * 128:(ch + 1) * 128, 0:HALO]), inc="sd", k=16)
    t_halo = k.tok("sd")
    for u in range(6):
        for hh in range(H):
            k.op("vector", lambda e, u=u, hh=hh: e.memset(
                vroll[:, u * VW + hh * 65 + 64: u * VW + hh * 65 + 65], 1.0))
    k.op("vector", lambda e: e.drain(), inc="sv")
    k.flush("p1_pre")

    def emit_h(xsrc, T, g, store_rows):
        ln_stats(xsrc, T, 1e-5, mu_b, r_b,
                 rows_to=(rowg(R_MU0, g, w=T), rowg(R_R0, g, w=T)) if store_rows else None)
        vt = k.tok("sv")
        k.wait("vector", vt)
        for ch in range(NCH):
            k.op("vector", lambda e, ch=ch, T=T: e.tensor_tensor(
                xn_g[:, ch * TG: ch * TG + T], xsrc(ch), mu_b[:, 0:T], ALU.subtract))
            k.op("vector", lambda e: e.drain())
            k.op("vector", lambda e, ch=ch, T=T: e.tensor_tensor(
                xn_g[:, ch * TG: ch * TG + T], xn_g[:, ch * TG: ch * TG + T],
                r_b[:, 0:T], ALU.mult))
            k.op("vector", lambda e: e.drain())
        k.op("vector", lambda e: e.drain(), inc="sv")

        def xnsrc(ch, T=T):
            return xn_g[:, ch * TG: ch * TG + T]

        ln_stats(xnsrc, T, 1e-5, mu2_b, r2_b)
        sf = sbf2.bitcast(F32)
        if store_rows:
            vt2 = k.tok("sv")
            k.wait("vector", vt2)
            k.op("vector", lambda e, T=T: e.scalar_tensor_tensor(
                aqm_b[:, 0:T], xnsrc(0), -1.0, xnsrc(0), ALU.mult, ALU.max))
            k.op("vector", lambda e: e.drain())
            for ch in range(1, NCH):
                k.op("vector", lambda e, ch=ch, T=T, sf=sf: e.scalar_tensor_tensor(
                    sf[:, 0:T], xnsrc(ch), -1.0, xnsrc(ch), ALU.mult, ALU.max))
                k.op("vector", lambda e: e.drain())
                k.op("vector", lambda e, T=T, sf=sf: e.tensor_tensor(
                    aqm_b[:, 0:T], aqm_b[:, 0:T], sf[:, 0:T], ALU.max))
                k.op("vector", lambda e: e.drain())
            va = k.op("vector", lambda e: e.drain(), inc="sv")
            for c in range(T // 128):
                mm_bank(2, (va,) if c == 0 else ())
                t1 = k.op("tensor", lambda e, c=c: e.transpose(
                    pb[2][:, 0:128], aqm_b[:, c * 128:(c + 1) * 128],
                    ident_sb[:]), inc="st")
                k.wait("vector", t1)
                k.op("vector", lambda e: e.reduce_max(
                    gsc[:, 0:1], pb[2][:, 0:128], AX.X))
                k.op("vector", lambda e: e.drain())
                k.op("vector", lambda e: e.tensor_scalar(
                    gsc[:, 1:2], gsc[:, 0:1], 1e-5, None, ALU.max, ALU.bypass))
                vcl = k.op("vector", lambda e: e.drain(), inc="sv")
                pb_tok[2] = vcl
                mm_bank(2)
                t2 = k.op("tensor", lambda e: e.transpose(
                    pb[2][0:1, 0:128], gsc[:, 1:2], ident_sb[:]), inc="st")
                k.wait("vector", t2)
                if c == 0:
                    k.wait("vector", k.tok("so"))
                pb_tok[2] = k.op("vector", lambda e, c=c: e.tensor_copy(
                    rowstage[0:1, c * 128:(c + 1) * 128],
                    pb[2][0:1, 0:128]), inc="sv")
                k.op("vector", lambda e: e.drain())
            vaqr = k.op("vector", lambda e: e.drain(), inc="sv")
            k.wait("sync", vaqr)
            k.op("sync", lambda e, g=g, T=T: e.dma_start(
                out=rows_d[D_AQM, g * TG: g * TG + T],
                in_=rowstage[0:1, 0:T]), inc="so", k=16)
        vt3 = k.tok("sv")
        for ch in range(NCH):
            if ch == 0:
                k.wait("vector", vt3)
            k.op("vector", lambda e, ch=ch, T=T, sf=sf: e.tensor_tensor(
                sf[:, 0:T], xnsrc(ch), mu2_b[:, 0:T], ALU.subtract))
            k.op("vector", lambda e: e.drain())
            k.op("vector", lambda e, T=T, sf=sf: e.tensor_tensor(
                sf[:, 0:T], sf[:, 0:T], r2_b[:, 0:T], ALU.mult))
            vch = k.op("vector", lambda e: e.drain(), inc="sv")
            k.wait("scalar", vch)
            sch = k.op("scalar", lambda e, ch=ch, T=T, sf=sf: e.activation(
                h_bf[:, ch * TG: ch * TG + T], sf[:, 0:T],
                ACTF.Identity, bias=pvec[:, 8 + ch:9 + ch], scale=pvec[:, ch:ch + 1]),
                inc="ss")
            k.wait("vector", sch)
        return k.tok("ss")

    def emit_kv(T, slot0):
        for m in range(8, 16):
            bank = 4 + (m % 2)
            mm_bank(bank)
            for kc in range(NCH):
                k.op("tensor", lambda e, m=m, kc=kc, bank=bank, T=T: e.matmul(
                    pb[bank][:, 0:T],
                    qk_t[:, kc * 2048 + m * 128: kc * 2048 + (m + 1) * 128],
                    h_bf[:, kc * TG: kc * TG + T],
                    start=(kc == 0), stop=(kc == NCH - 1)))
            tmm = k.mark("tensor", "st")
            k.wait("scalar", tmm)
            for sb in range(T // 256):
                slot = (slot0 + sb) % 3
                pb_tok[bank] = k.op(
                    "scalar", lambda e, m=m, sb=sb, slot=slot, bank=bank: e.activation(
                        kroll[:, (m - 8) * 768 + slot * 256:
                              (m - 8) * 768 + slot * 256 + 256],
                        pb[bank][:, sb * 256:(sb + 1) * 256],
                        ACTF.Identity, bias=0.0, scale=qscale[:, m:m + 1]), inc="ss")
        cnt = 0
        for mt in range(T // 128):
            for nn in range(2):
                bank = 4 + (cnt % 2)
                cnt += 1
                mm_bank(bank)
                for kc in range(NCH):
                    k.op("tensor", lambda e, mt=mt, nn=nn, kc=kc, bank=bank, T=T: e.matmul(
                        pb[bank][:, 0:512],
                        h_bf[:, kc * TG + mt * 128: kc * TG + (mt + 1) * 128],
                        v_t[:, kc * 1024 + nn * 512: kc * 1024 + (nn + 1) * 512],
                        start=(kc == 0), stop=(kc == NCH - 1)))
                tmm = k.mark("tensor", "st")
                k.wait("vector", tmm)
                u = ((slot0 + mt // 2) % 3) * 2 + (mt % 2)
                for hh in range(nn * 8, nn * 8 + 8):
                    k.op("vector", lambda e, u=u, hh=hh, nn=nn, bank=bank: e.tensor_copy(
                        vroll[:, u * VW + hh * 65: u * VW + hh * 65 + 64],
                        pb[bank][:, (hh - nn * 8) * 64:(hh - nn * 8) * 64 + 64]))
                pb_tok[bank] = k.op("vector", lambda e: e.drain(), inc="sv")

    k.waits("vector", [t_halo])

    def halosrc(ch):
        return halo_x[:, ch * HALO:(ch + 1) * HALO]

    hs = emit_h(halosrc, HALO, 0, False)
    k.wait("tensor", hs)
    emit_kv(HALO, 2)
    k.flush("p1_halo")

    for g in range(NG):
        def xsrc(ch, g=g):
            return x1c(ch, g)

        k.wait("scalar", k.tok("sv"))
        hs = emit_h(xsrc, TG, g, True)
        k.wait("tensor", hs)
        emit_kv(TG, (2 * g) % 3)
        for m in range(8):
            bank = 4 + (m % 2)
            mm_bank(bank)
            for kc in range(NCH):
                k.op("tensor", lambda e, m=m, kc=kc, bank=bank: e.matmul(
                    pb[bank][:, 0:TG],
                    qk_t[:, kc * 2048 + m * 128: kc * 2048 + (m + 1) * 128],
                    h_bf[:, kc * TG:(kc + 1) * TG],
                    start=(kc == 0), stop=(kc == NCH - 1)))
            tmm = k.mark("tensor", "st")
            k.wait("scalar", tmm)
            pb_tok[bank] = k.op("scalar", lambda e, m=m, bank=bank: e.activation(
                q_bf[:, m * TG:(m + 1) * TG], pb[bank][:, 0:TG],
                ACTF.Identity, bias=0.0, scale=qscale[:, m:m + 1]), inc="ss")
        k.flush(f"p1_qkv_g{g}")

        for blk in range(2):
            B = 2 * g + blk
            cslot = B % 3
            pslot = (B - 1) % 3
            mask_sb = maskA_sb if B == 0 else maskB_sb
            tqkv = k.tok("ss")
            probs_b = [probs_sb, sbf2]

            def den_slot(h):
                if h < 8:
                    return rowsb[64:65, h * 256:(h + 1) * 256]
                tsr = [mu_b, r_b, mu2_b, r2_b][(h - 8) // 2]
                c0 = ((h - 8) % 2) * 256
                return tsr[64:65, c0:c0 + 256]

            tsc = [None] * H; se = [None] * H; vp = [None] * H
            tpv = [None] * H; sevac = [None] * H; tb = [None] * H
            vstt = [None] * H
            for i in range(H + 1):
                if i < H:
                    h = i
                    kch, poff = h // 2, (h % 2) * 64
                    b4 = 4 if h % 2 == 0 else 2
                    if h < 2:
                        k.wait("tensor", tqkv)
                        k.wait("tensor", pb_tok[b4])
                        k.wait("tensor", pb_tok[b4 + 1])
                    else:
                        k.wait("tensor", se[h - 2])
                    for kc in range(4):
                        slot = pslot if kc < 2 else cslot
                        sub = kc % 2
                        bank = b4 + kc // 2
                        k.op("tensor", lambda e, kch=kch, poff=poff, slot=slot,
                             sub=sub, bank=bank, kc=kc, blk=blk: e.matmul(
                            pb[bank][:, (kc % 2) * 256:(kc % 2) * 256 + 256],
                            kroll[poff:poff + 64,
                                  kch * 768 + slot * 256 + sub * 128:
                                  kch * 768 + slot * 256 + sub * 128 + 128],
                            q_bf[poff:poff + 64,
                                 kch * TG + blk * 256: kch * TG + blk * 256 + 256],
                            start=True, stop=True))
                    tsc[h] = k.mark("tensor", "st")
                if 0 <= i - 1 < H:
                    h = i - 1
                    pvb = 6 if h % 2 == 0 else 0
                    k.wait("tensor", vp[h])
                    if h < 2:
                        k.wait("tensor", pb_tok[pvb])
                    else:
                        k.wait("tensor", sevac[h - 2])
                    for kc in range(4):
                        slot = pslot if kc < 2 else cslot
                        u = slot * 2 + (kc % 2)
                        k.op("tensor", lambda e, u=u, h=h, kc=kc, pvb=pvb: e.matmul(
                            pb[pvb][0:65, 0:256],
                            vroll[:, u * VW + h * 65: u * VW + (h + 1) * 65],
                            probs_b[h % 2][:, kc * 256:(kc + 1) * 256],
                            start=(kc == 0), stop=(kc == 3)))
                    tpv[h] = k.mark("tensor", "st")
                if i < H:
                    h = i
                    b4 = 4 if h % 2 == 0 else 2
                    k.wait("scalar", tsc[h])
                    gd = tpv[h - 2] if h >= 2 else prev_tpv.get(h % 2)
                    k.wait("scalar", gd)
                    for kc in range(4):
                        bank = b4 + kc // 2
                        k.op("scalar", lambda e, kc=kc, bank=bank, h=h: e.activation(
                            probs_b[h % 2][:, kc * 256:(kc + 1) * 256],
                            pb[bank][:, (kc % 2) * 256:(kc % 2) * 256 + 256],
                            ACTF.Exp, bias=0.0, scale=1.0))
                    se[h] = k.op("scalar", lambda e: e.drain(), inc="ss")
                    pb_tok[b4] = se[h]
                    pb_tok[b4 + 1] = se[h]
                if 0 <= i - 1 < H:
                    h = i - 1
                    pvb = 6 if h % 2 == 0 else 0
                    kch, poff = h // 2, (h % 2) * 64
                    k.wait("scalar", tpv[h])
                    k.wait("scalar", prev_tb.get(h))
                    k.op("scalar", lambda e, kch=kch, poff=poff, pvb=pvb:
                         e.activation(
                        araw[poff:poff + 64, kch * 256:(kch + 1) * 256],
                        pb[pvb][0:64, 0:256], ACTF.Identity, bias=0.0, scale=1.0))
                    k.op("scalar", lambda e, h=h, pvb=pvb: e.activation(
                        den_slot(h), pb[pvb][64:65, 0:256], ACTF.Identity,
                        bias=0.0, scale=1.0))
                    sevac[h] = k.op("scalar", lambda e: e.drain(), inc="ss")
                    pb_tok[pvb] = sevac[h]
                if i < H:
                    h = i
                    k.wait("vector", se[h])
                    k.op("vector", lambda e, mask_sb=mask_sb, h=h: e.tensor_tensor(
                        probs_b[h % 2][:], probs_b[h % 2][:], mask_sb[:], ALU.mult))
                    vp[h] = k.op("vector", lambda e: e.drain(), inc="sv")
            k.wait("sync", sevac[H - 1])
            k.wait("sync", k.tok("st"))
            k.wait("sync", k.tok("stDN0"))
            dn1 = stDN("sync", lambda e: e.dma_start(
                out=mu2_b[0:8, 0:256], in_=rowsb[64:65, 0:2048]))
            dn2 = stDN("sync", lambda e: e.dma_start(
                out=mu2_b[8:10, 0:256], in_=mu_b[64:65, 0:512]))
            dn3 = stDN("sync", lambda e: e.dma_start(
                out=mu2_b[10:12, 0:256], in_=r_b[64:65, 0:512]))
            dn4 = stDN("sync", lambda e: e.dma_start(
                out=mu2_b[12:14, 0:256], in_=mu2_b[64:65, 0:512]))
            dn5 = stDN("sync", lambda e: e.dma_start(
                out=mu2_b[14:16, 0:256], in_=r2_b[64:65, 0:512]))
            k.wait("vector", dn5)
            k.op("vector", lambda e: e.reciprocal(
                mu2_b[0:16, 0:256], mu2_b[0:16, 0:256]))
            vrc = k.op("vector", lambda e: e.drain(), inc="sv")
            k.wait("sync", vrc)
            dn6 = stDN("sync", lambda e: e.dma_start(
                out=rowsb[64:65, 0:2048], in_=mu2_b[0:8, 0:256]))
            dn7 = stDN("sync", lambda e: e.dma_start(
                out=mu_b[64:65, 0:512], in_=mu2_b[8:10, 0:256]))
            dn8 = stDN("sync", lambda e: e.dma_start(
                out=r_b[64:65, 0:512], in_=mu2_b[10:12, 0:256]))
            dn9 = stDN("sync", lambda e: e.dma_start(
                out=mu2_b[64:65, 0:512], in_=mu2_b[12:14, 0:256]))
            dn10 = stDN("sync", lambda e: e.dma_start(
                out=r2_b[64:65, 0:512], in_=mu2_b[14:16, 0:256]))
            vrec = dn10
            for h in range(H):
                bcb = 7 if h % 2 == 0 else 1
                kch, poff = h // 2, (h % 2) * 64
                if h < 2:
                    k.wait("tensor", vrec)
                    k.wait("tensor", pb_tok[bcb])
                else:
                    k.wait("tensor", vstt[h - 2])
                tb[h] = k.op("tensor", lambda e, h=h, bcb=bcb: e.matmul(
                    pb[bcb][:, 0:256], onesf[64:65, :], den_slot(h),
                    start=True, stop=True), inc="st")
                k.wait("vector", tb[h])
                k.op("vector", lambda e, kch=kch, poff=poff, h=h, bcb=bcb:
                     e.scalar_tensor_tensor(
                    h_bf[poff:poff + 64, kch * TG: kch * TG + 256],
                    araw[poff:poff + 64, kch * 256:(kch + 1) * 256],
                    qscale[poff:poff + 64, 16 + kch:17 + kch],
                    pb[bcb][poff:poff + 64, 0:256],
                    ALU.mult, ALU.mult))
                vstt[h] = k.op("vector", lambda e: e.drain(), inc="sv")
                pb_tok[bcb] = vstt[h]
            prev_tb = {h: tb[h] for h in range(H)}
            prev_tpv = {0: tpv[H - 2], 1: tpv[H - 1]}
            k.wait("vector", vrec)
            va = k.tok("sv")
            for m in range(NCH):
                bank = 4 + (m % 2)
                mm_bank(bank, (va,))
                for kc in range(NCH):
                    k.op("tensor", lambda e, m=m, kc=kc, bank=bank: e.matmul(
                        pb[bank][:, 0:256],
                        ow_t[:, kc * 1024 + m * 128: kc * 1024 + (m + 1) * 128],
                        h_bf[:, kc * TG: kc * TG + 256],
                        start=(kc == 0), stop=(kc == NCH - 1)))
                tmm = k.mark("tensor", "st")
                k.wait("vector", tmm)
                pb_tok[bank] = k.op(
                    "vector", lambda e, m=m, g=g, blk=blk, bank=bank:
                    e.scalar_tensor_tensor(
                        x1c(m, g, off=blk * 256, w=256),
                        pb[bank][:, 0:256], oscale[:, m:m + 1],
                        x1c(m, g, off=blk * 256, w=256),
                        ALU.mult, ALU.add), inc="sv")
                k.op("vector", lambda e: e.drain())
            k.flush(f"p1_att_g{g}b{blk}")

    k.barrier()
    k.flush("p1_done")

    if "l0f" in PARTS:
        wb = [pf(0, 2048), pf(2048, 2048)]
        sb2 = [pf(4096, 2048), pf(6144, 2048)]
        ob = [pbf(8192, 2048), pbf(9216, 2048)]
        stP = [k.stream("stPa", 2), k.stream("stPb", 2)]
        P2Q = ("sync", "gpsimd")
        stQ = [k.stream("stQa", 1), k.stream("stQb", 1)]
        wb_free = [None, None]
        sb2_free = [None, None]
        ob_free = [None, None]

        def tile_src(wap, flat, t):
            if flat:
                return wap[:, t * 2048:(t + 1) * 2048]
            kc, h = t // 2, t % 2
            return wap[kc * 128:(kc + 1) * 128, h * 2048:(h + 1) * 2048]

        def tile_dst(dram, flat, t):
            if flat:
                return dram[:, t * 2048:(t + 1) * 2048]
            kc, h = t // 2, t % 2
            return dram[:, kc * INNER + h * 2048: kc * INNER + (h + 1) * 2048]

        NT = 16
        for wi, (wap, dram, flat) in enumerate([
                (wgT, ternG, False), (wuT, ternU, False), (wdT, ternD, True)]):
            r2w = r2_b[:, wi:wi + 1]
            muw = mu2_b[:, wi:wi + 1]
            mmtok = [None] * NT
            for t in range(NT):
                b = t % 2
                k.wait(P2Q[b], wb_free[b])
                tl = stP[b](P2Q[b], lambda e, wap=wap, flat=flat, t=t, b=b:
                            e.dma_start(out=wb[b][:], in_=tile_src(wap, flat, t)))
                eng = "scalar" if t % 2 == 0 else "vector"
                k.wait(eng, tl)
                k.wait(eng, sb2_free[b])
                if eng == "scalar":
                    k.op("scalar", lambda e, b=b: e.activation(
                        sb2[b][:], wb[b][:], ACTF.Abs, bias=0.0, scale=1.0))
                    atok = k.op("scalar", lambda e: e.drain(), inc="ss")
                else:
                    k.op("vector", lambda e, b=b: e.scalar_tensor_tensor(
                        sb2[b][:], wb[b][:], -1.0, wb[b][:], ALU.mult, ALU.max))
                    atok = k.op("vector", lambda e: e.drain(), inc="sv")
                wb_free[b] = atok
                k.wait("tensor", atok)
                if t == 0:
                    k.wait("tensor", pb_tok[2])
                for sx in range(4):
                    k.op("tensor", lambda e, b=b, sx=sx, t=t: e.matmul(
                        pb[2][:, 0:512], onesf[:],
                        sb2[b][:, sx * 512:(sx + 1) * 512],
                        start=(t == 0 and sx == 0), stop=(t == NT - 1 and sx == 3)))
                mmtok[t] = k.mark("tensor", "st")
                sb2_free[b] = mmtok[t]
            k.wait("vector", mmtok[NT - 1])
            k.op("vector", lambda e, muw=muw: e.reduce_sum(
                muw, pb[2][:, 0:512], AX.X))
            k.op("vector", lambda e: e.drain())
            k.op("vector", lambda e, muw=muw: e.tensor_scalar(
                muw, muw, 1.0 / (D * INNER), 1e-5, ALU.mult, ALU.max))
            k.op("vector", lambda e: e.drain())
            k.op("vector", lambda e, wi=wi, muw=muw: e.tensor_copy(
                pvec[:, 16 + wi:17 + wi], muw))
            k.op("vector", lambda e: e.drain())
            k.op("vector", lambda e, r2w=r2w, muw=muw: e.reciprocal(r2w, muw))
            vr2 = k.op("vector", lambda e: e.drain(), inc="sv")
            pb_tok[2] = vr2
            k.flush(f"p2s_{wi}")
            for t in range(NT):
                b = t % 2
                k.wait(P2Q[b], wb_free[b])
                tl = stP[b](P2Q[b], lambda e, wap=wap, flat=flat, t=t, b=b:
                            e.dma_start(out=wb[b][:], in_=tile_src(wap, flat, t)))
                k.wait("scalar", tl)
                if t == 0:
                    k.wait("scalar", vr2)
                k.wait("scalar", sb2_free[b])
                k.op("scalar", lambda e, b=b, r2w=r2w: e.activation(
                    sb2[b][:], wb[b][:], ACTF.Identity, bias=0.0, scale=r2w))
                tq = k.op("scalar", lambda e: e.drain(), inc="ss")
                k.wait("vector", tq)
                k.wait("vector", ob_free[b])
                k.op("vector", lambda e, b=b: e.tensor_scalar(
                    wb[b][:], sb2[b][:], -0.5, None, ALU.is_lt, ALU.bypass))
                k.op("vector", lambda e: e.drain())
                k.op("vector", lambda e, b=b: e.scalar_tensor_tensor(
                    ob[b][:], sb2[b][:], 0.5, wb[b][:], ALU.is_gt, ALU.subtract))
                vtt = k.op("vector", lambda e: e.drain(), inc="sv")
                wb_free[b] = vtt
                sb2_free[b] = vtt
                k.wait(P2Q[1 - b], vtt)
                ob_free[b] = stQ[b](P2Q[1 - b], lambda e, dram=dram, flat=flat,
                                    t=t, b=b: e.dma_start(
                    out=tile_dst(dram, flat, t), in_=ob[b][:]))
            k.flush(f"p2t_{wi}")
        k.wait("sync", ob_free[0])
        k.wait("sync", ob_free[1])
        k.barrier()
        k.flush("p2_done")

    if "l1m" in PARTS:
        web = pbf(11264, 2048)
        stWL = k.stream("stWL", 2)
        stWS = k.stream("stWS", 2)
        wetoks = []
        i_w = 0
        for dout in range(NCH):
            for e_ in range(NEX):
                buf = i_w % 2
                if i_w >= 2:
                    k.wait("gpsimd", wetoks[i_w - 2])
                tg1 = stWL("gpsimd", lambda e, dout=dout, e_=e_, buf=buf:
                           e.dma_start(
                    out=web[:, buf * 1024:(buf + 1) * 1024],
                    in_=We[dout, e_].rearrange("(a p) m -> p a m", p=128)))
                k.wait("gpsimd", tg1)
                wetoks.append(stWS("gpsimd", lambda e, dout=dout, e_=e_, buf=buf:
                                   e.dma_start(
                    out=WeBf[dout, e_].rearrange("(a p) m -> p a m", p=128),
                    in_=web[:, buf * 1024:(buf + 1) * 1024])))
                i_w += 1
        if wetoks:
            k.wait("gpsimd", wetoks[-1])
            k.wait("gpsimd", wetoks[-2])

    if "l0f" in PARTS:
        gu = pf(0, 8192)
        xg3 = pf(8192, 2048)
        fct_b = pf(10240, 512)
        fct2_b = pf(10752, 512)
        stG = k.stream("stG", 4)
        stU = k.stream("stU", 4)
        stD = k.stream("stD", 4)
        aqx3 = regC[:, 0:2048]
        fsg = regA[:, 0:4096]
        fsu = regA[:, 4096:8192]
        fsd = regA[:, 8192:12288]
        T3 = TG3

        for g3 in range(NG3):
            k.wait("sync", k.tok("sv"))
            toks = [k.op("sync", lambda e, ch=ch, g3=g3: e.dma_start(
                out=xg3[:, ch * T3:(ch + 1) * T3],
                in_=xT[ch * 128:(ch + 1) * 128, HALO + g3 * T3: HALO + (g3 + 1) * T3]),
                inc="sd", k=16) for ch in range(NCH)]
            bcast_rows([(rowg(R_MU0, g3, t=T3), mu_b),
                        (rowg(R_R0, g3, t=T3), r_b)], T3)
            bcast_drows([(D_AQM, g3 * T3, mu2_b)], T3)
            k.wait("vector", toks[-1])
            k.op("vector", lambda e: e.reciprocal(r2_b[:, 0:T3], mu2_b[:, 0:T3]))
            k.op("vector", lambda e: e.drain())
            k.op("vector", lambda e: e.tensor_scalar(
                r2_b[:, 0:T3], r2_b[:, 0:T3], 127.0, None, ALU.mult, ALU.bypass))
            k.op("vector", lambda e: e.drain())
            k.op("vector", lambda e: e.tensor_scalar(
                fct_b[:, 0:T3], mu2_b[:, 0:T3], pvec[:, 16:17], 1.0 / 127.0,
                ALU.mult, ALU.mult))
            k.op("vector", lambda e: e.drain())
            k.op("vector", lambda e: e.tensor_scalar(
                fct2_b[:, 0:T3], mu2_b[:, 0:T3], pvec[:, 17:18], 1.0 / 127.0,
                ALU.mult, ALU.mult))
            k.op("vector", lambda e: e.drain())
            for ch in range(NCH):
                sl = xg3[:, ch * T3:(ch + 1) * T3]
                k.op("vector", lambda e, sl=sl: e.tensor_tensor(
                    sl, sl, mu_b[:, 0:T3], ALU.subtract))
                k.op("vector", lambda e: e.drain())
                k.op("vector", lambda e, sl=sl: e.tensor_tensor(
                    sl, sl, r_b[:, 0:T3], ALU.mult))
                k.op("vector", lambda e: e.drain())
                k.op("vector", lambda e, sl=sl: e.tensor_tensor(
                    sl, sl, r2_b[:, 0:T3], ALU.mult))
                k.op("vector", lambda e: e.drain())
                k.op("vector", lambda e, sl=sl: e.tensor_scalar(
                    sl, sl, MAGIC, -MAGIC, ALU.add, ALU.add))
                k.op("vector", lambda e: e.drain())
                k.op("vector", lambda e, ch=ch, sl=sl: e.tensor_scalar(
                    aqx3[:, ch * T3:(ch + 1) * T3], sl, -128.0, 127.0,
                    ALU.max, ALU.min))
                k.op("vector", lambda e: e.drain())
            vaq = k.op("vector", lambda e: e.drain(), inc="sv")
            if DEBUG and g3 == 0:
                k.wait("sync", vaq)
                k.op("sync", lambda e: e.dma_start(
                    out=dbg.bitcast(BF16)[0:128, 0:2048], in_=aqx3[:, 0:2048]),
                    inc="so", k=16)
                k.op("sync", lambda e: e.dma_start(
                    out=dbg[0:128, 4096:4096 + 256], in_=fct_b[:, 0:256]),
                    inc="so", k=16)
                k.op("sync", lambda e: e.dma_start(
                    out=dbg[0:128, 4352:4352 + 256], in_=fct2_b[:, 0:256]),
                    inc="so", k=16)
                k.op("sync", lambda e: e.dma_start(
                    out=dbg[0:128, 4608:4608 + 256], in_=r2_b[:, 0:256]),
                    inc="so", k=16)
                k.wait("vector", k.tok("so"))

            gutoks = []
            for m in range(32):
                buf = m % 4
                if m >= 4:
                    k.wait("sync", gutoks[m - 4])
                tg_ = stG("sync", lambda e, m=m, buf=buf: e.dma_start(
                    out=fsg[:, buf * 1024:(buf + 1) * 1024],
                    in_=ternG.rearrange("p (c x) -> p c x", x=INNER)[
                        :, :, m * 128:(m + 1) * 128]))
                tu_ = stU("sync", lambda e, m=m, buf=buf: e.dma_start(
                    out=fsu[:, buf * 1024:(buf + 1) * 1024],
                    in_=ternU.rearrange("p (c x) -> p c x", x=INNER)[
                        :, :, m * 128:(m + 1) * 128]))
                bg_ = 0 if (m % 2) == 0 else 2
                bu_ = 1 if (m % 2) == 0 else 3
                mm_bank(bg_, (tg_, tu_, vaq) if m < 2 else (tg_, tu_))
                for kc in range(NCH):
                    k.op("tensor", lambda e, m=m, kc=kc, buf=buf, bg_=bg_: e.matmul(
                        pb[bg_][:, 0:T3],
                        fsg[:, buf * 1024 + kc * 128: buf * 1024 + (kc + 1) * 128],
                        aqx3[:, kc * T3:(kc + 1) * T3],
                        start=(kc == 0), stop=(kc == NCH - 1)))
                mm_bank(bu_, (tu_,))
                for kc in range(NCH):
                    k.op("tensor", lambda e, m=m, kc=kc, buf=buf, bu_=bu_: e.matmul(
                        pb[bu_][:, 0:T3],
                        fsu[:, buf * 1024 + kc * 128: buf * 1024 + (kc + 1) * 128],
                        aqx3[:, kc * T3:(kc + 1) * T3],
                        start=(kc == 0), stop=(kc == NCH - 1)))
                tmm = k.mark("tensor", "st")
                sf = sbf2.bitcast(F32)
                k.wait("vector", tmm)
                k.op("vector", lambda e, bg_=bg_: e.tensor_tensor(
                    sf[:, 0:T3], pb[bg_][:, 0:T3], fct_b[:, 0:T3], ALU.mult))
                vg = k.op("vector", lambda e: e.drain(), inc="sv")
                k.wait("scalar", vg)
                ssil = k.op("scalar", lambda e, sf=sf: e.activation(
                    sf[:, T3:2 * T3], sf[:, 0:T3], ACTF.Silu, bias=0.0, scale=1.0),
                    inc="ss")
                k.wait("vector", ssil)
                gu_m = gu[:, m * T3:(m + 1) * T3]
                k.op("vector", lambda e, gu_m=gu_m, sf=sf, bu_=bu_: e.tensor_tensor(
                    gu_m, sf[:, T3:2 * T3], pb[bu_][:, 0:T3], ALU.mult))
                vgu = k.op("vector", lambda e: e.drain(), inc="sv")
                pb_tok[bg_] = vgu
                pb_tok[bu_] = vgu
                gutoks.append(tmm)
                if m == 0:
                    k.op("vector", lambda e, gu_m=gu_m: e.scalar_tensor_tensor(
                        r_b[:, 0:T3], gu_m, -1.0, gu_m, ALU.mult, ALU.max))
                else:
                    k.op("vector", lambda e, gu_m=gu_m, sf=sf:
                         e.scalar_tensor_tensor(
                        sf[:, 0:T3], gu_m, -1.0, gu_m, ALU.mult, ALU.max))
                    k.op("vector", lambda e: e.drain())
                    k.op("vector", lambda e, sf=sf: e.tensor_tensor(
                        r_b[:, 0:T3], r_b[:, 0:T3], sf[:, 0:T3], ALU.max))
                k.op("vector", lambda e: e.drain())
            vmx = k.op("vector", lambda e: e.drain(), inc="sv")
            t3 = None
            for c in range(T3 // 128):
                mm_bank(4, (vmx,) if c == 0 else ())
                t1 = k.op("tensor", lambda e, c=c: e.transpose(
                    pb[4][:, 0:128], r_b[:, c * 128:(c + 1) * 128],
                    ident_sb[:]), inc="st")
                k.wait("vector", t1)
                k.op("vector", lambda e: e.reduce_max(
                    gsc[:, 0:1], pb[4][:, 0:128], AX.X))
                vcl = k.op("vector", lambda e: e.drain(), inc="sv")
                pb_tok[4] = vcl
                mm_bank(5, (vcl,))
                t2 = k.op("tensor", lambda e: e.transpose(
                    pb[5][0:1, 0:128], gsc[:, 0:1], ident_sb[:]), inc="st")
                k.wait("vector", t2)
                k.wait("vector", t3)
                vrow = k.op("vector", lambda e: e.tensor_copy(
                    rowstage[0:1, 0:128], pb[5][0:1, 0:128]), inc="sv")
                pb_tok[5] = vrow
                k.op("vector", lambda e: e.drain())
                mm_bank(4, (vrow,))
                t3 = k.op("tensor", lambda e: e.matmul(
                    pb[4][:, 0:128], onesf[0:1, :], rowstage[0:1, 0:128],
                    start=True, stop=True), inc="st")
                k.wait("vector", t3)
                pb_tok[4] = k.op("vector", lambda e, c=c: e.tensor_copy(
                    mu_b[:, c * 128:(c + 1) * 128], pb[4][:, 0:128]), inc="sv")
                k.op("vector", lambda e: e.drain())
            k.op("vector", lambda e: e.tensor_tensor(
                mu_b[:, 0:T3], mu_b[:, 0:T3], fct2_b[:, 0:T3], ALU.mult))
            k.op("vector", lambda e: e.drain())
            k.op("vector", lambda e: e.tensor_scalar(
                mu_b[:, 0:T3], mu_b[:, 0:T3], 1e-5, None, ALU.max, ALU.bypass))
            k.op("vector", lambda e: e.drain())
            k.op("vector", lambda e: e.tensor_scalar(
                r2_b[:, 0:T3], mu_b[:, 0:T3], pvec[:, 18:19], 1.0 / 127.0,
                ALU.mult, ALU.mult))
            k.op("vector", lambda e: e.drain())
            k.op("vector", lambda e: e.reciprocal(mu_b[:, 0:T3], mu_b[:, 0:T3]))
            k.op("vector", lambda e: e.drain())
            k.op("vector", lambda e: e.tensor_tensor(
                mu_b[:, 0:T3], mu_b[:, 0:T3], fct2_b[:, 0:T3], ALU.mult))
            k.op("vector", lambda e: e.drain())
            k.op("vector", lambda e: e.tensor_scalar(
                mu_b[:, 0:T3], mu_b[:, 0:T3], 127.0, None, ALU.mult, ALU.bypass))
            k.op("vector", lambda e: e.drain())
            for m in range(32):
                gu_m = gu[:, m * T3:(m + 1) * T3]
                aq2_m = gu.bitcast(BF16)[:, 2 * m * T3: 2 * m * T3 + T3]
                k.op("vector", lambda e, gu_m=gu_m: e.tensor_tensor(
                    gu_m, gu_m, mu_b[:, 0:T3], ALU.mult))
                k.op("vector", lambda e: e.drain())
                k.op("vector", lambda e, gu_m=gu_m: e.tensor_scalar(
                    gu_m, gu_m, MAGIC, -MAGIC, ALU.add, ALU.add))
                k.op("vector", lambda e: e.drain())
                k.op("vector", lambda e, gu_m=gu_m, aq2_m=aq2_m: e.tensor_scalar(
                    aq2_m, gu_m, -128.0, 127.0, ALU.max, ALU.min))
                k.op("vector", lambda e: e.drain())
            vaq2 = k.op("vector", lambda e: e.drain(), inc="sv")
            if DEBUG and g3 == 0:
                k.wait("sync", vaq2)
                for mm_ in range(4):
                    k.op("sync", lambda e, mm_=mm_: e.dma_start(
                        out=dbg.bitcast(BF16)[0:128, 16384 + mm_ * 256:
                                              16384 + (mm_ + 1) * 256],
                        in_=gu.bitcast(BF16)[:, 2 * mm_ * T3: 2 * mm_ * T3 + T3]),
                        inc="so", k=16)
                k.op("sync", lambda e: e.dma_start(
                    out=dbg[0:128, 9216:9216 + 256], in_=mu_b[:, 0:256]),
                    inc="so", k=16)
                k.wait("vector", k.tok("so"))
            for i in range(8):
                k.wait("tensor", pb_tok[i])
            dttoks = []
            for m in range(32):
                buf = m % 4
                if m >= 4:
                    k.wait("sync", dttoks[m - 4])
                tw = stD("sync", lambda e, m=m, buf=buf: e.dma_start(
                    out=fsd[:, buf * 1024:(buf + 1) * 1024],
                    in_=ternD[:, m * D:(m + 1) * D]))
                k.wait("tensor", tw)
                if m == 0:
                    k.wait("tensor", vaq2)
                for dout in range(NCH):
                    k.op("tensor", lambda e, m=m, buf=buf, dout=dout: e.matmul(
                        pb[dout][:, 0:T3],
                        fsd[:, buf * 1024 + dout * 128: buf * 1024 + (dout + 1) * 128],
                        gu.bitcast(BF16)[:, 2 * m * T3: 2 * m * T3 + T3],
                        start=(m == 0), stop=(m == 31)))
                dttoks.append(k.mark("tensor", "st"))
            tdn = dttoks[-1]
            k.wait("vector", tdn)
            sf = sbf2.bitcast(F32)
            for dout in range(NCH):
                k.op("vector", lambda e, dout=dout, sf=sf: e.tensor_tensor(
                    sf[:, 0:T3], pb[dout][:, 0:T3], r2_b[:, 0:T3], ALU.mult))
                k.op("vector", lambda e: e.drain())
                pb_tok[dout] = k.op(
                    "vector", lambda e, dout=dout, g3=g3, sf=sf: e.tensor_tensor(
                        x1c(dout, g3, t=T3), x1c(dout, g3, t=T3), sf[:, 0:T3],
                        ALU.add), inc="sv")
                k.op("vector", lambda e: e.drain())
            k.flush(f"p3_g{g3}")
        k.barrier()
        k.flush("p3_done")


    if "l1a" in PARTS:
        xn1_bf = regA
        ucx = pbf(0, 16384)
        wslot = pbf(8192, 8192)
        wsf = pf(8192, 4096)
        decb = wsf[:, 0:2048]
        xb = regB[:, 0:4096]
        sc32 = regB.bitcast(F32)

        twt = []
        for kc in range(NCH):
            twt.append(k.op("gpsimd", lambda e, kc=kc: e.dma_start(
                out=wslot[:, kc * 1024:(kc + 1) * 1024],
                in_=Wt[kc * 128:(kc + 1) * 128, :]), inc="sg", k=16))

        for g in range(NG):
            k.wait("vector", k.tok("so"))
            vst0 = ln_stats(lambda ch, g=g: x1c(ch, g), TG, 1e-5, mu_b, r_b)
            k.wait("sync", vst0)
            k.op("sync", lambda e, g=g: e.dma_start(
                out=rows_d[D_MUX, g * TG:(g + 1) * TG], in_=mu_b[0:1, 0:TG]),
                inc="so", k=16)
            k.op("sync", lambda e, g=g: e.dma_start(
                out=rows_d[D_RX, g * TG:(g + 1) * TG], in_=r_b[0:1, 0:TG]),
                inc="so", k=16)
            vt = k.tok("sv")
            k.wait("vector", vt)
            sf = sbf2.bitcast(F32)
            for ch in range(NCH):
                k.op("vector", lambda e, ch=ch, g=g, sf=sf: e.tensor_tensor(
                    sf[:, 0:TG], x1c(ch, g), mu_b[:, 0:TG], ALU.subtract))
                k.op("vector", lambda e: e.drain())
                k.op("vector", lambda e, ch=ch, g=g, sf=sf: e.tensor_tensor(
                    xn1_bf[:, ch * 2048 + g * TG: ch * 2048 + (g + 1) * TG],
                    sf[:, 0:TG], r_b[:, 0:TG], ALU.mult))
                k.op("vector", lambda e: e.drain())
            k.op("vector", lambda e: e.drain(), inc="sv")
            vst1 = ln_stats(lambda ch, g=g: xn1_bf[:, ch * 2048 + g * TG:
                                            ch * 2048 + (g + 1) * TG],
                            TG, 1e-6, mu2_b, r2_b)
            k.wait("sync", vst1)
            k.op("sync", lambda e, g=g: e.dma_start(
                out=rows_d[D_MU1, g * TG:(g + 1) * TG], in_=mu2_b[0:1, 0:TG]),
                inc="so", k=16)
            k.op("sync", lambda e, g=g: e.dma_start(
                out=rows_d[D_R1, g * TG:(g + 1) * TG], in_=r2_b[0:1, 0:TG]),
                inc="so", k=16)
            vt2 = k.tok("sv")
            k.wait("vector", vt2)
            for ch in range(NCH):
                k.op("vector", lambda e, ch=ch, g=g, sf=sf: e.tensor_tensor(
                    sf[:, 0:TG], x1c(ch, g), mu_b[:, 0:TG], ALU.subtract))
                k.op("vector", lambda e: e.drain())
                k.op("vector", lambda e, sf=sf: e.tensor_tensor(
                    sf[:, 0:TG], sf[:, 0:TG], r_b[:, 0:TG], ALU.mult))
                k.op("vector", lambda e: e.drain())
                k.op("vector", lambda e, sf=sf: e.tensor_tensor(
                    sf[:, 0:TG], sf[:, 0:TG], mu2_b[:, 0:TG], ALU.subtract))
                k.op("vector", lambda e: e.drain())
                k.op("vector", lambda e, sf=sf: e.tensor_tensor(
                    sf[:, 0:TG], sf[:, 0:TG], r2_b[:, 0:TG], ALU.mult))
                vc = k.op("vector", lambda e: e.drain(), inc="sv")
                k.wait("scalar", vc)
                sc = k.op("scalar", lambda e, ch=ch, sf=sf: e.activation(
                    xb[:, ch * TG:(ch + 1) * TG], sf[:, 0:TG], ACTF.Identity,
                    bias=lpar[:, 8 + ch:9 + ch], scale=lpar[:, ch:ch + 1]), inc="ss")
                k.wait("vector", sc)
            hx = k.tok("ss")
            k.wait("tensor", twt[-1])
            k.wait("tensor", hx)
            for m in range(NCH):
                bank = 4 + (m % 2)
                mm_bank(bank)
                for kc in range(NCH):
                    k.op("tensor", lambda e, m=m, kc=kc, bank=bank: e.matmul(
                        pb[bank][:, 0:TG],
                        wslot[:, kc * 1024 + m * 128: kc * 1024 + (m + 1) * 128],
                        xb[:, kc * TG:(kc + 1) * TG],
                        start=(kc == 0), stop=(kc == NCH - 1)))
                tmm = k.mark("tensor", "st")
                k.wait("scalar", tmm)
                pb_tok[bank] = k.op("scalar", lambda e, m=m, g=g, bank=bank:
                                    e.activation(
                    ucx[:, m * 2048 + g * TG: m * 2048 + (g + 1) * TG],
                    pb[bank][:, 0:TG], ACTF.Identity, bias=0.0,
                    scale=betav[:, m:m + 1]), inc="ss")
            k.flush(f"p4a_g{g}")

        uw = k.tok("ss")
        u32v = wsf[:, 2048:4096]
        k.waits("vector", [uw])
        for ch in range(NCH):
            k.op("vector", lambda e: e.memset(decb[:, 0:2048], 1.0))
            k.op("vector", lambda e: e.drain())
            k.op("vector", lambda e, ch=ch: e.tensor_scalar(
                decb[:, 0:2048], decb[:, 0:2048], dvec[:, ch:ch + 1], None,
                ALU.mult, ALU.bypass))
            k.op("vector", lambda e: e.drain())
            k.op("vector", lambda e, ch=ch: e.tensor_copy(
                u32v[:, 0:2048], ucx[:, ch * 2048:(ch + 1) * 2048]))
            k.op("vector", lambda e: e.drain())
            k.op("vector", lambda e: e.tensor_tensor_scan(
                sc32[:, 0:2048], decb[:, 0:2048],
                u32v[:, 0:2048], 0.0, ALU.mult, ALU.add))
            k.op("vector", lambda e: e.drain())
            k.op("vector", lambda e, ch=ch: e.tensor_copy(
                carry_sb[:, ch:ch + 1], sc32[:, 2047:2048]))
            k.op("vector", lambda e: e.drain())
        vst = k.tok("sv")
        bcast_oh_done = None
        mm_bank(0, (vst,))
        tb = k.op("tensor", lambda e: e.matmul(
            pb[0][:, 0:CORES], ones1f[0:1, :], oh2_sb[0:1, :],
            start=True, stop=True), inc="st")
        k.wait("vector", tb)
        pb_tok[0] = k.op("vector", lambda e: e.tensor_copy(
            oh2_sb[:, 0:CORES], pb[0][:, 0:CORES]), inc="sv")
        k.op("vector", lambda e: e.drain())
        for r_ in range(CORES):
            k.op("vector", lambda e, r_=r_: e.tensor_scalar(
                ccst[:, r_ * NCH:(r_ + 1) * NCH], carry_sb[:],
                oh2_sb[:, r_:r_ + 1], None, ALU.mult, ALU.bypass))
        vcc = k.op("vector", lambda e: e.drain(), inc="sv")
        k.wait("sync", vcc)
        dts = [k.op("sync", lambda e, r_=r_: e.dma_start(
            out=cc_in[r_ * 128:(r_ + 1) * 128, :],
            in_=ccst[:, r_ * NCH:(r_ + 1) * NCH]), inc="so", k=16)
            for r_ in range(CORES)]
        k.wait("gpsimd", dts[-1])
        tcc = k.op("gpsimd", lambda e: e.collective_compute(
            "ReduceScatter", ALU.add, replica_groups=[list(range(CORES))],
            ins=[cc_in[:, :].opt()], outs=[cc_out[:, :].opt()]), inc="cc")
        k.wait("gpsimd", tcc)
        tccd = k.op("gpsimd", lambda e: e.dma_start(out=ccres[:], in_=cc_out[:, :]),
                    inc="sg", k=16)
        k.flush("p4b_scan1")

        gbf = rowsb.bitcast(BF16)[:, 0:2048]
        wsfr = wsf
        xfr = wsfr[:, 0:2048]
        wg1v = wsfr[:, 2048:4096]
        wg2v = r2_b[:, 0:80]
        stWG = k.stream("stWG", 2)
        k.wait("sync", k.tok("sv"))
        twg1 = stWG("sync", lambda e: e.dma_start(
            out=wg1v, in_=wg1.rearrange("(a p) m -> p a m", p=128)))
        twg2l = stWG("sync", lambda e: e.dma_start(
            out=wg2v, in_=wg2.rearrange("(a p) m -> p a m", p=128)))
        k.wait("tensor", twg1)
        k.wait("tensor", twg2l)
        T4 = 256
        k.wait("sync", k.tok("so"))
        for gh in range(8):
            bcast_drows([(D_MUX, gh * T4, mu_b), (D_RX, gh * T4, r_b)], T4)
            vt = k.tok("sv")
            k.wait("vector", vt)
            sf = sbf2.bitcast(F32)
            for ch in range(NCH):
                k.op("vector", lambda e, ch=ch, gh=gh, sf=sf: e.tensor_tensor(
                    sf[:, 0:T4], x1[:, ch * TLOC + gh * T4: ch * TLOC + (gh + 1) * T4],
                    mu_b[:, 0:T4], ALU.subtract))
                k.op("vector", lambda e: e.drain())
                k.op("vector", lambda e, ch=ch, sf=sf: e.tensor_tensor(
                    xfr[:, ch * T4:(ch + 1) * T4], sf[:, 0:T4],
                    r_b[:, 0:T4], ALU.mult))
                k.op("vector", lambda e: e.drain())
            vx = k.op("vector", lambda e: e.drain(), inc="sv")
            for m2 in range(2):
                mm_bank(2 + m2, (vx,) if m2 == 0 else ())
                for kc in range(NCH):
                    k.op("tensor", lambda e, m2=m2, kc=kc: e.matmul(
                        pb[2 + m2][:, 0:T4],
                        wg1v[:, kc * 256 + m2 * 128: kc * 256 + (m2 + 1) * 128],
                        xfr[:, kc * T4:(kc + 1) * T4],
                        start=(kc == 0), stop=(kc == NCH - 1)))
            tg1 = k.mark("tensor", "st")
            k.wait("scalar", tg1)
            for m2 in range(2):
                pb_tok[2 + m2] = k.op("scalar", lambda e, m2=m2: e.activation(
                    probs_sb.bitcast(F32)[:, m2 * T4:(m2 + 1) * T4],
                    pb[2 + m2][:, 0:T4], ACTF.Gelu, bias=bg1_sb[:, m2:m2 + 1],
                    scale=1.0), inc="ss")
            sg1 = k.tok("ss")
            mm_bank(4, (sg1,))
            for m2 in range(2):
                k.op("tensor", lambda e, m2=m2: e.matmul(
                    pb[4][0:40, 0:T4], wg2v[:, m2 * 40:(m2 + 1) * 40],
                    probs_sb.bitcast(F32)[:, m2 * T4:(m2 + 1) * T4],
                    start=(m2 == 0), stop=(m2 == 1)))
            tg2 = k.mark("tensor", "st")
            k.wait("scalar", tg2)
            slog = k.op("scalar", lambda e: e.activation(
                mu2_b[0:20, 0:T4], pb[4][0:20, 0:T4], ACTF.Identity,
                bias=bg2_sb[0:20, 0:1], scale=1.0), inc="ss")
            pb_tok[4] = slog
            for q in range(2):
                mm_bank(5, (slog,) if q == 0 else ())
                tq = k.op("tensor", lambda e, q=q: e.transpose(
                    pb[5][0:128, 0:20], mu2_b[0:20, q * 128:(q + 1) * 128],
                    ident_sb[0:20, 0:20]), inc="st")
                k.wait("vector", tq)
                k.op("vector", lambda e: e.tensor_copy(gtm[:, 0:20], pb[5][:, 0:20]))
                k.op("vector", lambda e: e.drain())
                k.op("vector", lambda e: e.reduce_max(gsc[:, 0:1], gtm[:, 0:20], AX.X))
                vmx = k.op("vector", lambda e: e.drain(), inc="sv")
                k.wait("scalar", vmx)
                k.op("vector", lambda e: e.tensor_scalar(
                    gsc[:, 1:2], gsc[:, 0:1], -1.0, None, ALU.mult, ALU.bypass))
                vneg = k.op("vector", lambda e: e.drain(), inc="sv")
                k.wait("scalar", vneg)
                sexp = k.op("scalar", lambda e: e.activation(
                    gtm[:, 32:52], gtm[:, 0:20], ACTF.Exp, bias=gsc[:, 1:2],
                    scale=1.0, accum_out=gsc[:, 2:3]), inc="ss")
                k.wait("vector", sexp)
                k.op("vector", lambda e: e.reciprocal(gsc[:, 3:4], gsc[:, 2:3]))
                k.op("vector", lambda e: e.drain())
                k.op("vector", lambda e: e.tensor_scalar(
                    gtm[:, 32:52], gtm[:, 32:52], gsc[:, 3:4], None,
                    ALU.mult, ALU.bypass))
                k.op("vector", lambda e: e.drain())
                k.op("vector", lambda e: e.reduce_max(gsc[:, 0:1], gtm[:, 32:52], AX.X))
                k.op("vector", lambda e: e.drain())
                k.op("vector", lambda e: e.tensor_scalar(
                    gtm[:, 64:84], gtm[:, 32:52], gsc[:, 0:1], None,
                    ALU.is_equal, ALU.bypass))
                k.op("vector", lambda e: e.drain())
                k.op("vector", lambda e: e.reduce_sum(gsc[:, 1:2], gtm[:, 64:84], AX.X))
                k.op("vector", lambda e: e.drain())
                k.op("vector", lambda e: e.scalar_tensor_tensor(
                    gtm[:, 64:84], gtm[:, 64:84], -2.0, gtm[:, 32:52],
                    ALU.mult, ALU.add))
                k.op("vector", lambda e: e.drain())
                k.op("vector", lambda e: e.reduce_max(gsc[:, 2:3], gtm[:, 64:84], AX.X))
                k.op("vector", lambda e: e.drain())
                k.op("vector", lambda e: e.tensor_scalar(
                    gsc[:, 4:5], gsc[:, 1:2], 2.0, None, ALU.is_ge, ALU.bypass))
                k.op("vector", lambda e: e.drain())
                k.op("vector", lambda e: e.tensor_tensor(
                    gsc[:, 5:6], gsc[:, 0:1], gsc[:, 2:3], ALU.subtract))
                k.op("vector", lambda e: e.drain())
                k.op("vector", lambda e: e.tensor_tensor(
                    gsc[:, 5:6], gsc[:, 5:6], gsc[:, 4:5], ALU.mult))
                k.op("vector", lambda e: e.drain())
                k.op("vector", lambda e: e.tensor_tensor(
                    gsc[:, 5:6], gsc[:, 5:6], gsc[:, 2:3], ALU.add))
                k.op("vector", lambda e: e.drain())
                k.op("vector", lambda e: e.tensor_scalar(
                    gtm[:, 96:116], gtm[:, 32:52], gsc[:, 5:6], None,
                    ALU.is_ge, ALU.bypass))
                k.op("vector", lambda e: e.drain())
                k.op("vector", lambda e: e.tensor_tensor(
                    gtm[:, 96:116], gtm[:, 96:116], gtm[:, 32:52], ALU.mult))
                vg2 = k.op("vector", lambda e: e.drain(), inc="sv")
                mm_bank(6, (vg2,))
                tq2 = k.op("tensor", lambda e: e.transpose(
                    pb[6][0:20, 0:128], gtm[:, 96:116], ident_sb[:]), inc="st")
                k.wait("vector", tq2)
                pb_tok[6] = k.op("vector", lambda e, gh=gh, q=q: e.tensor_copy(
                    gbf[0:20, gh * T4 + q * 128: gh * T4 + (q + 1) * 128],
                    pb[6][0:20, 0:128]), inc="sv")
                k.op("vector", lambda e: e.drain())
                pb_tok[5] = pb_tok[6]
        k.flush("p4c_gating")

        k.wait("vector", k.tok("cc"))
        k.waits("vector", [tccd])
        for ch in range(NCH):
            k.op("vector", lambda e: e.memset(decb[:, 0:2048], 1.0))
            k.op("vector", lambda e: e.drain())
            k.op("vector", lambda e, ch=ch: e.tensor_scalar(
                decb[:, 0:2048], decb[:, 0:2048], dvec[:, ch:ch + 1], None,
                ALU.mult, ALU.bypass))
            k.op("vector", lambda e: e.drain())
            k.op("vector", lambda e, ch=ch: e.tensor_copy(
                u32v[:, 0:2048], ucx[:, ch * 2048:(ch + 1) * 2048]))
            k.op("vector", lambda e: e.drain())
            k.op("vector", lambda e, ch=ch: e.tensor_tensor_scan(
                ucx[:, ch * 2048:(ch + 1) * 2048], decb[:, 0:2048],
                u32v[:, 0:2048], ccres[:, ch:ch + 1], ALU.mult, ALU.add))
            k.op("vector", lambda e: e.drain())
        k.op("vector", lambda e: e.drain(), inc="sv")
        k.flush("p4d_scan2")

        scan2done = k.tok("sv")
        lncx = regB[:, 0:4096]
        out_bf = regB[:, 4096:8192]
        res_bf = regC[:, 0:4096]
        z_bf = regC[:, 4096:8192]
        s32a = wsf[:, 0:512]
        s32b = wsf[:, 512:1024]

        for g in range(NG):
            ln_stats(lambda ch, g=g: ucx[:, ch * 2048 + g * TG:
                                         ch * 2048 + (g + 1) * TG],
                     TG, 1e-6, mu_b, r_b)
            vt = k.tok("sv")
            k.wait("vector", vt)
            sf = sbf2.bitcast(F32)
            for ch in range(NCH):
                k.op("vector", lambda e, ch=ch, g=g, sf=sf: e.tensor_copy(
                    sf[:, 0:TG], ucx[:, ch * 2048 + g * TG: ch * 2048 + (g + 1) * TG]))
                k.op("vector", lambda e: e.drain())
                k.op("vector", lambda e, sf=sf: e.tensor_tensor(
                    sf[:, 0:TG], sf[:, 0:TG], mu_b[:, 0:TG], ALU.subtract))
                k.op("vector", lambda e: e.drain())
                k.op("vector", lambda e, sf=sf: e.tensor_tensor(
                    sf[:, 0:TG], sf[:, 0:TG], r_b[:, 0:TG], ALU.mult))
                vc = k.op("vector", lambda e: e.drain(), inc="sv")
                k.wait("scalar", vc)
                scx = k.op("scalar", lambda e, ch=ch, sf=sf: e.activation(
                    lncx[:, ch * TG:(ch + 1) * TG], sf[:, 0:TG], ACTF.Identity,
                    bias=lpar[:, 24 + ch:25 + ch], scale=lpar[:, 16 + ch:17 + ch]),
                    inc="ss")
                k.wait("vector", scx)
            hlncx = k.tok("ss")
            k.wait("gpsimd", scan2done)
            k.wait("gpsimd", k.tok("st"))
            k.wait("gpsimd", k.tok("sv"))
            two = [k.op("gpsimd", lambda e, kc=kc: e.dma_start(
                out=wslot[:, kc * 1024:(kc + 1) * 1024],
                in_=Wo[kc * 128:(kc + 1) * 128, :]), inc="sg", k=16)
                for kc in range(NCH)]
            k.waits("tensor", [two[-1], hlncx])
            for m in range(NCH):
                bank = 2 + (m % 2)
                mm_bank(bank)
                for kc in range(NCH):
                    k.op("tensor", lambda e, m=m, kc=kc, bank=bank: e.matmul(
                        pb[bank][:, 0:TG],
                        wslot[:, kc * 1024 + m * 128: kc * 1024 + (m + 1) * 128],
                        lncx[:, kc * TG:(kc + 1) * TG],
                        start=(kc == 0), stop=(kc == NCH - 1)))
                tmm = k.mark("tensor", "st")
                k.wait("scalar", tmm)
                pb_tok[bank] = k.op("scalar", lambda e, m=m, bank=bank: e.activation(
                    out_bf[:, m * TG:(m + 1) * TG], pb[bank][:, 0:TG],
                    ACTF.Identity, bias=0.0, scale=1.0), inc="ss")
            bcast_drows([(D_MUX, g * TG, mu_b), (D_RX, g * TG, r_b),
                         (D_MU1, g * TG, mu2_b), (D_R1, g * TG, r2_b)], TG)
            vt2 = k.tok("sv")
            k.wait("vector", vt2)
            for ch in range(NCH):
                k.op("vector", lambda e, ch=ch, g=g, sf=sf: e.tensor_tensor(
                    sf[:, 0:TG], x1c(ch, g), mu_b[:, 0:TG], ALU.subtract))
                k.op("vector", lambda e: e.drain())
                k.op("vector", lambda e, sf=sf: e.tensor_tensor(
                    sf[:, 0:TG], sf[:, 0:TG], r_b[:, 0:TG], ALU.mult))
                k.op("vector", lambda e: e.drain())
                k.op("vector", lambda e, sf=sf: e.tensor_tensor(
                    sf[:, 0:TG], sf[:, 0:TG], mu2_b[:, 0:TG], ALU.subtract))
                k.op("vector", lambda e: e.drain())
                k.op("vector", lambda e, sf=sf: e.tensor_tensor(
                    sf[:, 0:TG], sf[:, 0:TG], r2_b[:, 0:TG], ALU.mult))
                vc = k.op("vector", lambda e: e.drain(), inc="sv")
                k.wait("scalar", vc)
                sres = k.op("scalar", lambda e, ch=ch, sf=sf: e.activation(
                    res_bf[:, ch * TG:(ch + 1) * TG], sf[:, 0:TG], ACTF.Identity,
                    bias=lpar[:, 8 + ch:9 + ch], scale=lpar[:, ch:ch + 1]), inc="ss")
                k.wait("vector", sres)
            hres = k.tok("ss")
            k.wait("gpsimd", k.tok("st"))
            twg = [k.op("gpsimd", lambda e, kc=kc: e.dma_start(
                out=wslot[:, kc * 1024:(kc + 1) * 1024],
                in_=Wg[kc * 128:(kc + 1) * 128, :]), inc="sg", k=16)
                for kc in range(NCH)]
            k.wait("tensor", twg[-1])
            k.wait("tensor", hres)
            for i in range(8):
                k.wait("tensor", pb_tok[i])
            for m in range(NCH):
                for kc in range(NCH):
                    k.op("tensor", lambda e, m=m, kc=kc: e.matmul(
                        pb[m][:, 0:TG],
                        wslot[:, kc * 1024 + m * 128: kc * 1024 + (m + 1) * 128],
                        res_bf[:, kc * TG:(kc + 1) * TG],
                        start=(kc == 0), stop=False))
            th1 = k.mark("tensor", "st")
            k.wait("gpsimd", th1)
            twg2 = [k.op("gpsimd", lambda e, kc=kc: e.dma_start(
                out=wslot[:, kc * 1024:(kc + 1) * 1024],
                in_=Wg[D + kc * 128: D + (kc + 1) * 128, :]), inc="sg", k=16)
                for kc in range(NCH)]
            k.wait("tensor", twg2[-1])
            for m in range(NCH):
                for kc in range(NCH):
                    k.op("tensor", lambda e, m=m, kc=kc: e.matmul(
                        pb[m][:, 0:TG],
                        wslot[:, kc * 1024 + m * 128: kc * 1024 + (m + 1) * 128],
                        out_bf[:, kc * TG:(kc + 1) * TG],
                        start=False, stop=(kc == NCH - 1)))
            th2 = k.mark("tensor", "st")
            k.wait("scalar", th2)
            for m in range(NCH):
                pb_tok[m] = k.op("scalar", lambda e, m=m: e.activation(
                    z_bf[:, m * TG:(m + 1) * TG], pb[m][:, 0:TG], ACTF.Identity,
                    bias=lpar[:, 48 + m:49 + m], scale=1.0), inc="ss")
            zs = k.tok("ss")
            k.wait("vector", zs)
            ln_stats(lambda ch: z_bf[:, ch * TG:(ch + 1) * TG], TG, 1e-6, mu_b, r_b)
            vt3 = k.tok("sv")
            k.wait("vector", vt3)
            for ch in range(NCH):
                k.op("vector", lambda e, ch=ch, sf=sf: e.tensor_copy(
                    sf[:, 0:TG], z_bf[:, ch * TG:(ch + 1) * TG]))
                k.op("vector", lambda e: e.drain())
                k.op("vector", lambda e, sf=sf: e.tensor_tensor(
                    sf[:, 0:TG], sf[:, 0:TG], mu_b[:, 0:TG], ALU.subtract))
                k.op("vector", lambda e: e.drain())
                k.op("vector", lambda e, sf=sf: e.tensor_tensor(
                    sf[:, 0:TG], sf[:, 0:TG], r_b[:, 0:TG], ALU.mult))
                vc = k.op("vector", lambda e: e.drain(), inc="sv")
                k.wait("scalar", vc)
                sgate = k.op("scalar", lambda e, ch=ch, sf=sf: e.activation(
                    s32a[:, 0:TG], sf[:, 0:TG], ACTF.Sigmoid,
                    bias=lpar[:, 40 + ch:41 + ch], scale=lpar[:, 32 + ch:33 + ch]),
                    inc="ss")
                k.wait("vector", sgate)
                k.op("vector", lambda e, ch=ch: e.tensor_tensor(
                    s32b[:, 0:TG], out_bf[:, ch * TG:(ch + 1) * TG],
                    res_bf[:, ch * TG:(ch + 1) * TG], ALU.subtract))
                k.op("vector", lambda e: e.drain())
                k.op("vector", lambda e: e.tensor_tensor(
                    s32b[:, 0:TG], s32b[:, 0:TG], s32a[:, 0:TG], ALU.mult))
                k.op("vector", lambda e: e.drain())
                k.op("vector", lambda e, ch=ch, sf=sf: e.tensor_copy(
                    sf[:, 0:TG], res_bf[:, ch * TG:(ch + 1) * TG]))
                k.op("vector", lambda e: e.drain())
                k.op("vector", lambda e, sf=sf: e.tensor_tensor(
                    s32b[:, 0:TG], s32b[:, 0:TG], sf[:, 0:TG], ALU.add))
                k.op("vector", lambda e: e.drain())
                k.op("vector", lambda e, ch=ch, g=g: e.tensor_tensor(
                    x1c(ch, g), x1c(ch, g), s32b[:, 0:TG], ALU.add))
                k.op("vector", lambda e: e.drain(), inc="sv")
            k.flush(f"p4d_g{g}")
        k.barrier()
        k.flush("p4_done")

    if "l1m" in PARTS:
        xn1_bf = regA
        gbf = rowsb.bitcast(BF16)[:, 0:2048]
        sc_e = [regB, regC]
        wsl = [pbf(512 + i * 512, 1024) for i in range(3)] + \
              [pbf(8960 + i * 512, 1024) for i in range(3)]
        WEQ = ("sync", "gpsimd", "scalar")
        stWE = [k.stream(f"stWE{i}", 2) for i in range(3)]
        be_sb = pbf(2048, 1024)
        gall = pbf(2560, 10240)
        stBE = k.stream("stBE", 1)
        tbe = stBE("gpsimd", lambda e: e.dma_start(out=be_sb[0:NEX, :], in_=bev))
        selg = pbf(7680, 2560)
        vsel = k.op("sync", lambda e: e.dma_start(out=selg[:], in_=selgd),
                    inc="sd", k=16)

        for cg in range(4):
            if cg == 0:
                k.wait("tensor", vsel)
            for i in range(8):
                k.wait("tensor", pb_tok[i])
            for e_ in range(NEX):
                bank = e_ % 2
                if e_ >= 2:
                    k.wait("tensor", pb_tok[bank])
                tb_ = k.op("tensor", lambda e, e_=e_, cg=cg, bank=bank: e.matmul(
                    pb[bank][:, 0:TG], selg[0:NEX, e_ * 128:(e_ + 1) * 128],
                    gbf[0:NEX, cg * TG:(cg + 1) * TG],
                    start=True, stop=True), inc="st")
                k.wait("vector", tb_)
                pb_tok[bank] = k.op("vector", lambda e, e_=e_, bank=bank:
                                    e.tensor_copy(
                    gall[:, e_ * TG:(e_ + 1) * TG], pb[bank][:, 0:TG]), inc="sv")
                k.op("vector", lambda e: e.drain())
            k.wait("tensor", pb_tok[0])
            k.wait("tensor", pb_tok[1])
            sltoks = []
            sli = 0
            exp_last = []
            for e_ in range(NEX):
                scb = sc_e[e_ % 2]
                gbx = gall[:, e_ * TG:(e_ + 1) * TG]
                if e_ >= 2:
                    k.wait("vector", exp_last[e_ - 2])
                for ch in range(NCH):
                    k.op("vector", lambda e, ch=ch, cg=cg, scb=scb, gbx=gbx:
                         e.tensor_tensor(
                        scb[:, ch * TG:(ch + 1) * TG],
                        xn1_bf[:, ch * 2048 + cg * TG: ch * 2048 + (cg + 1) * TG],
                        gbx[:, 0:TG], ALU.mult))
                vsc = k.op("vector", lambda e: e.drain(), inc="sv")
                if e_ == 0:
                    k.wait("tensor", tbe)
                k.wait("tensor", vsc)
                for dout in range(NCH):
                    wbuf = wsl[sli % 6]
                    q_ = WEQ[(sli % 6) % 3]
                    if sli >= 6:
                        k.wait(q_, sltoks[sli - 6])
                    tw = stWE[(sli % 6) % 3](q_, lambda e, dout=dout, e_=e_,
                                             wbuf=wbuf: e.dma_start(
                        out=wbuf[:, 0:1024],
                        in_=WeBf[dout, e_].rearrange("(a p) m -> p a m", p=128)))
                    k.wait("tensor", tw)
                    if e_ == 0:
                        k.op("tensor", lambda e, dout=dout, cg=cg: e.matmul(
                            pb[dout][:, 0:TG],
                            be_sb[0:NEX, dout * 128:(dout + 1) * 128],
                            gbf[0:NEX, cg * TG:(cg + 1) * TG],
                            start=True, stop=False))
                    for din in range(NCH):
                        k.op("tensor", lambda e, dout=dout, din=din, scb=scb,
                             wbuf=wbuf, last=(e_ == NEX - 1 and din == NCH - 1):
                             e.matmul(
                            pb[dout][:, 0:TG],
                            wbuf[:, din * 128:(din + 1) * 128],
                            scb[:, din * TG:(din + 1) * TG],
                            start=False, stop=last))
                    sltoks.append(k.mark("tensor", "st"))
                    sli += 1
                exp_last.append(sltoks[-1])
            tcl = sltoks[-1]
            k.wait("vector", tcl)
            for dout in range(NCH):
                pb_tok[dout] = k.op("vector", lambda e, dout=dout, cg=cg:
                                    e.tensor_tensor(
                    x1c(dout, cg), x1c(dout, cg), pb[dout][:, 0:TG], ALU.add),
                    inc="sv")
                k.op("vector", lambda e: e.drain())
            k.flush(f"p5_cg{cg}")
        k.barrier()
        k.flush("p5_done")

    k.barrier()
    for ch in range(NCH):
        k.op("sync", lambda e, ch=ch: e.dma_start(
            out=y[ch * 128:(ch + 1) * 128, :], in_=x1c(ch)), inc="so", k=16)
    k.wait("sync", k.tok("so"))
    k.flush("out")
    k.close()
    return nc


def _build_mask(first_block_causal):
    q = np.arange(256)[None, :]
    j = np.arange(512)[:, None]
    rel = j - 256 - q
    valid = (rel <= 0) & (rel > -WIN)
    if first_block_causal:
        valid = valid & (j >= 256)
    m = valid.astype(ml_dtypes.bfloat16)
    out = np.zeros((128, 4 * 256), dtype=ml_dtypes.bfloat16)
    for kc in range(4):
        out[:, kc * 256:(kc + 1) * 256] = m[kc * 128:(kc + 1) * 128, :]
    return out


def _build_selg():
    s = np.zeros((128, NEX * 128), dtype=ml_dtypes.bfloat16)
    for e in range(NEX):
        s[e, e * 128:(e + 1) * 128] = 1.0
    return s


def make_inmaps(inputs):
    x = np.asarray(inputs["L0_attn_x"])
    T = lambda a: np.ascontiguousarray(np.asarray(a).T)
    C = np.ascontiguousarray
    PC = lambda a: C(np.asarray(a).reshape(NCH, 128).T)
    decay = (0.5 ** (1.0 / np.linspace(2.0, 8.0, D, dtype=np.float32))).astype(np.float32)
    We = np.asarray(inputs["L1_ffn_We"])
    wdT0 = np.asarray(inputs["L0_ffn_w_down"]).T
    base = {
        "qkvT": T(inputs["L0_attn_qkv_w"]),
        "outwT": T(inputs["L0_attn_out_w"]),
        "ln2w": PC(inputs["L0_attn_norm_w"]),
        "ln2b": PC(inputs["L0_attn_norm_b"]),
        "wgT": T(inputs["L0_ffn_w_gate"]),
        "wuT": T(inputs["L0_ffn_w_up"]),
        "wdT": C(wdT0.reshape(32, 128, D).transpose(1, 0, 2).reshape(128, 32 * D)),
        "Wt": C(np.asarray(inputs["L1_attn_Wt"])),
        "Wo": C(np.asarray(inputs["L1_attn_Wo"])),
        "Wg": C(np.asarray(inputs["L1_attn_Wg"])),
        "bgv": PC(inputs["L1_attn_bg"]),
        "ln1w": PC(inputs["L1_attn_ln1_w"]),
        "ln1b": PC(inputs["L1_attn_ln1_b"]),
        "lncw": PC(inputs["L1_attn_lnc_w"]),
        "lncb": PC(inputs["L1_attn_lnc_b"]),
        "lngw": PC(inputs["L1_attn_lng_w"]),
        "lngb": PC(inputs["L1_attn_lng_b"]),
        "wg1": C(np.asarray(inputs["L1_ffn_wg1"])),
        "bg1": C(np.asarray(inputs["L1_ffn_bg1"]).reshape(2, 128).T),
        "wg2": C(np.asarray(inputs["L1_ffn_wg2"])),
        "bg2": C(np.asarray(inputs["L1_ffn_bg2"]).reshape(40, 1)),
        "We": C(We.reshape(NEX, D, NCH, 128).transpose(2, 0, 1, 3)),
        "bev": C(np.asarray(inputs["L1_ffn_be"])),
        "ident": np.eye(128, dtype=np.float32),
        "decayc": C(decay.reshape(NCH, 128).T),
        "maskB": _build_mask(False),
        "selg": _build_selg(),
    }
    mA_first = _build_mask(True)
    in_maps = []
    for c in range(CORES):
        p, hh = c // 2, c % 2
        xp = np.asarray(x[p])
        if hh == 0:
            halo = np.zeros((HALO, D), np.float32)
            own = xp[0:TLOC]
        else:
            halo = xp[TLOC - HALO:TLOC]
            own = xp[TLOC:2 * TLOC]
        oh = np.zeros((1, CORES), np.float32)
        if hh == 0:
            oh[0, c + 1] = 1.0
        m = dict(base)
        m["xT"] = C(np.concatenate([halo, own], axis=0).T)
        m["oh2"] = oh
        m["maskA"] = mA_first if hh == 0 else base["maskB"]
        in_maps.append(m)
    return in_maps


_NC_CACHE = {}


def kernel(**inputs):
    if PARTS not in _NC_CACHE:
        _NC_CACHE[PARTS] = build_program()
    nc = _NC_CACHE[PARTS]
    in_maps = make_inmaps(inputs)
    res = run_bass_kernel_spmd(nc, in_maps, core_ids=list(range(CORES)))
    out = np.empty((4, 4096, 1024), np.float32)
    for c in range(CORES):
        p, hh = c // 2, c % 2
        out[p, hh * TLOC:(hh + 1) * TLOC, :] = np.asarray(res.results[c]["y"]).T
    return out


# version 13
# speedup vs baseline: 1.0494x; 1.0450x over previous
import numpy as np
import ml_dtypes
import concourse.bass as bass
import concourse.mybir as mybir
from concourse.bass_utils import run_bass_kernel_spmd

F32 = mybir.dt.float32
F32R = mybir.dt.float32r
BF16 = mybir.dt.bfloat16
ALU = mybir.AluOpType
ACTF = mybir.ActivationFunctionType
AX = mybir.AxisListType

D = 1024
NCH = 8
TLOC = 2048
HALO = 256
TTOT = TLOC + HALO
NG = 4
TG = 512
NG3 = 8
TG3 = 256
H, DH, WIN = 16, 64, 256
INNER = 4096
NEX = 20
MAGIC = 12582912.0
CORES = 8
ENGS = ("sync", "vector", "scalar", "tensor", "gpsimd")

PARTS = ("l0a", "l0f", "l1a", "l1m")
DEBUG = False


class K:
    def __init__(self, nc):
        self.nc = nc
        self.q = {e: [] for e in ENGS}
        self.cnt = {}
        self.sems = {}
        self._exits = []
        self.blocks = 0
        self.nbar = 0
        self._pace = {}
        self._bound = {}
        self._lastw = {}
        self._hist = {}

    def sem(self, name):
        cm = self.nc.semaphore(name)
        self.sems[name] = cm.__enter__()
        self._exits.append(cm)
        self.cnt[name] = 0

    def sbuf(self, name, shape, dtype=F32):
        cm = self.nc.sbuf_tensor(name, shape, dtype)
        t = cm.__enter__()
        self._exits.append(cm)
        return t

    def psum(self, name, shape, dtype=F32):
        cm = self.nc.psum_tensor(name, shape, dtype)
        t = cm.__enter__()
        self._exits.append(cm)
        return t

    def op(self, eng, fn, inc=None, k=1):
        tok = None
        if inc is not None:
            self.cnt[inc] += k
            tok = (inc, self.cnt[inc])
            self.q[eng].append(("op", fn, inc, k))
        else:
            self.q[eng].append(("op", fn, None, 0))
        return tok

    def stream(self, name, n=2):
        sems = []
        for i in range(n):
            nm = f"{name}{i}"
            self.sem(nm)
            sems.append(nm)
        st = {"i": 0, "sems": sems}

        def issue(eng, fn):
            sm = st["sems"][st["i"] % n]
            st["i"] += 1
            return self.op(eng, fn, inc=sm, k=16)
        return issue

    def mark(self, eng, inc, k=1):
        it = self.q[eng][-1]
        assert it[0] == "op" and it[2] is None, it
        self.cnt[inc] += k
        self.q[eng][-1] = ("op", it[1], inc, k)
        return (inc, self.cnt[inc])

    def wait(self, eng, tok):
        if tok is not None:
            self.q[eng].append(("wait", tok))

    def waits(self, eng, toks):
        for t in toks:
            self.wait(eng, t)

    def tok(self, name):
        return (name, self.cnt[name])

    def barrier(self):
        self.nbar += 1
        target = 5 * self.nbar
        self.q["sync"].append(("wait", self.tok("sd")))
        self.q["sync"].append(("wait", self.tok("so")))
        self.q["gpsimd"].append(("wait", self.tok("sg")))
        for e in ENGS:
            self.cnt["bar"] += 1
            self.q[e].append(("op", lambda eng: eng.nop(), "bar", 1))
        for e in ENGS:
            self.q[e].append(("wait", ("bar", target)))

    def flush(self, name=None):
        self.blocks += 1
        name = name or f"b{self.blocks}"
        qs = {e: self.q[e] for e in ENGS if self.q[e]}
        self.q = {e: [] for e in ENGS}
        with self.nc.named_scope(name), self.nc.Block(name) as b:
            for e, items in qs.items():
                def body(eng, items=items):
                    for it in items:
                        if it[0] == "wait":
                            _, (sn, sv) = it
                            eng.wait_ge(self.sems[sn], sv)
                        else:
                            _, fn, inc, kk = it
                            ins = fn(eng)
                            if inc is not None:
                                ins.then_inc(self.sems[inc], kk)
                getattr(b, e)(body)

    def close(self):
        for cm in reversed(self._exits):
            cm.__exit__(None, None, None)


def build_program():
    nc = bass.Bass("TRN2", target_bir_lowering=False)
    k = K(nc)
    for s in ("sd", "sv", "ss", "st", "sg", "sgc", "so", "cc", "bar"):
        k.sem(s)

    def din(name, shape, dtype=F32):
        return nc.dram_tensor(name, shape, dtype, kind="ExternalInput").ap()

    xT = din("xT", [D, TTOT])
    qkvT = din("qkvT", [D, 3 * D])
    outwT = din("outwT", [D, D])
    ln2w = din("ln2w", [128, NCH]); ln2b = din("ln2b", [128, NCH])
    wgT = din("wgT", [D, INNER]); wuT = din("wuT", [D, INNER])
    wdT = din("wdT", [128, 32 * D])
    Wt = din("Wt", [D, D]); Wo = din("Wo", [D, D]); Wg = din("Wg", [2 * D, D])
    bgv = din("bgv", [128, NCH])
    ln1w = din("ln1w", [128, NCH]); ln1b = din("ln1b", [128, NCH])
    lncw = din("lncw", [128, NCH]); lncb = din("lncb", [128, NCH])
    lngw = din("lngw", [128, NCH]); lngb = din("lngb", [128, NCH])
    wg1 = din("wg1", [D, 256]); bg1 = din("bg1", [128, 2])
    wg2 = din("wg2", [256, 40]); bg2 = din("bg2", [40, 1])
    We = din("We", [NCH, NEX, D, 128])
    bev = din("bev", [NEX, D])
    maskA = din("maskA", [128, 4 * 256], BF16)
    maskB = din("maskB", [128, 4 * 256], BF16)
    selgd = din("selg", [128, NEX * 128], BF16)
    ident = din("ident", [128, 128])
    decayc = din("decayc", [128, NCH])
    oh2 = din("oh2", [1, CORES])
    y = nc.dram_tensor("y", [D, TLOC], F32, kind="ExternalOutput").ap()
    ternG = nc.dram_tensor("ternG", [128, NCH * INNER], BF16).ap()
    ternU = nc.dram_tensor("ternU", [128, NCH * INNER], BF16).ap()
    ternD = nc.dram_tensor("ternD", [128, 32 * D], BF16).ap()
    WeBf = nc.dram_tensor("WeBf", [NCH, NEX, D, 128], BF16).ap()
    cc_in = nc.dram_tensor("cc_in", [CORES * 128, NCH], F32).ap()
    cc_out = nc.dram_tensor("cc_out", [128, NCH], F32).ap()
    rows_d = nc.dram_tensor("rows_d", [5, TLOC], F32).ap()
    dbg = nc.dram_tensor("dbg", [128, 16384], F32).ap()

    x1 = k.sbuf("x1", [128, NCH * TLOC])
    POOLN = 12352
    pool = k.sbuf("pool", [128, POOLN])
    regA = k.sbuf("regA", [128, 16384], BF16)
    regB = k.sbuf("regB", [128, 8192], BF16)
    regC = k.sbuf("regC", [128, 8192], BF16)
    rowsb = k.sbuf("rowsb", [128, TLOC])
    rowstage = k.sbuf("rowstage", [1, TG])
    qscale = k.sbuf("qscale", [128, 24])
    oscale = k.sbuf("oscale", [128, 8])
    pvec = k.sbuf("pvec", [128, 26])
    dvec = k.sbuf("dvec", [128, NCH])
    betav = k.sbuf("betav", [128, NCH])
    mu_b = k.sbuf("mu_b", [128, TG])
    r_b = k.sbuf("r_b", [128, TG])
    mu2_b = k.sbuf("mu2_b", [128, TG])
    r2_b = k.sbuf("r2_b", [128, TG])
    sbf2 = k.sbuf("sbf2", [128, 2 * TG], BF16)
    probs_sb = k.sbuf("probs_sb", [128, 4 * 256], BF16)
    ident_sb = k.sbuf("ident_sb", [128, 128])
    maskA_sb = k.sbuf("maskA_sb", [128, 4 * 256], BF16)
    maskB_sb = k.sbuf("maskB_sb", [128, 4 * 256], BF16)
    ones_bf = k.sbuf("ones_bf", [128, 128], BF16)
    ones1f = k.sbuf("ones1f", [1, 128])
    onesf = k.sbuf("onesf", [128, 128])
    carry_sb = k.sbuf("carry_sb", [128, NCH])
    ccst = k.sbuf("ccst", [128, CORES * NCH])
    ccres = k.sbuf("ccres", [128, NCH])
    oh2_sb = k.sbuf("oh2_sb", [128, CORES])
    bg1_sb = k.sbuf("bg1_sb", [128, 2])
    bg2_sb = k.sbuf("bg2_sb", [40, 1])
    gtm = k.sbuf("gtm", [128, 128])
    gsc = k.sbuf("gsc", [128, 8])
    epsv = k.sbuf("epsv", [128, 2])
    lpar = k.sbuf("lpar", [128, 56])

    R_MU0, R_R0, R_AQM = 0, 32, 64
    D_MUX, D_RX, D_MU1, D_R1, D_AQM = range(5)

    def rowg(i, g, t=TG, w=None):
        return rowsb[i:i + 1, g * t: g * t + (w or t)]

    def x1c(ch, g=None, t=TG, off=0, w=None):
        if g is None:
            return x1[:, ch * TLOC:(ch + 1) * TLOC]
        a = ch * TLOC + g * t + off
        return x1[:, a: a + (w or t)]

    PF = pool
    PB = pool.bitcast(BF16)

    def pf(off, cols):
        return PF[:, off:off + cols]

    def pbf(off, cols):
        return PB[:, 2 * off:2 * off + cols]

    pb = [k.psum(f"pb{i}", [128, 512]) for i in range(8)]
    pb_tok = [None] * 8

    def mm_bank(i, toks=()):
        k.wait("tensor", pb_tok[i])
        for t in toks:
            k.wait("tensor", t)

    def ln_stats(src_fn, T, eps, mu, r, rows_to=None, src_bf=True):
        epsap = epsv[:, 0:1] if eps == 1e-5 else epsv[:, 1:2]
        """Feature-major LN stats; src_fn(ch)->AP [128,T].
        Squares on scalar (double-buffered in sbf2 halves); the mean matmul
        reads the source directly (fp32 ones lhsT when src_bf=False)."""
        onessrc = ones_bf if src_bf else onesf
        vgate = k.tok("sv")
        k.wait("scalar", vgate)
        k.wait("tensor", vgate)
        sqtok = [None, None]
        mmtok2 = [None, None]
        for ch in range(NCH):
            sb = ch % 2
            k.wait("scalar", mmtok2[sb])
            k.op("scalar", lambda e, ch=ch, sb=sb: e.activation(
                sbf2[:, sb * T:(sb + 1) * T], src_fn(ch), ACTF.Square,
                bias=0.0, scale=1.0))
            sqtok[sb] = k.op("scalar", lambda e: e.drain(), inc="ss")
            if ch == 0:
                k.wait("tensor", pb_tok[0])
                k.wait("tensor", pb_tok[1])
            k.op("tensor", lambda e, ch=ch: e.matmul(
                pb[0][:, 0:T], onessrc[:], src_fn(ch),
                start=(ch == 0), stop=(ch == NCH - 1)))
            k.wait("tensor", sqtok[sb])
            k.op("tensor", lambda e, ch=ch, sb=sb: e.matmul(
                pb[1][:, 0:T], ones_bf[:], sbf2[:, sb * T:(sb + 1) * T],
                start=(ch == 0), stop=(ch == NCH - 1)))
            mmtok2[sb] = k.mark("tensor", "st")
        tt = mmtok2[(NCH - 1) % 2]
        k.wait("vector", tt)
        k.op("vector", lambda e: e.tensor_scalar(
            mu[:, 0:T], pb[0][:, 0:T], 1.0 / D, None, ALU.mult, ALU.bypass))
        k.op("vector", lambda e: e.drain())
        k.op("vector", lambda e: e.tensor_scalar(
            r[:, 0:T], pb[1][:, 0:T], 1.0 / D, None, ALU.mult, ALU.bypass))
        k.op("vector", lambda e: e.drain())
        sf = sbf2.bitcast(F32)
        k.op("vector", lambda e, sf=sf: e.tensor_tensor(
            sf[:, 0:T], mu[:, 0:T], mu[:, 0:T], ALU.mult))
        k.op("vector", lambda e: e.drain())
        k.op("vector", lambda e, sf=sf: e.tensor_tensor(
            r[:, 0:T], r[:, 0:T], sf[:, 0:T], ALU.subtract))
        vv = k.op("vector", lambda e: e.drain(), inc="sv")
        pb_tok[0] = vv
        pb_tok[1] = vv
        k.wait("scalar", vv)
        sv = k.op("scalar", lambda e, epsap=epsap: e.activation(
            r[:, 0:T], r[:, 0:T], ACTF.Sqrt, bias=epsap, scale=1.0), inc="ss")
        k.wait("vector", sv)
        k.op("vector", lambda e: e.reciprocal(r[:, 0:T], r[:, 0:T]))
        vtok = k.op("vector", lambda e: e.drain(), inc="sv")
        if rows_to is not None:
            mu_row, r_row = rows_to
            ip_mu = mu_row.base_partition() if callable(mu_row.base_partition) else mu_row.base_partition
            ip_r = r_row.base_partition() if callable(r_row.base_partition) else r_row.base_partition
            k.op("vector", lambda e, mu_row=mu_row, ip=ip_mu: e.tensor_copy(
                mu_row, mu[ip:ip + 1, 0:T]))
            k.op("vector", lambda e, r_row=r_row, ip=ip_r: e.tensor_copy(
                r_row, r[ip:ip + 1, 0:T]))
            vtok = k.op("vector", lambda e: e.drain(), inc="sv")
        return vtok

    stRS = k.stream("stRS", 2)
    stDN = k.stream("stDN", 1)

    def bcast_drows(specs, T, toks=()):
        first = True
        for rid, c0, dst in specs:
            k.wait("sync", k.tok("st"))
            td = stRS("sync", lambda e, rid=rid, c0=c0, T=T: e.dma_start(
                out=rowstage[0:1, 0:T], in_=rows_d[rid, c0:c0 + T]))
            mm_bank(0, (toks if first else ()) + (td,) if False else (td,))
            if first:
                for t_ in toks:
                    k.wait("tensor", t_)
            first = False
            tm = k.op("tensor", lambda e, T=T: e.matmul(
                pb[0][:, 0:T], ones1f[0:1, :], rowstage[0:1, 0:T],
                start=True, stop=True), inc="st")
            k.wait("vector", tm)
            pb_tok[0] = k.op("vector", lambda e, dst=dst, T=T: e.tensor_copy(
                dst[:, 0:T], pb[0][:, 0:T]), inc="sv")
            k.op("vector", lambda e: e.drain())

    def bcast_rows(specs, T, toks=()):
        first = True
        for rowap, dst in specs:
            mm_bank(0, toks if first else ())
            first = False
            bp = rowap.base_partition
            bp = bp() if callable(bp) else bp
            tm = k.op("tensor", lambda e, rowap=rowap, T=T, bp=bp: e.matmul(
                pb[0][:, 0:T], onesf[bp:bp + 1, :], rowap, start=True, stop=True),
                inc="st")
            k.wait("vector", tm)
            pb_tok[0] = k.op("vector", lambda e, dst=dst, T=T: e.tensor_copy(
                dst[:, 0:T], pb[0][:, 0:T]), inc="sv")
            k.op("vector", lambda e: e.drain())

    for ch in range(NCH):
        k.op("sync", lambda e, ch=ch: e.dma_start(
            out=x1c(ch), in_=xT[ch * 128:(ch + 1) * 128, HALO:TTOT]), inc="sd", k=16)
    k.op("sync", lambda e: e.dma_start(out=ident_sb[:], in_=ident), inc="sd", k=16)
    k.op("sync", lambda e: e.dma_start(out=maskA_sb[:], in_=maskA), inc="sd", k=16)
    k.op("sync", lambda e: e.dma_start(out=maskB_sb[:], in_=maskB), inc="sd", k=16)
    k.op("sync", lambda e: e.dma_start(out=pvec[:, 0:NCH], in_=ln2w), inc="sd", k=16)
    k.op("sync", lambda e: e.dma_start(out=pvec[:, 8:16], in_=ln2b), inc="sd", k=16)
    k.op("sync", lambda e: e.dma_start(out=dvec[:], in_=decayc), inc="sd", k=16)
    k.op("sync", lambda e: e.dma_start(out=bg1_sb[:], in_=bg1), inc="sd", k=16)
    k.op("sync", lambda e: e.dma_start(out=bg2_sb[:], in_=bg2), inc="sd", k=16)
    k.op("sync", lambda e: e.dma_start(out=oh2_sb[0:1, :], in_=oh2), inc="sd", k=16)
    for i_, ap_ in enumerate([ln1w, ln1b, lncw, lncb, lngw, lngb, bgv]):
        k.op("sync", lambda e, i_=i_, ap_=ap_: e.dma_start(
            out=lpar[:, i_ * 8:(i_ + 1) * 8], in_=ap_), inc="sd", k=16)
    k.op("vector", lambda e: e.memset(ones_bf[:], 1.0))
    k.op("vector", lambda e: e.memset(ones1f[:], 1.0))
    k.op("vector", lambda e: e.memset(onesf[:], 1.0))
    k.op("vector", lambda e: e.memset(epsv[:, 0:1], 1e-5))
    k.op("vector", lambda e: e.memset(epsv[:, 1:2], 1e-6))
    k.op("vector", lambda e: e.drain(), inc="sv")
    k.barrier()
    k.op("vector", lambda e: e.tensor_scalar(betav[:], dvec[:], -1.0, 1.0,
                                             ALU.mult, ALU.add))
    k.op("vector", lambda e: e.drain(), inc="sv")
    k.flush("p0_loads")

    qw = pf(0, 4096)
    qaf = pf(4096, 4096)
    qa = qaf
    qsb = mu_b
    qrb = r_b

    def quant_rowscaled(wT_ap, dout, dst_fn, sc_out):
        for jg in range(dout // TG):
            k.wait("sync", k.tok("sv"))
            toks = [k.op("sync", lambda e, ch=ch, jg=jg: e.dma_start(
                out=qw[:, ch * TG:(ch + 1) * TG],
                in_=wT_ap[ch * 128:(ch + 1) * 128, jg * TG:(jg + 1) * TG]),
                inc="sd", k=16) for ch in range(NCH)]
            k.wait("vector", toks[-1])
            for ch in range(NCH):
                k.op("vector", lambda e, ch=ch: e.scalar_tensor_tensor(
                    qa[:, ch * TG:(ch + 1) * TG], qw[:, ch * TG:(ch + 1) * TG],
                    -1.0, qw[:, ch * TG:(ch + 1) * TG], ALU.mult, ALU.max))
            va = k.op("vector", lambda e: e.drain(), inc="sv")
            mm_bank(2, (va,))
            for ch in range(NCH):
                k.op("tensor", lambda e, ch=ch: e.matmul(
                    pb[2][:, 0:TG], onesf[:], qa[:, ch * TG:(ch + 1) * TG],
                    start=(ch == 0), stop=(ch == NCH - 1)))
            tmm = k.mark("tensor", "st")
            k.wait("vector", tmm)
            k.op("vector", lambda e: e.tensor_scalar(
                qsb[:], pb[2][:, 0:TG], 1.0 / D, 1e-5, ALU.mult, ALU.max))
            vq = k.op("vector", lambda e: e.drain(), inc="sv")
            pb_tok[2] = vq
            k.op("vector", lambda e: e.reciprocal(qrb[:], qsb[:]))
            k.op("vector", lambda e: e.drain())
            for q in range(TG // 128):
                j = jg * (TG // 128) + q
                vq2 = k.op("vector", lambda e: e.drain(), inc="sv")
                mm_bank(3, (vq2,))
                tq = k.op("tensor", lambda e, q=q: e.transpose(
                    pb[3][:, 0:128], qsb[:, q * 128:(q + 1) * 128], ident_sb[:]),
                    inc="st")
                k.wait("vector", tq)
                pb_tok[3] = k.op("vector", lambda e, j=j: e.tensor_copy(
                    sc_out[:, j:j + 1], pb[3][:, 0:1]), inc="sv")
                k.op("vector", lambda e: e.drain())
            for ch in range(NCH):
                tmp = qaf[:, ch * TG:(ch + 1) * TG]
                neg = qw[:, ch * TG:(ch + 1) * TG]
                k.op("vector", lambda e, ch=ch, tmp=tmp, neg=neg: e.tensor_tensor(
                    tmp, neg, qrb[:], ALU.mult))
                k.op("vector", lambda e: e.drain())
                k.op("vector", lambda e, tmp=tmp, neg=neg: e.tensor_scalar(
                    neg, tmp, -0.5, None, ALU.is_lt, ALU.bypass))
                k.op("vector", lambda e: e.drain())
                for q in range(TG // 128):
                    j = jg * (TG // 128) + q
                    k.op("vector", lambda e, ch=ch, tmp=tmp, neg=neg, q=q, j=j:
                         e.scalar_tensor_tensor(
                        dst_fn(ch, j), tmp[:, q * 128:(q + 1) * 128], 0.5,
                        neg[:, q * 128:(q + 1) * 128], ALU.is_gt, ALU.subtract))
                k.op("vector", lambda e: e.drain())
            k.op("vector", lambda e: e.drain(), inc="sv")
            k.flush(f"q_{wT_ap.tensor.name}_{jg}")

    qk_t = regA
    v_t = regB
    ow_t = regC

    def qkv_dst(ch, j):
        if j >= 16:
            return v_t[:, ch * 1024 + (j - 16) * 128: ch * 1024 + (j - 15) * 128]
        return qk_t[:, ch * 2048 + j * 128: ch * 2048 + (j + 1) * 128]

    def ow_dst(ch, j):
        return ow_t[:, ch * 1024 + j * 128: ch * 1024 + (j + 1) * 128]

    quant_rowscaled(qkvT, 3 * D, qkv_dst, qscale)
    quant_rowscaled(outwT, D, ow_dst, oscale)
    k.op("vector", lambda e: e.tensor_scalar(
        qscale[:, 0:8], qscale[:, 0:8], DH ** -0.5, None, ALU.mult, ALU.bypass))
    k.op("vector", lambda e: e.drain(), inc="sv")
    k.barrier()
    k.flush("p0_done")

    halo_x = pf(0, 2048)
    q_bf = pbf(0, 4096)
    xn_g = pf(2048, 4096)
    h_bf = pbf(2048, 4096)
    araw = pf(4096, 2048)
    kroll = pbf(6144, 6144)
    vroll = pbf(9216, 6240)
    VW = H * 65
    aqm_b = r_b
    probs_b = None
    prev_tb = {}
    prev_tpv = {}

    for ch in range(NCH):
        k.op("sync", lambda e, ch=ch: e.dma_start(
            out=halo_x[:, ch * HALO:(ch + 1) * HALO],
            in_=xT[ch * 128:(ch + 1) * 128, 0:HALO]), inc="sd", k=16)
    t_halo = k.tok("sd")
    for u in range(6):
        for hh in range(H):
            k.op("vector", lambda e, u=u, hh=hh: e.memset(
                vroll[:, u * VW + hh * 65 + 64: u * VW + hh * 65 + 65], 1.0))
    k.op("vector", lambda e: e.drain(), inc="sv")
    k.flush("p1_pre")

    def emit_h(xsrc, T, g, store_rows):
        ln_stats(xsrc, T, 1e-5, mu_b, r_b, src_bf=False,
                 rows_to=(rowg(R_MU0, g, w=T), rowg(R_R0, g, w=T)) if store_rows else None)
        vt = k.tok("sv")
        k.wait("vector", vt)
        for ch in range(NCH):
            k.op("vector", lambda e, ch=ch, T=T: e.tensor_tensor(
                xn_g[:, ch * TG: ch * TG + T], xsrc(ch), mu_b[:, 0:T], ALU.subtract))
            k.op("vector", lambda e: e.drain())
            k.op("vector", lambda e, ch=ch, T=T: e.tensor_tensor(
                xn_g[:, ch * TG: ch * TG + T], xn_g[:, ch * TG: ch * TG + T],
                r_b[:, 0:T], ALU.mult))
            k.op("vector", lambda e: e.drain())
        k.op("vector", lambda e: e.drain(), inc="sv")

        def xnsrc(ch, T=T):
            return xn_g[:, ch * TG: ch * TG + T]

        ln_stats(xnsrc, T, 1e-5, mu2_b, r2_b, src_bf=False)
        sf = sbf2.bitcast(F32)
        if store_rows:
            vt2 = k.tok("sv")
            k.wait("vector", vt2)
            k.op("vector", lambda e, T=T: e.scalar_tensor_tensor(
                aqm_b[:, 0:T], xnsrc(0), -1.0, xnsrc(0), ALU.mult, ALU.max))
            k.op("vector", lambda e: e.drain())
            for ch in range(1, NCH):
                k.op("vector", lambda e, ch=ch, T=T, sf=sf: e.scalar_tensor_tensor(
                    sf[:, 0:T], xnsrc(ch), -1.0, xnsrc(ch), ALU.mult, ALU.max))
                k.op("vector", lambda e: e.drain())
                k.op("vector", lambda e, T=T, sf=sf: e.tensor_tensor(
                    aqm_b[:, 0:T], aqm_b[:, 0:T], sf[:, 0:T], ALU.max))
                k.op("vector", lambda e: e.drain())
            va = k.op("vector", lambda e: e.drain(), inc="sv")
            for c in range(T // 128):
                mm_bank(2, (va,) if c == 0 else ())
                t1 = k.op("tensor", lambda e, c=c: e.transpose(
                    pb[2][:, 0:128], aqm_b[:, c * 128:(c + 1) * 128],
                    ident_sb[:]), inc="st")
                k.wait("vector", t1)
                k.op("vector", lambda e: e.reduce_max(
                    gsc[:, 0:1], pb[2][:, 0:128], AX.X))
                k.op("vector", lambda e: e.drain())
                k.op("vector", lambda e: e.tensor_scalar(
                    gsc[:, 1:2], gsc[:, 0:1], 1e-5, None, ALU.max, ALU.bypass))
                vcl = k.op("vector", lambda e: e.drain(), inc="sv")
                pb_tok[2] = vcl
                mm_bank(2)
                t2 = k.op("tensor", lambda e: e.transpose(
                    pb[2][0:1, 0:128], gsc[:, 1:2], ident_sb[:]), inc="st")
                k.wait("vector", t2)
                if c == 0:
                    k.wait("vector", k.tok("so"))
                pb_tok[2] = k.op("vector", lambda e, c=c: e.tensor_copy(
                    rowstage[0:1, c * 128:(c + 1) * 128],
                    pb[2][0:1, 0:128]), inc="sv")
                k.op("vector", lambda e: e.drain())
            vaqr = k.op("vector", lambda e: e.drain(), inc="sv")
            k.wait("sync", vaqr)
            k.op("sync", lambda e, g=g, T=T: e.dma_start(
                out=rows_d[D_AQM, g * TG: g * TG + T],
                in_=rowstage[0:1, 0:T]), inc="so", k=16)
        vt3 = k.tok("sv")
        for ch in range(NCH):
            if ch == 0:
                k.wait("vector", vt3)
            k.op("vector", lambda e, ch=ch, T=T, sf=sf: e.tensor_tensor(
                sf[:, 0:T], xnsrc(ch), mu2_b[:, 0:T], ALU.subtract))
            k.op("vector", lambda e: e.drain())
            k.op("vector", lambda e, T=T, sf=sf: e.tensor_tensor(
                sf[:, 0:T], sf[:, 0:T], r2_b[:, 0:T], ALU.mult))
            vch = k.op("vector", lambda e: e.drain(), inc="sv")
            k.wait("scalar", vch)
            sch = k.op("scalar", lambda e, ch=ch, T=T, sf=sf: e.activation(
                h_bf[:, ch * TG: ch * TG + T], sf[:, 0:T],
                ACTF.Identity, bias=pvec[:, 8 + ch:9 + ch], scale=pvec[:, ch:ch + 1]),
                inc="ss")
            k.wait("vector", sch)
        return k.tok("ss")

    def emit_kv(T, slot0):
        for m in range(8, 16):
            bank = 4 + (m % 2)
            mm_bank(bank)
            for kc in range(NCH):
                k.op("tensor", lambda e, m=m, kc=kc, bank=bank, T=T: e.matmul(
                    pb[bank][:, 0:T],
                    qk_t[:, kc * 2048 + m * 128: kc * 2048 + (m + 1) * 128],
                    h_bf[:, kc * TG: kc * TG + T],
                    start=(kc == 0), stop=(kc == NCH - 1)))
            tmm = k.mark("tensor", "st")
            k.wait("scalar", tmm)
            for sb in range(T // 256):
                slot = (slot0 + sb) % 3
                pb_tok[bank] = k.op(
                    "scalar", lambda e, m=m, sb=sb, slot=slot, bank=bank: e.activation(
                        kroll[:, (m - 8) * 768 + slot * 256:
                              (m - 8) * 768 + slot * 256 + 256],
                        pb[bank][:, sb * 256:(sb + 1) * 256],
                        ACTF.Identity, bias=0.0, scale=qscale[:, m:m + 1]), inc="ss")
        cnt = 0
        for mt in range(T // 128):
            for nn in range(2):
                bank = 4 + (cnt % 2)
                cnt += 1
                mm_bank(bank)
                for kc in range(NCH):
                    k.op("tensor", lambda e, mt=mt, nn=nn, kc=kc, bank=bank, T=T: e.matmul(
                        pb[bank][:, 0:512],
                        h_bf[:, kc * TG + mt * 128: kc * TG + (mt + 1) * 128],
                        v_t[:, kc * 1024 + nn * 512: kc * 1024 + (nn + 1) * 512],
                        start=(kc == 0), stop=(kc == NCH - 1)))
                tmm = k.mark("tensor", "st")
                k.wait("vector", tmm)
                u = ((slot0 + mt // 2) % 3) * 2 + (mt % 2)
                for hh in range(nn * 8, nn * 8 + 8):
                    k.op("vector", lambda e, u=u, hh=hh, nn=nn, bank=bank: e.tensor_copy(
                        vroll[:, u * VW + hh * 65: u * VW + hh * 65 + 64],
                        pb[bank][:, (hh - nn * 8) * 64:(hh - nn * 8) * 64 + 64]))
                pb_tok[bank] = k.op("vector", lambda e: e.drain(), inc="sv")

    k.waits("vector", [t_halo])
    k.wait("tensor", t_halo)
    k.wait("scalar", t_halo)

    def halosrc(ch):
        return halo_x[:, ch * HALO:(ch + 1) * HALO]

    hs = emit_h(halosrc, HALO, 0, False)
    k.wait("tensor", hs)
    emit_kv(HALO, 2)
    k.flush("p1_halo")

    for g in range(NG):
        def xsrc(ch, g=g):
            return x1c(ch, g)

        k.wait("scalar", k.tok("sv"))
        hs = emit_h(xsrc, TG, g, True)
        k.wait("tensor", hs)
        emit_kv(TG, (2 * g) % 3)
        for m in range(8):
            bank = 4 + (m % 2)
            mm_bank(bank)
            for kc in range(NCH):
                k.op("tensor", lambda e, m=m, kc=kc, bank=bank: e.matmul(
                    pb[bank][:, 0:TG],
                    qk_t[:, kc * 2048 + m * 128: kc * 2048 + (m + 1) * 128],
                    h_bf[:, kc * TG:(kc + 1) * TG],
                    start=(kc == 0), stop=(kc == NCH - 1)))
            tmm = k.mark("tensor", "st")
            k.wait("scalar", tmm)
            pb_tok[bank] = k.op("scalar", lambda e, m=m, bank=bank: e.activation(
                q_bf[:, m * TG:(m + 1) * TG], pb[bank][:, 0:TG],
                ACTF.Identity, bias=0.0, scale=qscale[:, m:m + 1]), inc="ss")
        k.flush(f"p1_qkv_g{g}")

        for blk in range(2):
            B = 2 * g + blk
            cslot = B % 3
            pslot = (B - 1) % 3
            mask_sb = maskA_sb if B == 0 else maskB_sb
            tqkv = k.tok("ss")
            probs_b = [probs_sb, sbf2]

            def den_slot(h):
                if h < 8:
                    return rowsb[64:65, h * 256:(h + 1) * 256]
                tsr = [mu_b, r_b, mu2_b, r2_b][(h - 8) // 2]
                c0 = ((h - 8) % 2) * 256
                return tsr[64:65, c0:c0 + 256]

            tsc = [None] * H; se = [None] * H; vp = [None] * H
            tpv = [None] * H; sevac = [None] * H; tb = [None] * H
            vstt = [None] * H
            for i in range(H + 1):
                if i < H:
                    h = i
                    kch, poff = h // 2, (h % 2) * 64
                    b4 = 4 if h % 2 == 0 else 2
                    if h < 2:
                        k.wait("tensor", tqkv)
                        k.wait("tensor", pb_tok[b4])
                        k.wait("tensor", pb_tok[b4 + 1])
                    else:
                        k.wait("tensor", se[h - 2])
                    for kc in range(4):
                        slot = pslot if kc < 2 else cslot
                        sub = kc % 2
                        bank = b4 + kc // 2
                        k.op("tensor", lambda e, kch=kch, poff=poff, slot=slot,
                             sub=sub, bank=bank, kc=kc, blk=blk: e.matmul(
                            pb[bank][:, (kc % 2) * 256:(kc % 2) * 256 + 256],
                            kroll[poff:poff + 64,
                                  kch * 768 + slot * 256 + sub * 128:
                                  kch * 768 + slot * 256 + sub * 128 + 128],
                            q_bf[poff:poff + 64,
                                 kch * TG + blk * 256: kch * TG + blk * 256 + 256],
                            start=True, stop=True))
                    tsc[h] = k.mark("tensor", "st")
                if 0 <= i - 1 < H:
                    h = i - 1
                    pvb = 6 if h % 2 == 0 else 0
                    k.wait("tensor", vp[h])
                    if h < 2:
                        k.wait("tensor", pb_tok[pvb])
                    else:
                        k.wait("tensor", sevac[h - 2])
                    for kc in range(4):
                        slot = pslot if kc < 2 else cslot
                        u = slot * 2 + (kc % 2)
                        k.op("tensor", lambda e, u=u, h=h, kc=kc, pvb=pvb: e.matmul(
                            pb[pvb][0:65, 0:256],
                            vroll[:, u * VW + h * 65: u * VW + (h + 1) * 65],
                            probs_b[h % 2][:, kc * 256:(kc + 1) * 256],
                            start=(kc == 0), stop=(kc == 3)))
                    tpv[h] = k.mark("tensor", "st")
                if i < H:
                    h = i
                    b4 = 4 if h % 2 == 0 else 2
                    k.wait("scalar", tsc[h])
                    gd = tpv[h - 2] if h >= 2 else prev_tpv.get(h % 2)
                    k.wait("scalar", gd)
                    for kc in range(4):
                        bank = b4 + kc // 2
                        k.op("scalar", lambda e, kc=kc, bank=bank, h=h: e.activation(
                            probs_b[h % 2][:, kc * 256:(kc + 1) * 256],
                            pb[bank][:, (kc % 2) * 256:(kc % 2) * 256 + 256],
                            ACTF.Exp, bias=0.0, scale=1.0))
                    se[h] = k.op("scalar", lambda e: e.drain(), inc="ss")
                    pb_tok[b4] = se[h]
                    pb_tok[b4 + 1] = se[h]
                if 0 <= i - 1 < H:
                    h = i - 1
                    pvb = 6 if h % 2 == 0 else 0
                    kch, poff = h // 2, (h % 2) * 64
                    k.wait("scalar", tpv[h])
                    k.wait("scalar", prev_tb.get(h))
                    k.op("scalar", lambda e, kch=kch, poff=poff, pvb=pvb:
                         e.activation(
                        araw[poff:poff + 64, kch * 256:(kch + 1) * 256],
                        pb[pvb][0:64, 0:256], ACTF.Identity, bias=0.0, scale=1.0))
                    k.op("scalar", lambda e, h=h, pvb=pvb: e.activation(
                        den_slot(h), pb[pvb][64:65, 0:256], ACTF.Identity,
                        bias=0.0, scale=1.0))
                    sevac[h] = k.op("scalar", lambda e: e.drain(), inc="ss")
                    pb_tok[pvb] = sevac[h]
                if i < H:
                    h = i
                    k.wait("vector", se[h])
                    k.op("vector", lambda e, mask_sb=mask_sb, h=h: e.tensor_tensor(
                        probs_b[h % 2][:], probs_b[h % 2][:], mask_sb[:], ALU.mult))
                    vp[h] = k.op("vector", lambda e: e.drain(), inc="sv")
            k.wait("sync", sevac[H - 1])
            k.wait("sync", k.tok("st"))
            k.wait("sync", k.tok("stDN0"))
            dn1 = stDN("sync", lambda e: e.dma_start(
                out=mu2_b[0:8, 0:256], in_=rowsb[64:65, 0:2048]))
            dn2 = stDN("sync", lambda e: e.dma_start(
                out=mu2_b[8:10, 0:256], in_=mu_b[64:65, 0:512]))
            dn3 = stDN("sync", lambda e: e.dma_start(
                out=mu2_b[10:12, 0:256], in_=r_b[64:65, 0:512]))
            dn4 = stDN("sync", lambda e: e.dma_start(
                out=mu2_b[12:14, 0:256], in_=mu2_b[64:65, 0:512]))
            dn5 = stDN("sync", lambda e: e.dma_start(
                out=mu2_b[14:16, 0:256], in_=r2_b[64:65, 0:512]))
            k.wait("vector", dn5)
            k.op("vector", lambda e: e.reciprocal(
                mu2_b[0:16, 0:256], mu2_b[0:16, 0:256]))
            vrc = k.op("vector", lambda e: e.drain(), inc="sv")
            k.wait("sync", vrc)
            dn6 = stDN("sync", lambda e: e.dma_start(
                out=rowsb[64:65, 0:2048], in_=mu2_b[0:8, 0:256]))
            dn7 = stDN("sync", lambda e: e.dma_start(
                out=mu_b[64:65, 0:512], in_=mu2_b[8:10, 0:256]))
            dn8 = stDN("sync", lambda e: e.dma_start(
                out=r_b[64:65, 0:512], in_=mu2_b[10:12, 0:256]))
            dn9 = stDN("sync", lambda e: e.dma_start(
                out=mu2_b[64:65, 0:512], in_=mu2_b[12:14, 0:256]))
            dn10 = stDN("sync", lambda e: e.dma_start(
                out=r2_b[64:65, 0:512], in_=mu2_b[14:16, 0:256]))
            vrec = dn10
            for h in range(H):
                bcb = 7 if h % 2 == 0 else 1
                kch, poff = h // 2, (h % 2) * 64
                if h < 2:
                    k.wait("tensor", vrec)
                    k.wait("tensor", pb_tok[bcb])
                else:
                    k.wait("tensor", vstt[h - 2])
                tb[h] = k.op("tensor", lambda e, h=h, bcb=bcb: e.matmul(
                    pb[bcb][:, 0:256], onesf[64:65, :], den_slot(h),
                    start=True, stop=True), inc="st")
                k.wait("vector", tb[h])
                k.op("vector", lambda e, kch=kch, poff=poff, h=h, bcb=bcb:
                     e.scalar_tensor_tensor(
                    h_bf[poff:poff + 64, kch * TG: kch * TG + 256],
                    araw[poff:poff + 64, kch * 256:(kch + 1) * 256],
                    qscale[poff:poff + 64, 16 + kch:17 + kch],
                    pb[bcb][poff:poff + 64, 0:256],
                    ALU.mult, ALU.mult))
                vstt[h] = k.op("vector", lambda e: e.drain(), inc="sv")
                pb_tok[bcb] = vstt[h]
            prev_tb = {h: tb[h] for h in range(H)}
            prev_tpv = {0: tpv[H - 2], 1: tpv[H - 1]}
            k.wait("vector", vrec)
            va = k.tok("sv")
            for m in range(NCH):
                bank = 4 + (m % 2)
                mm_bank(bank, (va,))
                for kc in range(NCH):
                    k.op("tensor", lambda e, m=m, kc=kc, bank=bank: e.matmul(
                        pb[bank][:, 0:256],
                        ow_t[:, kc * 1024 + m * 128: kc * 1024 + (m + 1) * 128],
                        h_bf[:, kc * TG: kc * TG + 256],
                        start=(kc == 0), stop=(kc == NCH - 1)))
                tmm = k.mark("tensor", "st")
                k.wait("vector", tmm)
                pb_tok[bank] = k.op(
                    "vector", lambda e, m=m, g=g, blk=blk, bank=bank:
                    e.scalar_tensor_tensor(
                        x1c(m, g, off=blk * 256, w=256),
                        pb[bank][:, 0:256], oscale[:, m:m + 1],
                        x1c(m, g, off=blk * 256, w=256),
                        ALU.mult, ALU.add), inc="sv")
                k.op("vector", lambda e: e.drain())
            k.flush(f"p1_att_g{g}b{blk}")

    k.barrier()
    k.flush("p1_done")

    if "l0f" in PARTS:
        wb = [pf(0, 2048), pf(2048, 2048)]
        sb2 = [pf(4096, 2048), pf(6144, 2048)]
        ob = [pbf(8192, 2048), pbf(9216, 2048)]
        stP = [k.stream("stPa", 2), k.stream("stPb", 2)]
        P2Q = ("sync", "gpsimd")
        stQ = [k.stream("stQa", 1), k.stream("stQb", 1)]
        wb_free = [None, None]
        sb2_free = [None, None]
        ob_free = [None, None]

        def tile_src(wap, flat, t):
            if flat:
                return wap[:, t * 2048:(t + 1) * 2048]
            kc, h = t // 2, t % 2
            return wap[kc * 128:(kc + 1) * 128, h * 2048:(h + 1) * 2048]

        def tile_dst(dram, flat, t):
            if flat:
                return dram[:, t * 2048:(t + 1) * 2048]
            kc, h = t // 2, t % 2
            return dram[:, kc * INNER + h * 2048: kc * INNER + (h + 1) * 2048]

        NT = 16
        for wi, (wap, dram, flat) in enumerate([
                (wgT, ternG, False), (wuT, ternU, False), (wdT, ternD, True)]):
            r2w = r2_b[:, wi:wi + 1]
            muw = mu2_b[:, wi:wi + 1]
            mmtok = [None] * NT
            for t in range(NT):
                b = t % 2
                k.wait(P2Q[b], wb_free[b])
                tl = stP[b](P2Q[b], lambda e, wap=wap, flat=flat, t=t, b=b:
                            e.dma_start(out=wb[b][:], in_=tile_src(wap, flat, t)))
                eng = "scalar" if t % 2 == 0 else "vector"
                k.wait(eng, tl)
                k.wait(eng, sb2_free[b])
                if eng == "scalar":
                    k.op("scalar", lambda e, b=b: e.activation(
                        sb2[b][:], wb[b][:], ACTF.Abs, bias=0.0, scale=1.0))
                    atok = k.op("scalar", lambda e: e.drain(), inc="ss")
                else:
                    k.op("vector", lambda e, b=b: e.scalar_tensor_tensor(
                        sb2[b][:], wb[b][:], -1.0, wb[b][:], ALU.mult, ALU.max))
                    atok = k.op("vector", lambda e: e.drain(), inc="sv")
                wb_free[b] = atok
                k.wait("tensor", atok)
                if t == 0:
                    k.wait("tensor", pb_tok[2])
                for sx in range(4):
                    k.op("tensor", lambda e, b=b, sx=sx, t=t: e.matmul(
                        pb[2][:, 0:512], onesf[:],
                        sb2[b][:, sx * 512:(sx + 1) * 512],
                        start=(t == 0 and sx == 0), stop=(t == NT - 1 and sx == 3)))
                mmtok[t] = k.mark("tensor", "st")
                sb2_free[b] = mmtok[t]
            k.wait("vector", mmtok[NT - 1])
            k.op("vector", lambda e, muw=muw: e.reduce_sum(
                muw, pb[2][:, 0:512], AX.X))
            k.op("vector", lambda e: e.drain())
            k.op("vector", lambda e, muw=muw: e.tensor_scalar(
                muw, muw, 1.0 / (D * INNER), 1e-5, ALU.mult, ALU.max))
            k.op("vector", lambda e: e.drain())
            k.op("vector", lambda e, wi=wi, muw=muw: e.tensor_copy(
                pvec[:, 16 + wi:17 + wi], muw))
            k.op("vector", lambda e: e.drain())
            k.op("vector", lambda e, r2w=r2w, muw=muw: e.reciprocal(r2w, muw))
            vr2 = k.op("vector", lambda e: e.drain(), inc="sv")
            pb_tok[2] = vr2
            k.flush(f"p2s_{wi}")
            for t in range(NT):
                b = t % 2
                k.wait(P2Q[b], wb_free[b])
                tl = stP[b](P2Q[b], lambda e, wap=wap, flat=flat, t=t, b=b:
                            e.dma_start(out=wb[b][:], in_=tile_src(wap, flat, t)))
                k.wait("scalar", tl)
                if t == 0:
                    k.wait("scalar", vr2)
                k.wait("scalar", sb2_free[b])
                k.op("scalar", lambda e, b=b, r2w=r2w: e.activation(
                    sb2[b][:], wb[b][:], ACTF.Identity, bias=0.0, scale=r2w))
                tq = k.op("scalar", lambda e: e.drain(), inc="ss")
                k.wait("vector", tq)
                k.wait("vector", ob_free[b])
                k.op("vector", lambda e, b=b: e.tensor_scalar(
                    wb[b][:], sb2[b][:], -0.5, None, ALU.is_lt, ALU.bypass))
                k.op("vector", lambda e: e.drain())
                k.op("vector", lambda e, b=b: e.scalar_tensor_tensor(
                    ob[b][:], sb2[b][:], 0.5, wb[b][:], ALU.is_gt, ALU.subtract))
                vtt = k.op("vector", lambda e: e.drain(), inc="sv")
                wb_free[b] = vtt
                sb2_free[b] = vtt
                k.wait(P2Q[1 - b], vtt)
                ob_free[b] = stQ[b](P2Q[1 - b], lambda e, dram=dram, flat=flat,
                                    t=t, b=b: e.dma_start(
                    out=tile_dst(dram, flat, t), in_=ob[b][:]))
            k.flush(f"p2t_{wi}")
        k.wait("sync", ob_free[0])
        k.wait("sync", ob_free[1])
        k.barrier()
        k.flush("p2_done")

    if "l1m" in PARTS:
        web = pbf(11264, 2048)
        stWL = k.stream("stWL", 2)
        stWS = k.stream("stWS", 2)
        wetoks = []
        i_w = 0
        for dout in range(NCH):
            for e_ in range(NEX):
                buf = i_w % 2
                if i_w >= 2:
                    k.wait("gpsimd", wetoks[i_w - 2])
                tg1 = stWL("gpsimd", lambda e, dout=dout, e_=e_, buf=buf:
                           e.dma_start(
                    out=web[:, buf * 1024:(buf + 1) * 1024],
                    in_=We[dout, e_].rearrange("(a p) m -> p a m", p=128)))
                k.wait("gpsimd", tg1)
                wetoks.append(stWS("gpsimd", lambda e, dout=dout, e_=e_, buf=buf:
                                   e.dma_start(
                    out=WeBf[dout, e_].rearrange("(a p) m -> p a m", p=128),
                    in_=web[:, buf * 1024:(buf + 1) * 1024])))
                i_w += 1
        if wetoks:
            k.wait("gpsimd", wetoks[-1])
            k.wait("gpsimd", wetoks[-2])

    if "l0f" in PARTS:
        gu = pf(0, 8192)
        xg3 = pf(8192, 2048)
        fct_b = pf(10240, 512)
        fct2_b = pf(10752, 512)
        stG = k.stream("stG", 4)
        stU = k.stream("stU", 4)
        stD = k.stream("stD", 4)
        aqx3 = regC[:, 0:2048]
        fsg = regA[:, 0:4096]
        fsu = regA[:, 4096:8192]
        fsd = regA[:, 8192:12288]
        T3 = TG3

        for g3 in range(NG3):
            k.wait("sync", k.tok("sv"))
            toks = [k.op("sync", lambda e, ch=ch, g3=g3: e.dma_start(
                out=xg3[:, ch * T3:(ch + 1) * T3],
                in_=xT[ch * 128:(ch + 1) * 128, HALO + g3 * T3: HALO + (g3 + 1) * T3]),
                inc="sd", k=16) for ch in range(NCH)]
            bcast_rows([(rowg(R_MU0, g3, t=T3), mu_b),
                        (rowg(R_R0, g3, t=T3), r_b)], T3)
            bcast_drows([(D_AQM, g3 * T3, mu2_b)], T3)
            k.wait("vector", toks[-1])
            k.op("vector", lambda e: e.reciprocal(r2_b[:, 0:T3], mu2_b[:, 0:T3]))
            k.op("vector", lambda e: e.drain())
            k.op("vector", lambda e: e.tensor_scalar(
                r2_b[:, 0:T3], r2_b[:, 0:T3], 127.0, None, ALU.mult, ALU.bypass))
            k.op("vector", lambda e: e.drain())
            k.op("vector", lambda e: e.tensor_scalar(
                fct_b[:, 0:T3], mu2_b[:, 0:T3], pvec[:, 16:17], 1.0 / 127.0,
                ALU.mult, ALU.mult))
            k.op("vector", lambda e: e.drain())
            k.op("vector", lambda e: e.tensor_scalar(
                fct2_b[:, 0:T3], mu2_b[:, 0:T3], pvec[:, 17:18], 1.0 / 127.0,
                ALU.mult, ALU.mult))
            k.op("vector", lambda e: e.drain())
            for st_ in range(5):
                for ch in range(NCH):
                    sl = xg3[:, ch * T3:(ch + 1) * T3]
                    if st_ == 0:
                        k.op("vector", lambda e, sl=sl: e.tensor_tensor(
                            sl, sl, mu_b[:, 0:T3], ALU.subtract))
                    elif st_ == 1:
                        k.op("vector", lambda e, sl=sl: e.tensor_tensor(
                            sl, sl, r_b[:, 0:T3], ALU.mult))
                    elif st_ == 2:
                        k.op("vector", lambda e, sl=sl: e.tensor_tensor(
                            sl, sl, r2_b[:, 0:T3], ALU.mult))
                    elif st_ == 3:
                        k.op("vector", lambda e, sl=sl: e.tensor_scalar(
                            sl, sl, MAGIC, -MAGIC, ALU.add, ALU.add))
                    else:
                        k.op("vector", lambda e, ch=ch, sl=sl: e.tensor_scalar(
                            aqx3[:, ch * T3:(ch + 1) * T3], sl, -128.0, 127.0,
                            ALU.max, ALU.min))
                k.op("vector", lambda e: e.drain())
            vaq = k.op("vector", lambda e: e.drain(), inc="sv")
            if DEBUG and g3 == 0:
                k.wait("sync", vaq)
                k.op("sync", lambda e: e.dma_start(
                    out=dbg.bitcast(BF16)[0:128, 0:2048], in_=aqx3[:, 0:2048]),
                    inc="so", k=16)
                k.op("sync", lambda e: e.dma_start(
                    out=dbg[0:128, 4096:4096 + 256], in_=fct_b[:, 0:256]),
                    inc="so", k=16)
                k.op("sync", lambda e: e.dma_start(
                    out=dbg[0:128, 4352:4352 + 256], in_=fct2_b[:, 0:256]),
                    inc="so", k=16)
                k.op("sync", lambda e: e.dma_start(
                    out=dbg[0:128, 4608:4608 + 256], in_=r2_b[:, 0:256]),
                    inc="so", k=16)
                k.wait("vector", k.tok("so"))

            gutoks = []
            for m in range(32):
                buf = m % 4
                if m >= 4:
                    k.wait("sync", gutoks[m - 4])
                tg_ = stG("sync", lambda e, m=m, buf=buf: e.dma_start(
                    out=fsg[:, buf * 1024:(buf + 1) * 1024],
                    in_=ternG.rearrange("p (c x) -> p c x", x=INNER)[
                        :, :, m * 128:(m + 1) * 128]))
                tu_ = stU("sync", lambda e, m=m, buf=buf: e.dma_start(
                    out=fsu[:, buf * 1024:(buf + 1) * 1024],
                    in_=ternU.rearrange("p (c x) -> p c x", x=INNER)[
                        :, :, m * 128:(m + 1) * 128]))
                bg_ = 0 if (m % 2) == 0 else 2
                bu_ = 1 if (m % 2) == 0 else 3
                mm_bank(bg_, (tg_, tu_, vaq) if m < 2 else (tg_, tu_))
                for kc in range(NCH):
                    k.op("tensor", lambda e, m=m, kc=kc, buf=buf, bg_=bg_: e.matmul(
                        pb[bg_][:, 0:T3],
                        fsg[:, buf * 1024 + kc * 128: buf * 1024 + (kc + 1) * 128],
                        aqx3[:, kc * T3:(kc + 1) * T3],
                        start=(kc == 0), stop=(kc == NCH - 1)))
                mm_bank(bu_, (tu_,))
                for kc in range(NCH):
                    k.op("tensor", lambda e, m=m, kc=kc, buf=buf, bu_=bu_: e.matmul(
                        pb[bu_][:, 0:T3],
                        fsu[:, buf * 1024 + kc * 128: buf * 1024 + (kc + 1) * 128],
                        aqx3[:, kc * T3:(kc + 1) * T3],
                        start=(kc == 0), stop=(kc == NCH - 1)))
                tmm = k.mark("tensor", "st")
                sf = sbf2.bitcast(F32)
                k.wait("vector", tmm)
                k.op("vector", lambda e, bg_=bg_: e.tensor_tensor(
                    sf[:, 0:T3], pb[bg_][:, 0:T3], fct_b[:, 0:T3], ALU.mult))
                vg = k.op("vector", lambda e: e.drain(), inc="sv")
                k.wait("scalar", vg)
                ssil = k.op("scalar", lambda e, sf=sf: e.activation(
                    sf[:, T3:2 * T3], sf[:, 0:T3], ACTF.Silu, bias=0.0, scale=1.0),
                    inc="ss")
                k.wait("vector", ssil)
                gu_m = gu[:, m * T3:(m + 1) * T3]
                k.op("vector", lambda e, gu_m=gu_m, sf=sf, bu_=bu_: e.tensor_tensor(
                    gu_m, sf[:, T3:2 * T3], pb[bu_][:, 0:T3], ALU.mult))
                vgu = k.op("vector", lambda e: e.drain(), inc="sv")
                pb_tok[bg_] = vgu
                pb_tok[bu_] = vgu
                gutoks.append(tmm)
                if m == 0:
                    k.op("vector", lambda e, gu_m=gu_m: e.scalar_tensor_tensor(
                        r_b[:, 0:T3], gu_m, -1.0, gu_m, ALU.mult, ALU.max))
                else:
                    k.op("vector", lambda e, gu_m=gu_m, sf=sf:
                         e.scalar_tensor_tensor(
                        sf[:, 0:T3], gu_m, -1.0, gu_m, ALU.mult, ALU.max))
                    k.op("vector", lambda e: e.drain())
                    k.op("vector", lambda e, sf=sf: e.tensor_tensor(
                        r_b[:, 0:T3], r_b[:, 0:T3], sf[:, 0:T3], ALU.max))
                k.op("vector", lambda e: e.drain())
            vmx = k.op("vector", lambda e: e.drain(), inc="sv")
            t3 = None
            for c in range(T3 // 128):
                mm_bank(4, (vmx,) if c == 0 else ())
                t1 = k.op("tensor", lambda e, c=c: e.transpose(
                    pb[4][:, 0:128], r_b[:, c * 128:(c + 1) * 128],
                    ident_sb[:]), inc="st")
                k.wait("vector", t1)
                k.op("vector", lambda e: e.reduce_max(
                    gsc[:, 0:1], pb[4][:, 0:128], AX.X))
                vcl = k.op("vector", lambda e: e.drain(), inc="sv")
                pb_tok[4] = vcl
                mm_bank(5, (vcl,))
                t2 = k.op("tensor", lambda e: e.transpose(
                    pb[5][0:1, 0:128], gsc[:, 0:1], ident_sb[:]), inc="st")
                k.wait("vector", t2)
                k.wait("vector", t3)
                vrow = k.op("vector", lambda e: e.tensor_copy(
                    rowstage[0:1, 0:128], pb[5][0:1, 0:128]), inc="sv")
                pb_tok[5] = vrow
                k.op("vector", lambda e: e.drain())
                mm_bank(4, (vrow,))
                t3 = k.op("tensor", lambda e: e.matmul(
                    pb[4][:, 0:128], onesf[0:1, :], rowstage[0:1, 0:128],
                    start=True, stop=True), inc="st")
                k.wait("vector", t3)
                pb_tok[4] = k.op("vector", lambda e, c=c: e.tensor_copy(
                    mu_b[:, c * 128:(c + 1) * 128], pb[4][:, 0:128]), inc="sv")
                k.op("vector", lambda e: e.drain())
            k.op("vector", lambda e: e.tensor_tensor(
                mu_b[:, 0:T3], mu_b[:, 0:T3], fct2_b[:, 0:T3], ALU.mult))
            k.op("vector", lambda e: e.drain())
            k.op("vector", lambda e: e.tensor_scalar(
                mu_b[:, 0:T3], mu_b[:, 0:T3], 1e-5, None, ALU.max, ALU.bypass))
            k.op("vector", lambda e: e.drain())
            k.op("vector", lambda e: e.tensor_scalar(
                r2_b[:, 0:T3], mu_b[:, 0:T3], pvec[:, 18:19], 1.0 / 127.0,
                ALU.mult, ALU.mult))
            k.op("vector", lambda e: e.drain())
            k.op("vector", lambda e: e.reciprocal(mu_b[:, 0:T3], mu_b[:, 0:T3]))
            k.op("vector", lambda e: e.drain())
            k.op("vector", lambda e: e.tensor_tensor(
                mu_b[:, 0:T3], mu_b[:, 0:T3], fct2_b[:, 0:T3], ALU.mult))
            k.op("vector", lambda e: e.drain())
            k.op("vector", lambda e: e.tensor_scalar(
                mu_b[:, 0:T3], mu_b[:, 0:T3], 127.0, None, ALU.mult, ALU.bypass))
            k.op("vector", lambda e: e.drain())
            mu4 = regC.bitcast(F32)[:, 1024:2048]
            for c4 in range(4):
                k.op("vector", lambda e, c4=c4: e.tensor_copy(
                    mu4[:, c4 * T3:(c4 + 1) * T3], mu_b[:, 0:T3]))
            k.op("vector", lambda e: e.drain())
            gub = gu.bitcast(BF16)
            for w in range(8):
                k.op("vector", lambda e, w=w: e.tensor_tensor(
                    gu[:, w * 1024:(w + 1) * 1024], gu[:, w * 1024:(w + 1) * 1024],
                    mu4[:], ALU.mult))
            k.op("vector", lambda e: e.drain())
            for w in range(8):
                k.op("vector", lambda e, w=w: e.tensor_scalar(
                    gu[:, w * 1024:(w + 1) * 1024], gu[:, w * 1024:(w + 1) * 1024],
                    MAGIC, -MAGIC, ALU.add, ALU.add))
            k.op("vector", lambda e: e.drain())
            for w in range(8):
                k.op("vector", lambda e, w=w: e.tensor_scalar(
                    gub.rearrange("p (m c) -> p m c", c=2 * T3)[
                        :, 4 * w:4 * w + 4, 0:T3],
                    gu.rearrange("p (m c) -> p m c", c=T3)[:, 4 * w:4 * w + 4, :],
                    -128.0, 127.0, ALU.max, ALU.min))
            k.op("vector", lambda e: e.drain())
            vaq2 = k.op("vector", lambda e: e.drain(), inc="sv")
            if DEBUG and g3 == 0:
                k.wait("sync", vaq2)
                for mm_ in range(4):
                    k.op("sync", lambda e, mm_=mm_: e.dma_start(
                        out=dbg.bitcast(BF16)[0:128, 16384 + mm_ * 256:
                                              16384 + (mm_ + 1) * 256],
                        in_=gu.bitcast(BF16)[:, 2 * mm_ * T3: 2 * mm_ * T3 + T3]),
                        inc="so", k=16)
                k.op("sync", lambda e: e.dma_start(
                    out=dbg[0:128, 9216:9216 + 256], in_=mu_b[:, 0:256]),
                    inc="so", k=16)
                k.wait("vector", k.tok("so"))
            for i in range(8):
                k.wait("tensor", pb_tok[i])
            dttoks = []
            for m in range(32):
                buf = m % 4
                if m >= 4:
                    k.wait("sync", dttoks[m - 4])
                tw = stD("sync", lambda e, m=m, buf=buf: e.dma_start(
                    out=fsd[:, buf * 1024:(buf + 1) * 1024],
                    in_=ternD[:, m * D:(m + 1) * D]))
                k.wait("tensor", tw)
                if m == 0:
                    k.wait("tensor", vaq2)
                for dout in range(NCH):
                    k.op("tensor", lambda e, m=m, buf=buf, dout=dout: e.matmul(
                        pb[dout][:, 0:T3],
                        fsd[:, buf * 1024 + dout * 128: buf * 1024 + (dout + 1) * 128],
                        gu.bitcast(BF16)[:, 2 * m * T3: 2 * m * T3 + T3],
                        start=(m == 0), stop=(m == 31)))
                dttoks.append(k.mark("tensor", "st"))
            tdn = dttoks[-1]
            k.wait("vector", tdn)
            sf = sbf2.bitcast(F32)
            for dout in range(NCH):
                k.op("vector", lambda e, dout=dout, sf=sf: e.tensor_tensor(
                    sf[:, 0:T3], pb[dout][:, 0:T3], r2_b[:, 0:T3], ALU.mult))
                k.op("vector", lambda e: e.drain())
                pb_tok[dout] = k.op(
                    "vector", lambda e, dout=dout, g3=g3, sf=sf: e.tensor_tensor(
                        x1c(dout, g3, t=T3), x1c(dout, g3, t=T3), sf[:, 0:T3],
                        ALU.add), inc="sv")
                k.op("vector", lambda e: e.drain())
            k.flush(f"p3_g{g3}")
        k.barrier()
        k.flush("p3_done")


    if "l1a" in PARTS:
        xn1_bf = regA
        ucx = pbf(0, 16384)
        wslot = pbf(8192, 8192)
        wsf = pf(8192, 4096)
        decb = wsf[:, 0:2048]
        xb = regB[:, 0:4096]
        sc32 = regB.bitcast(F32)

        twt = []
        for kc in range(NCH):
            twt.append(k.op("gpsimd", lambda e, kc=kc: e.dma_start(
                out=wslot[:, kc * 1024:(kc + 1) * 1024],
                in_=Wt[kc * 128:(kc + 1) * 128, :]), inc="sg", k=16))

        for g in range(NG):
            k.wait("vector", k.tok("so"))
            vst0 = ln_stats(lambda ch, g=g: x1c(ch, g), TG, 1e-5,
                            mu_b, r_b, src_bf=False)
            k.wait("sync", vst0)
            k.op("sync", lambda e, g=g: e.dma_start(
                out=rows_d[D_MUX, g * TG:(g + 1) * TG], in_=mu_b[0:1, 0:TG]),
                inc="so", k=16)
            k.op("sync", lambda e, g=g: e.dma_start(
                out=rows_d[D_RX, g * TG:(g + 1) * TG], in_=r_b[0:1, 0:TG]),
                inc="so", k=16)
            vt = k.tok("sv")
            k.wait("vector", vt)
            for ch in range(NCH):
                k.op("vector", lambda e, ch=ch, g=g: e.tensor_tensor(
                    xn1_bf[:, ch * 2048 + g * TG: ch * 2048 + (g + 1) * TG],
                    x1c(ch, g), mu_b[:, 0:TG], ALU.subtract))
            k.op("vector", lambda e: e.drain())
            for ch in range(NCH):
                k.op("vector", lambda e, ch=ch, g=g: e.tensor_tensor(
                    xn1_bf[:, ch * 2048 + g * TG: ch * 2048 + (g + 1) * TG],
                    xn1_bf[:, ch * 2048 + g * TG: ch * 2048 + (g + 1) * TG],
                    r_b[:, 0:TG], ALU.mult))
            k.op("vector", lambda e: e.drain(), inc="sv")
            vst1 = ln_stats(lambda ch, g=g: xn1_bf[:, ch * 2048 + g * TG:
                                            ch * 2048 + (g + 1) * TG],
                            TG, 1e-6, mu2_b, r2_b)
            k.wait("sync", vst1)
            k.op("sync", lambda e, g=g: e.dma_start(
                out=rows_d[D_MU1, g * TG:(g + 1) * TG], in_=mu2_b[0:1, 0:TG]),
                inc="so", k=16)
            k.op("sync", lambda e, g=g: e.dma_start(
                out=rows_d[D_R1, g * TG:(g + 1) * TG], in_=r2_b[0:1, 0:TG]),
                inc="so", k=16)
            vt2 = k.tok("sv")
            k.wait("vector", vt2)
            for ch in range(NCH):
                k.op("vector", lambda e, ch=ch, g=g, sf=sf: e.tensor_tensor(
                    sf[:, 0:TG], x1c(ch, g), mu_b[:, 0:TG], ALU.subtract))
                k.op("vector", lambda e: e.drain())
                k.op("vector", lambda e, sf=sf: e.tensor_tensor(
                    sf[:, 0:TG], sf[:, 0:TG], r_b[:, 0:TG], ALU.mult))
                k.op("vector", lambda e: e.drain())
                k.op("vector", lambda e, sf=sf: e.tensor_tensor(
                    sf[:, 0:TG], sf[:, 0:TG], mu2_b[:, 0:TG], ALU.subtract))
                k.op("vector", lambda e: e.drain())
                k.op("vector", lambda e, sf=sf: e.tensor_tensor(
                    sf[:, 0:TG], sf[:, 0:TG], r2_b[:, 0:TG], ALU.mult))
                vc = k.op("vector", lambda e: e.drain(), inc="sv")
                k.wait("scalar", vc)
                sc = k.op("scalar", lambda e, ch=ch, sf=sf: e.activation(
                    xb[:, ch * TG:(ch + 1) * TG], sf[:, 0:TG], ACTF.Identity,
                    bias=lpar[:, 8 + ch:9 + ch], scale=lpar[:, ch:ch + 1]), inc="ss")
                k.wait("vector", sc)
            hx = k.tok("ss")
            k.wait("tensor", twt[-1])
            k.wait("tensor", hx)
            for m in range(NCH):
                bank = 4 + (m % 2)
                mm_bank(bank)
                for kc in range(NCH):
                    k.op("tensor", lambda e, m=m, kc=kc, bank=bank: e.matmul(
                        pb[bank][:, 0:TG],
                        wslot[:, kc * 1024 + m * 128: kc * 1024 + (m + 1) * 128],
                        xb[:, kc * TG:(kc + 1) * TG],
                        start=(kc == 0), stop=(kc == NCH - 1)))
                tmm = k.mark("tensor", "st")
                k.wait("scalar", tmm)
                pb_tok[bank] = k.op("scalar", lambda e, m=m, g=g, bank=bank:
                                    e.activation(
                    ucx[:, m * 2048 + g * TG: m * 2048 + (g + 1) * TG],
                    pb[bank][:, 0:TG], ACTF.Identity, bias=0.0,
                    scale=betav[:, m:m + 1]), inc="ss")
            k.flush(f"p4a_g{g}")

        uw = k.tok("ss")
        u32v = wsf[:, 2048:4096]
        k.waits("vector", [uw])
        for ch in range(NCH):
            k.op("vector", lambda e: e.memset(decb[:, 0:2048], 1.0))
            k.op("vector", lambda e: e.drain())
            k.op("vector", lambda e, ch=ch: e.tensor_scalar(
                decb[:, 0:2048], decb[:, 0:2048], dvec[:, ch:ch + 1], None,
                ALU.mult, ALU.bypass))
            k.op("vector", lambda e: e.drain())
            k.op("vector", lambda e, ch=ch: e.tensor_copy(
                u32v[:, 0:2048], ucx[:, ch * 2048:(ch + 1) * 2048]))
            k.op("vector", lambda e: e.drain())
            k.op("vector", lambda e: e.tensor_tensor_scan(
                sc32[:, 0:2048], decb[:, 0:2048],
                u32v[:, 0:2048], 0.0, ALU.mult, ALU.add))
            k.op("vector", lambda e: e.drain())
            k.op("vector", lambda e, ch=ch: e.tensor_copy(
                carry_sb[:, ch:ch + 1], sc32[:, 2047:2048]))
            k.op("vector", lambda e: e.drain())
        vst = k.tok("sv")
        bcast_oh_done = None
        mm_bank(0, (vst,))
        tb = k.op("tensor", lambda e: e.matmul(
            pb[0][:, 0:CORES], ones1f[0:1, :], oh2_sb[0:1, :],
            start=True, stop=True), inc="st")
        k.wait("vector", tb)
        pb_tok[0] = k.op("vector", lambda e: e.tensor_copy(
            oh2_sb[:, 0:CORES], pb[0][:, 0:CORES]), inc="sv")
        k.op("vector", lambda e: e.drain())
        for r_ in range(CORES):
            k.op("vector", lambda e, r_=r_: e.tensor_scalar(
                ccst[:, r_ * NCH:(r_ + 1) * NCH], carry_sb[:],
                oh2_sb[:, r_:r_ + 1], None, ALU.mult, ALU.bypass))
        vcc = k.op("vector", lambda e: e.drain(), inc="sv")
        k.wait("sync", vcc)
        dts = [k.op("sync", lambda e, r_=r_: e.dma_start(
            out=cc_in[r_ * 128:(r_ + 1) * 128, :],
            in_=ccst[:, r_ * NCH:(r_ + 1) * NCH]), inc="so", k=16)
            for r_ in range(CORES)]
        k.wait("gpsimd", dts[-1])
        tcc = k.op("gpsimd", lambda e: e.collective_compute(
            "ReduceScatter", ALU.add, replica_groups=[list(range(CORES))],
            ins=[cc_in[:, :].opt()], outs=[cc_out[:, :].opt()]), inc="cc")
        k.wait("gpsimd", tcc)
        tccd = k.op("gpsimd", lambda e: e.dma_start(out=ccres[:], in_=cc_out[:, :]),
                    inc="sg", k=16)
        k.flush("p4b_scan1")

        gbf = rowsb.bitcast(BF16)[:, 0:2048]
        wsfr = wsf
        xfr = wsfr[:, 0:2048]
        wg1v = wsfr[:, 2048:4096]
        wg2v = r2_b[:, 0:80]
        stWG = k.stream("stWG", 2)
        k.wait("sync", k.tok("sv"))
        twg1 = stWG("sync", lambda e: e.dma_start(
            out=wg1v, in_=wg1.rearrange("(a p) m -> p a m", p=128)))
        twg2l = stWG("sync", lambda e: e.dma_start(
            out=wg2v, in_=wg2.rearrange("(a p) m -> p a m", p=128)))
        k.wait("tensor", twg1)
        k.wait("tensor", twg2l)
        T4 = 256
        k.wait("sync", k.tok("so"))
        for gh in range(8):
            bcast_drows([(D_MUX, gh * T4, mu_b), (D_RX, gh * T4, r_b)], T4)
            vt = k.tok("sv")
            k.wait("vector", vt)
            sf = sbf2.bitcast(F32)
            for ch in range(NCH):
                k.op("vector", lambda e, ch=ch, gh=gh, sf=sf: e.tensor_tensor(
                    sf[:, 0:T4], x1[:, ch * TLOC + gh * T4: ch * TLOC + (gh + 1) * T4],
                    mu_b[:, 0:T4], ALU.subtract))
                k.op("vector", lambda e: e.drain())
                k.op("vector", lambda e, ch=ch, sf=sf: e.tensor_tensor(
                    xfr[:, ch * T4:(ch + 1) * T4], sf[:, 0:T4],
                    r_b[:, 0:T4], ALU.mult))
                k.op("vector", lambda e: e.drain())
            vx = k.op("vector", lambda e: e.drain(), inc="sv")
            for m2 in range(2):
                mm_bank(2 + m2, (vx,) if m2 == 0 else ())
                for kc in range(NCH):
                    k.op("tensor", lambda e, m2=m2, kc=kc: e.matmul(
                        pb[2 + m2][:, 0:T4],
                        wg1v[:, kc * 256 + m2 * 128: kc * 256 + (m2 + 1) * 128],
                        xfr[:, kc * T4:(kc + 1) * T4],
                        start=(kc == 0), stop=(kc == NCH - 1)))
            tg1 = k.mark("tensor", "st")
            k.wait("scalar", tg1)
            for m2 in range(2):
                pb_tok[2 + m2] = k.op("scalar", lambda e, m2=m2: e.activation(
                    probs_sb.bitcast(F32)[:, m2 * T4:(m2 + 1) * T4],
                    pb[2 + m2][:, 0:T4], ACTF.Gelu, bias=bg1_sb[:, m2:m2 + 1],
                    scale=1.0), inc="ss")
            sg1 = k.tok("ss")
            mm_bank(4, (sg1,))
            for m2 in range(2):
                k.op("tensor", lambda e, m2=m2: e.matmul(
                    pb[4][0:40, 0:T4], wg2v[:, m2 * 40:(m2 + 1) * 40],
                    probs_sb.bitcast(F32)[:, m2 * T4:(m2 + 1) * T4],
                    start=(m2 == 0), stop=(m2 == 1)))
            tg2 = k.mark("tensor", "st")
            k.wait("scalar", tg2)
            slog = k.op("scalar", lambda e: e.activation(
                mu2_b[0:20, 0:T4], pb[4][0:20, 0:T4], ACTF.Identity,
                bias=bg2_sb[0:20, 0:1], scale=1.0), inc="ss")
            pb_tok[4] = slog
            for q in range(2):
                mm_bank(5, (slog,) if q == 0 else ())
                tq = k.op("tensor", lambda e, q=q: e.transpose(
                    pb[5][0:128, 0:20], mu2_b[0:20, q * 128:(q + 1) * 128],
                    ident_sb[0:20, 0:20]), inc="st")
                k.wait("vector", tq)
                k.op("vector", lambda e: e.tensor_copy(gtm[:, 0:20], pb[5][:, 0:20]))
                k.op("vector", lambda e: e.drain())
                k.op("vector", lambda e: e.reduce_max(gsc[:, 0:1], gtm[:, 0:20], AX.X))
                vmx = k.op("vector", lambda e: e.drain(), inc="sv")
                k.wait("scalar", vmx)
                k.op("vector", lambda e: e.tensor_scalar(
                    gsc[:, 1:2], gsc[:, 0:1], -1.0, None, ALU.mult, ALU.bypass))
                vneg = k.op("vector", lambda e: e.drain(), inc="sv")
                k.wait("scalar", vneg)
                sexp = k.op("scalar", lambda e: e.activation(
                    gtm[:, 32:52], gtm[:, 0:20], ACTF.Exp, bias=gsc[:, 1:2],
                    scale=1.0, accum_out=gsc[:, 2:3]), inc="ss")
                k.wait("vector", sexp)
                k.op("vector", lambda e: e.reciprocal(gsc[:, 3:4], gsc[:, 2:3]))
                k.op("vector", lambda e: e.drain())
                k.op("vector", lambda e: e.tensor_scalar(
                    gtm[:, 32:52], gtm[:, 32:52], gsc[:, 3:4], None,
                    ALU.mult, ALU.bypass))
                k.op("vector", lambda e: e.drain())
                k.op("vector", lambda e: e.reduce_max(gsc[:, 0:1], gtm[:, 32:52], AX.X))
                k.op("vector", lambda e: e.drain())
                k.op("vector", lambda e: e.tensor_scalar(
                    gtm[:, 64:84], gtm[:, 32:52], gsc[:, 0:1], None,
                    ALU.is_equal, ALU.bypass))
                k.op("vector", lambda e: e.drain())
                k.op("vector", lambda e: e.reduce_sum(gsc[:, 1:2], gtm[:, 64:84], AX.X))
                k.op("vector", lambda e: e.drain())
                k.op("vector", lambda e: e.scalar_tensor_tensor(
                    gtm[:, 64:84], gtm[:, 64:84], -2.0, gtm[:, 32:52],
                    ALU.mult, ALU.add))
                k.op("vector", lambda e: e.drain())
                k.op("vector", lambda e: e.reduce_max(gsc[:, 2:3], gtm[:, 64:84], AX.X))
                k.op("vector", lambda e: e.drain())
                k.op("vector", lambda e: e.tensor_scalar(
                    gsc[:, 4:5], gsc[:, 1:2], 2.0, None, ALU.is_ge, ALU.bypass))
                k.op("vector", lambda e: e.drain())
                k.op("vector", lambda e: e.tensor_tensor(
                    gsc[:, 5:6], gsc[:, 0:1], gsc[:, 2:3], ALU.subtract))
                k.op("vector", lambda e: e.drain())
                k.op("vector", lambda e: e.tensor_tensor(
                    gsc[:, 5:6], gsc[:, 5:6], gsc[:, 4:5], ALU.mult))
                k.op("vector", lambda e: e.drain())
                k.op("vector", lambda e: e.tensor_tensor(
                    gsc[:, 5:6], gsc[:, 5:6], gsc[:, 2:3], ALU.add))
                k.op("vector", lambda e: e.drain())
                k.op("vector", lambda e: e.tensor_scalar(
                    gtm[:, 96:116], gtm[:, 32:52], gsc[:, 5:6], None,
                    ALU.is_ge, ALU.bypass))
                k.op("vector", lambda e: e.drain())
                k.op("vector", lambda e: e.tensor_tensor(
                    gtm[:, 96:116], gtm[:, 96:116], gtm[:, 32:52], ALU.mult))
                vg2 = k.op("vector", lambda e: e.drain(), inc="sv")
                mm_bank(6, (vg2,))
                tq2 = k.op("tensor", lambda e: e.transpose(
                    pb[6][0:20, 0:128], gtm[:, 96:116], ident_sb[:]), inc="st")
                k.wait("vector", tq2)
                pb_tok[6] = k.op("vector", lambda e, gh=gh, q=q: e.tensor_copy(
                    gbf[0:20, gh * T4 + q * 128: gh * T4 + (q + 1) * 128],
                    pb[6][0:20, 0:128]), inc="sv")
                k.op("vector", lambda e: e.drain())
                pb_tok[5] = pb_tok[6]
        k.flush("p4c_gating")

        k.wait("vector", k.tok("cc"))
        k.waits("vector", [tccd])
        for ch in range(NCH):
            k.op("vector", lambda e: e.memset(decb[:, 0:2048], 1.0))
            k.op("vector", lambda e: e.drain())
            k.op("vector", lambda e, ch=ch: e.tensor_scalar(
                decb[:, 0:2048], decb[:, 0:2048], dvec[:, ch:ch + 1], None,
                ALU.mult, ALU.bypass))
            k.op("vector", lambda e: e.drain())
            k.op("vector", lambda e, ch=ch: e.tensor_copy(
                u32v[:, 0:2048], ucx[:, ch * 2048:(ch + 1) * 2048]))
            k.op("vector", lambda e: e.drain())
            k.op("vector", lambda e, ch=ch: e.tensor_tensor_scan(
                ucx[:, ch * 2048:(ch + 1) * 2048], decb[:, 0:2048],
                u32v[:, 0:2048], ccres[:, ch:ch + 1], ALU.mult, ALU.add))
            k.op("vector", lambda e: e.drain())
        k.op("vector", lambda e: e.drain(), inc="sv")
        k.flush("p4d_scan2")

        scan2done = k.tok("sv")
        lncx = regB[:, 0:4096]
        out_bf = regB[:, 4096:8192]
        res_bf = regC[:, 0:4096]
        z_bf = regC[:, 4096:8192]
        s32a = wsf[:, 0:512]
        s32b = wsf[:, 512:1024]

        for g in range(NG):
            ln_stats(lambda ch, g=g: ucx[:, ch * 2048 + g * TG:
                                         ch * 2048 + (g + 1) * TG],
                     TG, 1e-6, mu_b, r_b)
            vt = k.tok("sv")
            k.wait("vector", vt)
            sf = sbf2.bitcast(F32)
            for ch in range(NCH):
                k.op("vector", lambda e, ch=ch, g=g, sf=sf: e.tensor_copy(
                    sf[:, 0:TG], ucx[:, ch * 2048 + g * TG: ch * 2048 + (g + 1) * TG]))
                k.op("vector", lambda e: e.drain())
                k.op("vector", lambda e, sf=sf: e.tensor_tensor(
                    sf[:, 0:TG], sf[:, 0:TG], mu_b[:, 0:TG], ALU.subtract))
                k.op("vector", lambda e: e.drain())
                k.op("vector", lambda e, sf=sf: e.tensor_tensor(
                    sf[:, 0:TG], sf[:, 0:TG], r_b[:, 0:TG], ALU.mult))
                vc = k.op("vector", lambda e: e.drain(), inc="sv")
                k.wait("scalar", vc)
                scx = k.op("scalar", lambda e, ch=ch, sf=sf: e.activation(
                    lncx[:, ch * TG:(ch + 1) * TG], sf[:, 0:TG], ACTF.Identity,
                    bias=lpar[:, 24 + ch:25 + ch], scale=lpar[:, 16 + ch:17 + ch]),
                    inc="ss")
                k.wait("vector", scx)
            hlncx = k.tok("ss")
            k.wait("gpsimd", scan2done)
            k.wait("gpsimd", k.tok("st"))
            k.wait("gpsimd", k.tok("sv"))
            two = [k.op("gpsimd", lambda e, kc=kc: e.dma_start(
                out=wslot[:, kc * 1024:(kc + 1) * 1024],
                in_=Wo[kc * 128:(kc + 1) * 128, :]), inc="sg", k=16)
                for kc in range(NCH)]
            k.waits("tensor", [two[-1], hlncx])
            for m in range(NCH):
                bank = 2 + (m % 2)
                mm_bank(bank)
                for kc in range(NCH):
                    k.op("tensor", lambda e, m=m, kc=kc, bank=bank: e.matmul(
                        pb[bank][:, 0:TG],
                        wslot[:, kc * 1024 + m * 128: kc * 1024 + (m + 1) * 128],
                        lncx[:, kc * TG:(kc + 1) * TG],
                        start=(kc == 0), stop=(kc == NCH - 1)))
                tmm = k.mark("tensor", "st")
                k.wait("scalar", tmm)
                pb_tok[bank] = k.op("scalar", lambda e, m=m, bank=bank: e.activation(
                    out_bf[:, m * TG:(m + 1) * TG], pb[bank][:, 0:TG],
                    ACTF.Identity, bias=0.0, scale=1.0), inc="ss")
            bcast_drows([(D_MUX, g * TG, mu_b), (D_RX, g * TG, r_b),
                         (D_MU1, g * TG, mu2_b), (D_R1, g * TG, r2_b)], TG)
            vt2 = k.tok("sv")
            k.wait("vector", vt2)
            for ch in range(NCH):
                k.op("vector", lambda e, ch=ch, g=g, sf=sf: e.tensor_tensor(
                    sf[:, 0:TG], x1c(ch, g), mu_b[:, 0:TG], ALU.subtract))
                k.op("vector", lambda e: e.drain())
                k.op("vector", lambda e, sf=sf: e.tensor_tensor(
                    sf[:, 0:TG], sf[:, 0:TG], r_b[:, 0:TG], ALU.mult))
                k.op("vector", lambda e: e.drain())
                k.op("vector", lambda e, sf=sf: e.tensor_tensor(
                    sf[:, 0:TG], sf[:, 0:TG], mu2_b[:, 0:TG], ALU.subtract))
                k.op("vector", lambda e: e.drain())
                k.op("vector", lambda e, sf=sf: e.tensor_tensor(
                    sf[:, 0:TG], sf[:, 0:TG], r2_b[:, 0:TG], ALU.mult))
                vc = k.op("vector", lambda e: e.drain(), inc="sv")
                k.wait("scalar", vc)
                sres = k.op("scalar", lambda e, ch=ch, sf=sf: e.activation(
                    res_bf[:, ch * TG:(ch + 1) * TG], sf[:, 0:TG], ACTF.Identity,
                    bias=lpar[:, 8 + ch:9 + ch], scale=lpar[:, ch:ch + 1]), inc="ss")
                k.wait("vector", sres)
            hres = k.tok("ss")
            k.wait("gpsimd", k.tok("st"))
            twg = [k.op("gpsimd", lambda e, kc=kc: e.dma_start(
                out=wslot[:, kc * 1024:(kc + 1) * 1024],
                in_=Wg[kc * 128:(kc + 1) * 128, :]), inc="sg", k=16)
                for kc in range(NCH)]
            k.wait("tensor", twg[-1])
            k.wait("tensor", hres)
            for i in range(8):
                k.wait("tensor", pb_tok[i])
            for m in range(NCH):
                for kc in range(NCH):
                    k.op("tensor", lambda e, m=m, kc=kc: e.matmul(
                        pb[m][:, 0:TG],
                        wslot[:, kc * 1024 + m * 128: kc * 1024 + (m + 1) * 128],
                        res_bf[:, kc * TG:(kc + 1) * TG],
                        start=(kc == 0), stop=False))
            th1 = k.mark("tensor", "st")
            k.wait("gpsimd", th1)
            twg2 = [k.op("gpsimd", lambda e, kc=kc: e.dma_start(
                out=wslot[:, kc * 1024:(kc + 1) * 1024],
                in_=Wg[D + kc * 128: D + (kc + 1) * 128, :]), inc="sg", k=16)
                for kc in range(NCH)]
            k.wait("tensor", twg2[-1])
            for m in range(NCH):
                for kc in range(NCH):
                    k.op("tensor", lambda e, m=m, kc=kc: e.matmul(
                        pb[m][:, 0:TG],
                        wslot[:, kc * 1024 + m * 128: kc * 1024 + (m + 1) * 128],
                        out_bf[:, kc * TG:(kc + 1) * TG],
                        start=False, stop=(kc == NCH - 1)))
            th2 = k.mark("tensor", "st")
            k.wait("scalar", th2)
            for m in range(NCH):
                pb_tok[m] = k.op("scalar", lambda e, m=m: e.activation(
                    z_bf[:, m * TG:(m + 1) * TG], pb[m][:, 0:TG], ACTF.Identity,
                    bias=lpar[:, 48 + m:49 + m], scale=1.0), inc="ss")
            zs = k.tok("ss")
            k.wait("vector", zs)
            ln_stats(lambda ch: z_bf[:, ch * TG:(ch + 1) * TG], TG, 1e-6, mu_b, r_b)
            vt3 = k.tok("sv")
            k.wait("vector", vt3)
            for ch in range(NCH):
                k.op("vector", lambda e, ch=ch, sf=sf: e.tensor_copy(
                    sf[:, 0:TG], z_bf[:, ch * TG:(ch + 1) * TG]))
                k.op("vector", lambda e: e.drain())
                k.op("vector", lambda e, sf=sf: e.tensor_tensor(
                    sf[:, 0:TG], sf[:, 0:TG], mu_b[:, 0:TG], ALU.subtract))
                k.op("vector", lambda e: e.drain())
                k.op("vector", lambda e, sf=sf: e.tensor_tensor(
                    sf[:, 0:TG], sf[:, 0:TG], r_b[:, 0:TG], ALU.mult))
                vc = k.op("vector", lambda e: e.drain(), inc="sv")
                k.wait("scalar", vc)
                sgate = k.op("scalar", lambda e, ch=ch, sf=sf: e.activation(
                    s32a[:, 0:TG], sf[:, 0:TG], ACTF.Sigmoid,
                    bias=lpar[:, 40 + ch:41 + ch], scale=lpar[:, 32 + ch:33 + ch]),
                    inc="ss")
                k.wait("vector", sgate)
                k.op("vector", lambda e, ch=ch: e.tensor_tensor(
                    s32b[:, 0:TG], out_bf[:, ch * TG:(ch + 1) * TG],
                    res_bf[:, ch * TG:(ch + 1) * TG], ALU.subtract))
                k.op("vector", lambda e: e.drain())
                k.op("vector", lambda e: e.tensor_tensor(
                    s32b[:, 0:TG], s32b[:, 0:TG], s32a[:, 0:TG], ALU.mult))
                k.op("vector", lambda e: e.drain())
                k.op("vector", lambda e, ch=ch, sf=sf: e.tensor_copy(
                    sf[:, 0:TG], res_bf[:, ch * TG:(ch + 1) * TG]))
                k.op("vector", lambda e: e.drain())
                k.op("vector", lambda e, sf=sf: e.tensor_tensor(
                    s32b[:, 0:TG], s32b[:, 0:TG], sf[:, 0:TG], ALU.add))
                k.op("vector", lambda e: e.drain())
                k.op("vector", lambda e, ch=ch, g=g: e.tensor_tensor(
                    x1c(ch, g), x1c(ch, g), s32b[:, 0:TG], ALU.add))
                k.op("vector", lambda e: e.drain(), inc="sv")
            k.flush(f"p4d_g{g}")
        k.barrier()
        k.flush("p4_done")

    if "l1m" in PARTS:
        xn1_bf = regA
        gbf = rowsb.bitcast(BF16)[:, 0:2048]
        sc_e = [regB, regC]
        wsl = [pbf(512 + i * 512, 1024) for i in range(3)] + \
              [pbf(8960 + i * 512, 1024) for i in range(3)]
        WEQ = ("sync", "gpsimd", "scalar")
        stWE = [k.stream(f"stWE{i}", 2) for i in range(3)]
        be_sb = pbf(2048, 1024)
        gall = pbf(2560, 10240)
        stBE = k.stream("stBE", 1)
        tbe = stBE("gpsimd", lambda e: e.dma_start(out=be_sb[0:NEX, :], in_=bev))
        selg = pbf(7680, 2560)
        vsel = k.op("sync", lambda e: e.dma_start(out=selg[:], in_=selgd),
                    inc="sd", k=16)

        for cg in range(4):
            if cg == 0:
                k.wait("tensor", vsel)
            for i in range(8):
                k.wait("tensor", pb_tok[i])
            for e_ in range(NEX):
                bank = e_ % 2
                if e_ >= 2:
                    k.wait("tensor", pb_tok[bank])
                tb_ = k.op("tensor", lambda e, e_=e_, cg=cg, bank=bank: e.matmul(
                    pb[bank][:, 0:TG], selg[0:NEX, e_ * 128:(e_ + 1) * 128],
                    gbf[0:NEX, cg * TG:(cg + 1) * TG],
                    start=True, stop=True), inc="st")
                k.wait("vector", tb_)
                pb_tok[bank] = k.op("vector", lambda e, e_=e_, bank=bank:
                                    e.tensor_copy(
                    gall[:, e_ * TG:(e_ + 1) * TG], pb[bank][:, 0:TG]), inc="sv")
                k.op("vector", lambda e: e.drain())
            k.wait("tensor", pb_tok[0])
            k.wait("tensor", pb_tok[1])
            sltoks = []
            sli = 0
            exp_last = []
            for e_ in range(NEX):
                scb = sc_e[e_ % 2]
                gbx = gall[:, e_ * TG:(e_ + 1) * TG]
                if e_ >= 2:
                    k.wait("vector", exp_last[e_ - 2])
                for ch in range(NCH):
                    k.op("vector", lambda e, ch=ch, cg=cg, scb=scb, gbx=gbx:
                         e.tensor_tensor(
                        scb[:, ch * TG:(ch + 1) * TG],
                        xn1_bf[:, ch * 2048 + cg * TG: ch * 2048 + (cg + 1) * TG],
                        gbx[:, 0:TG], ALU.mult))
                vsc = k.op("vector", lambda e: e.drain(), inc="sv")
                if e_ == 0:
                    k.wait("tensor", tbe)
                k.wait("tensor", vsc)
                for dout in range(NCH):
                    wbuf = wsl[sli % 6]
                    q_ = WEQ[(sli % 6) % 3]
                    if sli >= 6:
                        k.wait(q_, sltoks[sli - 6])
                    tw = stWE[(sli % 6) % 3](q_, lambda e, dout=dout, e_=e_,
                                             wbuf=wbuf: e.dma_start(
                        out=wbuf[:, 0:1024],
                        in_=WeBf[dout, e_].rearrange("(a p) m -> p a m", p=128)))
                    k.wait("tensor", tw)
                    if e_ == 0:
                        k.op("tensor", lambda e, dout=dout, cg=cg: e.matmul(
                            pb[dout][:, 0:TG],
                            be_sb[0:NEX, dout * 128:(dout + 1) * 128],
                            gbf[0:NEX, cg * TG:(cg + 1) * TG],
                            start=True, stop=False))
                    for din in range(NCH):
                        k.op("tensor", lambda e, dout=dout, din=din, scb=scb,
                             wbuf=wbuf, last=(e_ == NEX - 1 and din == NCH - 1):
                             e.matmul(
                            pb[dout][:, 0:TG],
                            wbuf[:, din * 128:(din + 1) * 128],
                            scb[:, din * TG:(din + 1) * TG],
                            start=False, stop=last))
                    sltoks.append(k.mark("tensor", "st"))
                    sli += 1
                exp_last.append(sltoks[-1])
            tcl = sltoks[-1]
            k.wait("vector", tcl)
            for dout in range(NCH):
                pb_tok[dout] = k.op("vector", lambda e, dout=dout, cg=cg:
                                    e.tensor_tensor(
                    x1c(dout, cg), x1c(dout, cg), pb[dout][:, 0:TG], ALU.add),
                    inc="sv")
                k.op("vector", lambda e: e.drain())
            k.flush(f"p5_cg{cg}")
        k.barrier()
        k.flush("p5_done")

    k.barrier()
    for ch in range(NCH):
        k.op("sync", lambda e, ch=ch: e.dma_start(
            out=y[ch * 128:(ch + 1) * 128, :], in_=x1c(ch)), inc="so", k=16)
    k.wait("sync", k.tok("so"))
    k.flush("out")
    k.close()
    return nc


def _build_mask(first_block_causal):
    q = np.arange(256)[None, :]
    j = np.arange(512)[:, None]
    rel = j - 256 - q
    valid = (rel <= 0) & (rel > -WIN)
    if first_block_causal:
        valid = valid & (j >= 256)
    m = valid.astype(ml_dtypes.bfloat16)
    out = np.zeros((128, 4 * 256), dtype=ml_dtypes.bfloat16)
    for kc in range(4):
        out[:, kc * 256:(kc + 1) * 256] = m[kc * 128:(kc + 1) * 128, :]
    return out


def _build_selg():
    s = np.zeros((128, NEX * 128), dtype=ml_dtypes.bfloat16)
    for e in range(NEX):
        s[e, e * 128:(e + 1) * 128] = 1.0
    return s


def make_inmaps(inputs):
    x = np.asarray(inputs["L0_attn_x"])
    T = lambda a: np.ascontiguousarray(np.asarray(a).T)
    C = np.ascontiguousarray
    PC = lambda a: C(np.asarray(a).reshape(NCH, 128).T)
    decay = (0.5 ** (1.0 / np.linspace(2.0, 8.0, D, dtype=np.float32))).astype(np.float32)
    We = np.asarray(inputs["L1_ffn_We"])
    wdT0 = np.asarray(inputs["L0_ffn_w_down"]).T
    base = {
        "qkvT": T(inputs["L0_attn_qkv_w"]),
        "outwT": T(inputs["L0_attn_out_w"]),
        "ln2w": PC(inputs["L0_attn_norm_w"]),
        "ln2b": PC(inputs["L0_attn_norm_b"]),
        "wgT": T(inputs["L0_ffn_w_gate"]),
        "wuT": T(inputs["L0_ffn_w_up"]),
        "wdT": C(wdT0.reshape(32, 128, D).transpose(1, 0, 2).reshape(128, 32 * D)),
        "Wt": C(np.asarray(inputs["L1_attn_Wt"])),
        "Wo": C(np.asarray(inputs["L1_attn_Wo"])),
        "Wg": C(np.asarray(inputs["L1_attn_Wg"])),
        "bgv": PC(inputs["L1_attn_bg"]),
        "ln1w": PC(inputs["L1_attn_ln1_w"]),
        "ln1b": PC(inputs["L1_attn_ln1_b"]),
        "lncw": PC(inputs["L1_attn_lnc_w"]),
        "lncb": PC(inputs["L1_attn_lnc_b"]),
        "lngw": PC(inputs["L1_attn_lng_w"]),
        "lngb": PC(inputs["L1_attn_lng_b"]),
        "wg1": C(np.asarray(inputs["L1_ffn_wg1"])),
        "bg1": C(np.asarray(inputs["L1_ffn_bg1"]).reshape(2, 128).T),
        "wg2": C(np.asarray(inputs["L1_ffn_wg2"])),
        "bg2": C(np.asarray(inputs["L1_ffn_bg2"]).reshape(40, 1)),
        "We": C(We.reshape(NEX, D, NCH, 128).transpose(2, 0, 1, 3)),
        "bev": C(np.asarray(inputs["L1_ffn_be"])),
        "ident": np.eye(128, dtype=np.float32),
        "decayc": C(decay.reshape(NCH, 128).T),
        "maskB": _build_mask(False),
        "selg": _build_selg(),
    }
    mA_first = _build_mask(True)
    in_maps = []
    for c in range(CORES):
        p, hh = c // 2, c % 2
        xp = np.asarray(x[p])
        if hh == 0:
            halo = np.zeros((HALO, D), np.float32)
            own = xp[0:TLOC]
        else:
            halo = xp[TLOC - HALO:TLOC]
            own = xp[TLOC:2 * TLOC]
        oh = np.zeros((1, CORES), np.float32)
        if hh == 0:
            oh[0, c + 1] = 1.0
        m = dict(base)
        m["xT"] = C(np.concatenate([halo, own], axis=0).T)
        m["oh2"] = oh
        m["maskA"] = mA_first if hh == 0 else base["maskB"]
        in_maps.append(m)
    return in_maps


_NC_CACHE = {}


def kernel(**inputs):
    if PARTS not in _NC_CACHE:
        _NC_CACHE[PARTS] = build_program()
    nc = _NC_CACHE[PARTS]
    in_maps = make_inmaps(inputs)
    res = run_bass_kernel_spmd(nc, in_maps, core_ids=list(range(CORES)))
    out = np.empty((4, 4096, 1024), np.float32)
    for c in range(CORES):
        p, hh = c // 2, c % 2
        out[p, hh * TLOC:(hh + 1) * TLOC, :] = np.asarray(res.results[c]["y"]).T
    return out
